# version 06
# baseline (speedup 1.0000x reference)
import numpy as np
import ml_dtypes
import concourse.bass as bass
import concourse.mybir as mybir
from concourse.bass_utils import run_bass_kernel_spmd
from contextlib import ExitStack

F32 = mybir.dt.float32
BF16 = mybir.dt.bfloat16
FP8 = mybir.dt.float8e5
AF = mybir.ActivationFunctionType
ALU = mybir.AluOpType
AXX = mybir.AxisListType.X

T, D = 8192, 1024
CH = 128
NCH = T // CH
GRP = 4
NHEAD, HD, NPAIR = 16, 64, 8


class Prog:
    ENGS = ("sync", "tensor", "vector", "scalar", "gpsimd")

    def __init__(self):
        self.q = {e: [] for e in self.ENGS}
        self.cnt = {}
        self.ev = {}

    SELF_SEM = {"vector": "DV", "scalar": "AC", "gpsimd": "PL"}

    def emit(self, eng, fn, wait=(), inc=None, amt=1, event=None, dma=False):
        waits = []
        for w in wait:
            if w is None:
                continue
            waits.append(self.ev[w] if isinstance(w, str) else w)
        if not dma and eng in self.SELF_SEM:
            ss = self.SELF_SEM[eng]
            assert inc in (None, ss), (eng, inc)
            waits.append((ss, self.cnt.get(ss, 0)))
            inc, amt = ss, 1
        self.q[eng].append((fn, waits, inc, amt))
        if inc is not None:
            self.cnt[inc] = self.cnt.get(inc, 0) + amt
            if event is not None:
                self.ev[event] = (inc, self.cnt[inc])
        else:
            assert event is None, event

    def replay(self, eng_obj, name, sems):
        waited = {}
        for fn, waits, inc, amt in self.q[name]:
            for s, v in waits:
                if waited.get(s, -1) >= v:
                    continue
                eng_obj.wait_ge(sems[s], v)
                waited[s] = v
            ins = fn(eng_obj)
            if inc is not None:
                ins.then_inc(sems[inc], amt)


def build_nc(n_chunks=NCH, no_b=False, no_accum=False, no_pe=False, no_attn=False, no_scan=False, dbg=False, dbg2=False):
    nc = bass.Bass("TRN2", target_bir_lowering=False, dynamic_dma_scratch_size=16384)
    xap = nc.dram_tensor("x", [T, D], F32, kind="ExternalInput").ap()
    mkap = nc.dram_tensor("mask", [T, D], FP8, kind="ExternalInput").ap()
    wwap = nc.dram_tensor("Ww", [D, D], F32, kind="ExternalInput").ap()
    wrap_ = nc.dram_tensor("Wr", [D, D], F32, kind="ExternalInput").ap()
    w1ap = nc.dram_tensor("W1", [4 * D, D], F32, kind="ExternalInput").ap()
    w2ap = nc.dram_tensor("W2", [D, 4 * D], F32, kind="ExternalInput").ap()
    b1ap = nc.dram_tensor("b1", [4 * D], F32, kind="ExternalInput").ap()
    b2ap = nc.dram_tensor("b2", [D], F32, kind="ExternalInput").ap()
    dcap = nc.dram_tensor("decay", [NHEAD], F32, kind="ExternalInput").ap()
    laap = nc.dram_tensor("logalpha", [NHEAD], F32, kind="ExternalInput").ap()
    yap = nc.dram_tensor("y", [T, D], F32, kind="ExternalOutput").ap()

    p = Prog()
    E = p.emit
    with ExitStack() as ctx:
        def sb(name, shape, dt):
            return ctx.enter_context(nc.sbuf_tensor(name, shape, dt))

        WwT = sb("WwT", [128, 8 * 1024], BF16)
        WrT = sb("WrT", [128, 8 * 1024], BF16)
        W1T8 = sb("W1T8", [128, 8 * 4096], FP8)
        W2T8 = sb("W2T8", [128, 32 * 1024], FP8)
        b1qf = sb("b1qf", [128, 32], F32)
        b2row = sb("b2row", [128, 1024], FP8)
        MM = sb("MM", [128, 16 * 128], BF16)
        GP = sb("GP", [128, 8 * 128], BF16)
        GW = sb("GW", [128, 16 * 64], BF16)
        CONSTS = sb("CONSTS", [128, 4], F32)
        gcp = sb("gcp", [128, 8], F32)
        xA = sb("xA", [128, 2 * 1024], F32)
        xnbf = sb("xnbf", [128, 4 * 1024], BF16)
        prep_ctx = ExitStack()

        def pb(name, shape, dt):
            return prep_ctx.enter_context(nc.sbuf_tensor(name, shape, dt))

        IOT = pb("IOT", [128, 128], F32)
        JJ = pb("JJ", [128, 128], F32)
        KK = pb("KK", [128, 64], F32)
        CAUS = pb("CAUS", [128, 128], F32)
        E1 = pb("E1", [128, 128], F32)
        idb32 = pb("idb32", [128, 128], BF16)
        onesf = pb("onesf", [1, 128], F32)
        onesbf = pb("onesbf", [1, 128], BF16)
        ones11 = pb("ones11", [1, 1], BF16)
        rowA = pb("rowA", [1, 64], F32)
        lgh = pb("lgh", [128, 16], F32)
        alh = pb("alh", [128, 16], F32)
        gch = pb("gch", [128, 16], F32)
        lgp = pb("lgp", [128, 8], F32)
        alp = pb("alp", [128, 8], F32)
        prep_ctx.close()
        xnf = sb("xnf", [128, 1024], F32)
        xq8 = sb("xq8", [128, 1024], FP8)
        xqbf = sb("xqbf", [128, 1024], BF16)
        XQTb = sb("XQTb", [128, 1024], BF16)
        XNF = sb("XNF", [128, 8 * 384], BF16)
        XQF8 = sb("XQF8", [128, 8 * 1024], FP8)
        WKF = sb("WKF", [128, 1024], BF16)
        WKT = sb("WKT", [128, 1024], BF16)
        PREV = sb("PREV", [128, 8], BF16)
        rkgp = sb("rkgp", [128, 1024], BF16)
        VZ = sb("VZ", [128, 16 * 128], BF16)
        gv = sb("gv", [128, 1024], BF16)
        SM = sb("SM", [128, 2 * 512], BF16)
        RD = sb("RD", [128, 4 * 1024], BF16)
        SF32a = sb("SF32a", [128, 1024], F32)
        SBF = sb("SBF", [128, 1024], BF16)
        DGM = sb("DGM", [128, 1024], BF16)
        HQ8 = sb("HQ8", [128, 32 * 512], FP8)
        wbf = sb("wbf", [128, 2 * 512], BF16)
        maskb = sb("maskb", [128, 2 * 1024], FP8)
        y2b = sb("y2b", [128, 1024], BF16)
        ydrop = sb("ydrop", [128, 1024], F32)
        tcomb = sb("tcomb", [128, 1024], F32)
        mu = sb("mu", [128, 2], F32)
        ssq = sb("ssq", [128, 2], F32)
        mcol = sb("mcol", [128, 2], F32)
        m2col = sb("m2col", [128, 2], F32)
        vcol = sb("vcol", [128, 2], F32)
        lvcol = sb("lvcol", [128, 2], F32)
        rstd = sb("rstd", [128, 8], F32)
        nmb = sb("nmb", [128, 8], F32)

        pAB = ctx.enter_context(nc.psum_tensor("pAB", [128, 1024], F32))
        pCD = ctx.enter_context(nc.psum_tensor("pCD", [128, 1024], F32))
        pEF = ctx.enter_context(nc.psum_tensor("pEF", [128, 1024], F32))
        pGH = ctx.enter_context(nc.psum_tensor("pGH", [128, 1024], F32))

        sems = {}
        for s in ("LX0", "LX1", "LM0", "LM1", "LP", "TX0", "TX1", "TX2",
                  "TQ", "TW", "TP", "GA", "ST", "PE", "DV", "AC", "PL"):
            sems[s] = ctx.enter_context(nc.semaphore(s))

        XNF3 = XNF[:].rearrange("p (l t) -> p l t", l=8)
        PREV3 = PREV[:].rearrange("p (l t) -> p l t", l=8)

        E("sync", lambda e: e.dma_start(out=rowA[0:1, 0:16], in_=dcap.rearrange("(a b) -> a b", a=1)),
          inc="LP", amt=16)
        E("sync", lambda e: e.dma_start(out=rowA[0:1, 32:48], in_=laap.rearrange("(a b) -> a b", a=1)),
          inc="LP", amt=16, event="ld_small")
        p.ev["ld_decay"] = p.ev["ld_small"]
        p.ev["ld_la"] = p.ev["ld_small"]

        E("gpsimd", lambda e: e.iota(IOT[:], pattern=[[1, 128]], base=0, channel_multiplier=-1,
                                     allow_small_or_imprecise_dtypes=True), inc="PL", event="iota1")
        E("gpsimd", lambda e: e.iota(JJ[:], pattern=[[1, 128]], base=0, channel_multiplier=0,
                                     allow_small_or_imprecise_dtypes=True), inc="PL", event="iota2")
        E("gpsimd", lambda e: e.iota(KK[:], pattern=[[0, 64]], base=127, channel_multiplier=-1,
                                     allow_small_or_imprecise_dtypes=True), inc="PL", event="iota3")
        E("gpsimd", lambda e: e.memset(PREV[:], 0.0), inc="PL", event="prev0")
        E("vector", lambda e: e.memset(SF32a[:], 0.0))
        E("vector", lambda e: e.memset(SBF[:], 0.0))
        E("vector", lambda e: e.memset(VZ[:], 0.0))
        E("vector", lambda e: e.memset(DGM[:], 0.0))
        E("vector", lambda e: e.memset(CONSTS[:, 0:1], 1e-5))
        E("vector", lambda e: e.memset(CONSTS[:, 1:2], -0.5))
        E("vector", lambda e: e.memset(CONSTS[:, 2:3], 128.0))
        E("vector", lambda e: e.memset(onesf[:], 1.0))
        E("vector", lambda e: e.memset(onesbf[:], 1.0))
        E("vector", lambda e: e.memset(ones11[:], 1.0), inc="DV", event="memsets")
        E("vector", lambda e: e.tensor_scalar(CAUS[:], IOT[:], 0.5, None, op0=ALU.is_ge),
          wait=["iota1"], inc="DV", event="caus")
        E("vector", lambda e: e.tensor_scalar(E1[:], IOT[:], -1.0, None, op0=ALU.add),
          wait=["iota1"], inc="DV", event="e1")

        E("scalar", lambda e: e.activation(rowA[0:1, 16:32], rowA[0:1, 0:16], AF.Sigmoid),
          wait=["ld_decay"], inc="AC", event="grow")
        E("scalar", lambda e: e.activation(rowA[0:1, 0:16], rowA[0:1, 16:32], AF.Ln),
          wait=["grow"], inc="AC", event="lgrow")
        E("scalar", lambda e: e.activation(rowA[0:1, 48:64], rowA[0:1, 32:48], AF.Exp),
          wait=["ld_la"], inc="AC", event="alrow")

        E("tensor", lambda e: e.matmul(pEF[0:128, 0:16], onesf[:], rowA[0:1, 0:16], start=True, stop=True),
          wait=["lgrow", "memsets"], inc="PE", event="mm_lg")
        E("tensor", lambda e: e.matmul(pGH[0:128, 0:16], onesf[:], rowA[0:1, 48:64], start=True, stop=True),
          wait=["alrow"], inc="PE", event="mm_al")
        E("vector", lambda e: e.tensor_copy(lgh[:], pEF[:, 0:16]), wait=["mm_lg"], inc="DV")
        E("vector", lambda e: e.tensor_copy(alh[:], pGH[:, 0:16]), wait=["mm_al"], inc="DV", event="bcast")
        E("scalar", lambda e: e.activation(gch[:], lgh[:], AF.Exp, bias=0.0, scale=CONSTS[:, 2:3]),
          wait=["bcast"], inc="AC", event="gch")

        def halves(dst, src, ev_w, ev_name):
            for g in range(NPAIR):
                E("vector", lambda e, g=g, dst=dst, src=src:
                  e.tensor_copy(dst[0:64, g:g + 1], src[0:64, 2 * g:2 * g + 1]),
                  wait=[ev_w] if g == 0 else [])
                E("vector", lambda e, g=g, dst=dst, src=src:
                  e.tensor_copy(dst[64:128, g:g + 1], src[64:128, 2 * g + 1:2 * g + 2]),
                  inc="DV" if g == NPAIR - 1 else None,
                  event=ev_name if g == NPAIR - 1 else None)
        for h in range(NHEAD):
            gpr, rh = h // 2, (h % 2) * 64
            E("vector", lambda e, gpr=gpr, rh=rh: e.memset(
                DGM[rh:rh + 64, gpr * 128 + rh: gpr * 128 + rh + 64], 1.0))
        halves(lgp, lgh, "bcast", "lgp")
        halves(alp, alh, "bcast", "alp")
        halves(gcp, gch, "gch", "gcp")

        for h in range(NHEAD):
            E("vector", lambda e, h=h: e.tensor_scalar(
                ydrop[:, 128:256], E1[:], lgh[:, h:h + 1], 0.0, op0=ALU.mult, op1=ALU.min),
              wait=(["e1", "bcast"] if h == 0 else [f"mme:{h - 1}"]),
              inc="DV", event=f"mmu:{h}")
            E("scalar", lambda e: e.activation(ydrop[:, 256:384], ydrop[:, 128:256], AF.Exp),
              wait=[f"mmu:{h}"], inc="AC", event=f"mme:{h}")
            E("vector", lambda e, h=h: e.tensor_tensor(
                MM[:, h * 128:(h + 1) * 128], ydrop[:, 256:384], CAUS[:], op=ALU.mult),
              wait=[f"mme:{h}", "caus"], inc="DV", event=f"mmk:{h}")
        for g in range(NPAIR):
            E("vector", lambda e, g=g: e.tensor_scalar(
                ydrop[:, 384:512], JJ[:], lgp[:, g:g + 1], 0.0, op0=ALU.mult, op1=ALU.min),
              wait=(["iota2", "lgp", "mme:15"] if g == 0 else [f"gpe:{g - 1}"]),
              inc="DV", event=f"gpu:{g}")
            E("scalar", lambda e, g=g: e.activation(
                GP[:, g * 128:(g + 1) * 128], ydrop[:, 384:512], AF.Exp),
              wait=[f"gpu:{g}"], inc="AC", event=f"gpe:{g}")
        for h in range(NHEAD):
            E("vector", lambda e, h=h: e.tensor_scalar(
                ydrop[:, 128:192], KK[:], lgh[:, h:h + 1], 0.0, op0=ALU.mult, op1=ALU.min),
              wait=(["iota3", "mme:15"] if h == 0 else [f"gwe:{h - 1}"]),
              inc="DV", event=f"gwu:{h}")
            E("scalar", lambda e, h=h: e.activation(
                GW[:, h * 64:(h + 1) * 64], ydrop[:, 128:192], AF.Exp),
              wait=[f"gwu:{h}"], inc="AC", event=f"gwe:{h}")

        wi = [0]
        def prep_w_bf16(dst, src_ap):
            for mt in range(8):
                i = wi[0]
                E("sync", lambda e, mt=mt, src_ap=src_ap: e.dma_start(
                    out=xA[:, 0:1024], in_=src_ap[mt * 128:(mt + 1) * 128, :]),
                  wait=[f"wtr:{i - 1}"] if i > 0 else ["mm_al"], inc="LP", amt=16, event=f"wld:{i}")
                E("vector", lambda e: e.tensor_copy(xnbf[:, 0:1024], xA[:, 0:1024]),
                  wait=[f"wld:{i}"], inc="DV", event=f"wcast:{i}")
                for db in range(8):
                    E("scalar", lambda e, db=db, mt=mt, dst=dst: e.dma_start(
                        out=dst[:, db * 1024 + mt * 128: db * 1024 + (mt + 1) * 128],
                        in_=xnbf[:, db * 128:(db + 1) * 128], transpose=True),
                      wait=[f"wcast:{i}"] if db == 0 else [],
                      inc="TP", amt=16, event=f"wtr:{i}" if db == 7 else None, dma=True)
                wi[0] += 1
        prep_w_bf16(WwT, wwap)
        prep_w_bf16(WrT, wrap_)
        nw = wi[0]
        for db in range(8):
            E("vector", lambda e, db=db: e.tensor_scalar(
                WrT[:, db * 1024:(db + 1) * 1024], WrT[:, db * 1024:(db + 1) * 1024],
                alp[:, db:db + 1], None, op0=ALU.mult),
              wait=[f"wtr:{nw - 1}", "alp"] if db == 0 else [],
              inc="DV" if db == 7 else None, event="wrfold" if db == 7 else None)

        qi = [0]
        def prep_w_fp8(src_slice, lane_dst3):
            i = qi[0]
            E("sync", lambda e, src_slice=src_slice: e.dma_start(out=xA[:, 0:1024], in_=src_slice),
              wait=[f"q8t:{i - 1}"] if i > 0 else [f"wtr:{nw - 1}"], inc="LP", amt=16, event=f"qld:{i}")
            E("vector", lambda e: e.tensor_copy(xq8[:, 0:1024], xA[:, 0:1024]),
              wait=[f"qld:{i}"], inc="DV", event=f"qc1:{i}")
            E("vector", lambda e: e.tensor_copy(xnbf[:, 0:1024], xq8[:, 0:1024]),
              wait=[f"qc1:{i}"], inc="DV", event=f"qc2:{i}")
            for db in range(8):
                E("scalar", lambda e, db=db: e.dma_start(
                    out=XQTb[:, db * 128:(db + 1) * 128],
                    in_=xnbf[:, db * 128:(db + 1) * 128], transpose=True),
                  wait=([f"qc2:{i}"] + ([f"q8t:{i - 1}"] if i > 0 else [])) if db == 0 else [],
                  inc="TP", amt=16, event=f"qtr:{i}" if db == 7 else None, dma=True)
            E("gpsimd", lambda e, lane_dst3=lane_dst3: e.tensor_copy(
                lane_dst3, XQTb[:, 0:1024].rearrange("p (l t) -> p l t", l=8)),
              wait=[f"qtr:{i}"], inc="PL", event=f"q8t:{i}")
            qi[0] += 1
        for nt in range(32):
            d3 = W1T8[:].rearrange("p (l n) -> p l n", l=8)[:, :, nt * 128:(nt + 1) * 128]
            prep_w_fp8(w1ap[nt * 128:(nt + 1) * 128, :], d3)
        for mt in range(8):
            for pc in range(4):
                d3 = W2T8[:].rearrange("p (l n) -> p l n", l=32)[:, pc * 8:(pc + 1) * 8, mt * 128:(mt + 1) * 128]
                prep_w_fp8(w2ap[mt * 128:(mt + 1) * 128, pc * 1024:(pc + 1) * 1024], d3)

        E("vector", lambda e: e.tensor_scalar(ydrop[:, 0:128], IOT[:], 0.0, None, op0=ALU.is_equal),
          wait=["iota1"], inc="DV")
        E("vector", lambda e: e.tensor_copy(idb32[:], ydrop[:, 0:128]), inc="DV", event="idb")
        E("sync", lambda e: e.dma_start(
            out=xA[0:32, 0:128], in_=b1ap.rearrange("(a b) -> a b", a=32)),
          wait=[f"q8t:{qi[0] - 1}", "idb"], inc="LP", amt=16, event="b1l")
        E("vector", lambda e: e.tensor_copy(xq8[0:32, 0:128], xA[0:32, 0:128]),
          wait=["b1l"], inc="DV")
        E("vector", lambda e: e.tensor_copy(xnbf[0:32, 0:128], xq8[0:32, 0:128]), inc="DV", event="b1q")
        E("tensor", lambda e: e.matmul(
            pEF[:, 0:32], xnbf[0:32, 0:128], idb32[0:32, 0:32], start=True, stop=True),
          wait=["b1q", "idb"], inc="PE", event="b1mm")
        E("vector", lambda e: e.tensor_copy(b1qf[:], pEF[:, 0:32]), wait=["b1mm"], inc="DV", event="b1done")
        E("sync", lambda e: e.dma_start(out=tcomb[0:1, 0:1024], in_=b2ap.rearrange("(a b) -> a b", a=1)),
          wait=["b1l"], inc="LP", amt=16, event="b2l")
        E("vector", lambda e: e.tensor_copy(HQ8[0:1, 0:1024], tcomb[0:1, 0:1024]),
          wait=["b2l", "b1q"], inc="DV")
        E("vector", lambda e: e.tensor_copy(RD[0:1, 0:1024], HQ8[0:1, 0:1024]), inc="DV", event="b2q")
        for hf in range(2):
            E("tensor", lambda e, hf=hf: e.matmul(
                pAB[:, hf * 512:(hf + 1) * 512], onesbf[:], RD[0:1, hf * 512:(hf + 1) * 512],
                start=True, stop=True),
              wait=["b2q", "b1mm"] if hf == 0 else [], inc="PE" if hf == 1 else None,
              event="b2mm" if hf == 1 else None)
        E("vector", lambda e: e.tensor_copy(b2row[:], pAB[:]), wait=["b2mm"], inc="DV", event="b2done")

        if dbg2:
            pdump = [("p_iot", IOT[:], F32), ("p_caus", CAUS[:], F32), ("p_ydtmp", ydrop[:, 0:128], F32),
                     ("p_tcomb", tcomb[0:1, 0:1024], F32), ("p_hqrow", HQ8[0:1, 0:1024], FP8),
                     ("p_rdrow", RD[0:1, 0:1024], BF16), ("p_ones", onesbf[0:1, 0:128], BF16),
                     ("p_idb", idb32[:], BF16), ("p_mm", MM[:], BF16), ("p_gw", GW[:], BF16),
                     ("p_gp", GP[:], BF16), ("p_gcp", gcp[:], F32), ("p_dgm", DGM[:], BF16),
                     ("p_b2row", b2row[:], FP8)]
            first = True
            for nm, src, dt_ in pdump:
                dst = nc.dram_tensor(nm, list(src.shape), dt_, kind="ExternalOutput").ap()
                E("gpsimd", lambda e, dst=dst, src=src: e.dma_start(out=dst, in_=src),
                  wait=([(s, p.cnt.get(s, 0)) for s in ("LP", "TP", "PE", "DV", "AC", "PL")] if first else []),
                  inc="ST", amt=16, dma=True)
                first = False
        E("sync", lambda e: e.nop())
        prepw = [(s, p.cnt.get(s, 0)) for s in ("LP", "TP", "PE", "DV", "AC", "PL", "ST")]
        for eng in Prog.ENGS:
            E(eng, lambda e: e.nop(), wait=prepw)

        def W(name):
            return name if name in p.ev else None

        for c in range(n_chunks):
            s2, s3, s4 = c % 2, c % 3, c % 4
            g, cc = c // GRP, c % GRP
            gs = g % 2

            E("sync", lambda e, c=c, s2=s2: e.dma_start(
                out=xA[:, s2 * 1024:(s2 + 1) * 1024], in_=xap[c * 128:(c + 1) * 128, :]),
              wait=[W(f"xn:{c - 2}")], inc=f"LX{s2}", amt=16, event=f"xld:{c}")

            E("vector", lambda e, s2=s2: e.tensor_reduce(
                mu[:, s2:s2 + 1], xA[:, s2 * 1024:(s2 + 1) * 1024], axis=AXX, op=ALU.add),
              wait=[f"xld:{c}"], inc="DV", event=f"mu:{c}")
            E("scalar", lambda e, s2=s2: e.activation(
                xnf[:, 0:1024], xA[:, s2 * 1024:(s2 + 1) * 1024], AF.Square,
                accum_out=ssq[:, s2:s2 + 1]),
              wait=[f"xld:{c}", W(f"xq8:{c - 1}"), W(f"xnbf:{c - 1}")], inc="AC", event=f"ssq:{c}")
            E("vector", lambda e, s2=s2: e.tensor_scalar(
                mcol[:, s2:s2 + 1], mu[:, s2:s2 + 1], 1.0 / 1024.0, None, op0=ALU.mult),
              wait=[f"ssq:{c}", f"mu:{c}"])
            E("vector", lambda e, s2=s2: e.tensor_tensor(
                m2col[:, s2:s2 + 1], mcol[:, s2:s2 + 1], mcol[:, s2:s2 + 1], op=ALU.mult))
            E("vector", lambda e, s2=s2: e.scalar_tensor_tensor(
                vcol[:, s2:s2 + 1], ssq[:, s2:s2 + 1], 1.0 / 1024.0, m2col[:, s2:s2 + 1],
                op0=ALU.mult, op1=ALU.subtract), inc="DV", event=f"var:{c}")
            E("scalar", lambda e, s2=s2: e.activation(
                lvcol[:, s2:s2 + 1], vcol[:, s2:s2 + 1], AF.Ln, bias=CONSTS[:, 0:1]),
              wait=[f"var:{c}"])
            E("scalar", lambda e, s2=s2, s8=c % 8: e.activation(
                rstd[:, s8:s8 + 1], lvcol[:, s2:s2 + 1], AF.Exp, bias=0.0, scale=CONSTS[:, 1:2]),
              inc="AC", event=f"rstd:{c}")
            E("vector", lambda e, s2=s2, s8=c % 8: e.scalar_tensor_tensor(
                nmb[:, s8:s8 + 1], mcol[:, s2:s2 + 1], -1.0, rstd[:, s8:s8 + 1],
                op0=ALU.mult, op1=ALU.mult),
              wait=[f"rstd:{c}"], inc="DV", event=f"nmb:{c}")
            E("vector", lambda e, s2=s2, s8=c % 8: e.tensor_scalar(
                xnf[:, 0:1024], xA[:, s2 * 1024:(s2 + 1) * 1024],
                rstd[:, s8:s8 + 1], nmb[:, s8:s8 + 1], op0=ALU.mult, op1=ALU.add),
              wait=[f"nmb:{c}", f"ssq:{c}"], inc="DV", event=f"xn:{c}")

            E("vector", lambda e, s4=s4: e.tensor_copy(
                xnbf[:, s4 * 1024:(s4 + 1) * 1024], xnf[:, 0:1024]),
              wait=[f"xn:{c}", W(f"out:{c - 4}"), W(f"xnT:{c - 4}")], inc="DV", event=f"xnbf:{c}")
            E("gpsimd", lambda e: e.tensor_copy(xq8[:, 0:1024], xnf[:, 0:1024]),
              wait=[f"xn:{c}"], inc="PL", event=f"xq8:{c}")
            E("gpsimd", lambda e: e.tensor_copy(xqbf[:, 0:1024], xq8[:, 0:1024]),
              wait=[W(f"xqT:{c - 1}")], inc="PL", event=f"xqbf:{c}")

            for db in range(8):
                E("scalar", lambda e, db=db, s3=s3, s4=s4: e.dma_start(
                    out=XNF[:, db * 384 + s3 * 128: db * 384 + (s3 + 1) * 128],
                    in_=xnbf[:, s4 * 1024 + db * 128: s4 * 1024 + (db + 1) * 128], transpose=True),
                  wait=([f"xnbf:{c}", W(f"sc:{c - 3}:3"), W(f"rkgp:{c - 3}")] if db == 0 else []),
                  inc=f"TX{s3}", amt=16, event=(f"xnT:{c}" if db == 7 else None), dma=True)
            for db in range(8):
                E("scalar", lambda e, db=db: e.dma_start(
                    out=XQTb[:, db * 128:(db + 1) * 128],
                    in_=xqbf[:, db * 128:(db + 1) * 128], transpose=True),
                  wait=([f"xqbf:{c}", W(f"xqf8:{c - 1}")] if db == 0 else []),
                  inc="TQ", amt=16, event=(f"xqT:{c}" if db == 7 else None), dma=True)
            E("gpsimd", lambda e, s4=s4, gs=gs: e.tensor_copy(
                XQF8[:].rearrange("p (l t) -> p l t", l=8)[:, :, gs * 512 + s4 * 128: gs * 512 + (s4 + 1) * 128],
                XQTb[:, 0:1024].rearrange("p (l t) -> p l t", l=8)),
              wait=[f"xqT:{c}", W(f"ffn1:{g - 2}")], inc="PL", event=f"xqf8:{c}")

            E("gpsimd", lambda e, s3=s3: e.tensor_copy(
                WKF[:].rearrange("p (l t) -> p l t", l=8)[:, :, 1:128],
                XNF3[:, :, s3 * 128: s3 * 128 + 127]),
              wait=[f"xnT:{c}", W(f"sc:{c - 1}:3"), W(f"wkT:{c - 1}"), "prev0"], inc="PL")
            E("gpsimd", lambda e: e.tensor_copy(
                WKF[:].rearrange("p (l t) -> p l t", l=8)[:, :, 0:1], PREV3), inc="PL")
            E("gpsimd", lambda e, s3=s3: e.tensor_copy(
                PREV3, XNF3[:, :, s3 * 128 + 127: s3 * 128 + 128]),
              inc="PL", event=f"wkf:{c}")
            for db in range(8):
                E("scalar", lambda e, db=db: e.dma_start(
                    out=WKT[:, db * 128:(db + 1) * 128],
                    in_=WKF[:, db * 128:(db + 1) * 128], transpose=True),
                  wait=([f"wkf:{c}", W(f"upd:{c - 1}:1")] if db == 0 else []),
                  inc="TW", amt=16, event=(f"wkT:{c}" if db == 7 else None), dma=True)

            E("vector", lambda e, s3=s3: e.tensor_tensor(
                rkgp[:].rearrange("p (l t) -> p l t", l=8),
                XNF3[:, :, s3 * 128:(s3 + 1) * 128],
                GP[:].rearrange("p (l t) -> p l t", l=8), op=ALU.mult),
              wait=[f"xnT:{c}", W(f"reads:{c - 1}")], inc="DV", event=f"rkgp:{c}")

            vw = [f"xnT:{c}", W(f"vevac:{c - 1}")]
            if cc == 0 and g > 0:
                vw.append(f"y2:{g - 1}:3")
            for db in range(8):
                for hf in range(2):
                    E("tensor", lambda e, db=db, hf=hf, s3=s3: e.matmul(
                        pAB[:, hf * 512:(hf + 1) * 512],
                        XNF[:, db * 384 + s3 * 128: db * 384 + (s3 + 1) * 128],
                        WwT[:, db * 1024 + hf * 512: db * 1024 + (hf + 1) * 512],
                        start=(db == 0), stop=(db == 7)),
                      wait=(vw if (db == 0 and hf == 0) else []),
                      inc=("PE" if (db == 7 and hf == 1) else None),
                      event=(f"vproj:{c}" if (db == 7 and hf == 1) else None))
            first = True
            for h in range(NHEAD):
                off = h * 128 + (h % 2) * 64
                E("vector", lambda e, h=h, off=off: e.tensor_copy(
                    VZ[:, off:off + 64], pAB[:, h * 64:(h + 1) * 64]),
                  wait=([f"vproj:{c}", W(f"upd:{c - 1}:1")] if first else []), inc="DV")
                first = False
            E("vector", lambda e: e.tensor_tensor(gv[:, 0:1024], pAB[:], GW[:], op=ALU.mult),
              inc="DV", event=f"vevac:{c}")
            if dbg2 and c >= 11 and c <= 15:
                dvz = nc.dram_tensor(f"dbg_vz{c}", [128, 2048], BF16, kind="ExternalOutput").ap()
                dgv = nc.dram_tensor(f"dbg_gv{c}", [128, 1024], BF16, kind="ExternalOutput").ap()
                dxn = nc.dram_tensor(f"dbg_xn{c}", [128, 3072], BF16, kind="ExternalOutput").ap()
                E("gpsimd", lambda e, dvz=dvz: e.dma_start(out=dvz, in_=VZ[:]),
                  wait=[f"vevac:{c}"], inc="GA", amt=16, dma=True)
                E("gpsimd", lambda e, dgv=dgv: e.dma_start(out=dgv, in_=gv[:]),
                  inc="GA", amt=16, dma=True)
                E("gpsimd", lambda e, dxn=dxn: e.dma_start(out=dxn, in_=XNF[:]),
                  inc="GA", amt=16, event=f"dbgd:{c}", dma=True)
                p.q["tensor"].append((lambda e, _w=p.ev[f"dbgd:{c}"]: e.wait_ge(sems[_w[0]], _w[1]), [], None, 1))

            SCB = [(pEF, 0), (pEF, 512), (pGH, 0), (pGH, 512)]

            if no_scan:
                continue

            def sc_wave(s):
                sw = [f"wkf:{c}", f"xnT:{c}"]
                if s == 0:
                    sw.append(W(f"sm:{c - 1}:3"))
                    sw.append(W(f"state:{c - 1}"))
                    if cc == 0 and g > 0:
                        sw.append(W(f"wev:{g - 1}:31"))
                else:
                    sw.append(f"sm:{c}:{s - 1}")
                for j in range(4):
                    h = s * 4 + j
                    db, rh = h // 2, (h % 2) * 64
                    pt, off = SCB[j]
                    E("tensor", lambda e, db=db, rh=rh, pt=pt, off=off, s3=s3: e.matmul(
                        pt[:, off:off + 128],
                        WKF[rh:rh + 64, db * 128:(db + 1) * 128],
                        XNF[rh:rh + 64, db * 384 + s3 * 128: db * 384 + (s3 + 1) * 128],
                        start=True, stop=True),
                      wait=(sw if j == 0 else []),
                      inc=("PE" if j == 3 else None), event=(f"sc:{c}:{s}" if j == 3 else None))

            def sm_evac(s):
                smw = [f"sc:{c}:{s}"]
                if s >= 2:
                    smw.append(W(f"iw:{c}:{s - 2}"))
                elif c > 0:
                    smw.append(W(f"iw:{c - 1}:{s + 2}"))
                if s == 0:
                    smw.append(W(f"reads:{c - 1}"))
                sl = (s % 2) * 512
                E("vector", lambda e, s=s, sl=sl: e.tensor_tensor(
                    SM[:, sl:sl + 256].rearrange("p (a b) -> p a b", a=2),
                    pEF[:].rearrange("p (a b) -> p a b", a=2)[:, :, 0:128],
                    MM[:, s * 512:s * 512 + 256].rearrange("p (a b) -> p a b", a=2),
                    op=ALU.mult),
                  wait=smw, inc="DV")
                E("vector", lambda e, s=s, sl=sl: e.tensor_tensor(
                    SM[:, sl + 256:sl + 512].rearrange("p (a b) -> p a b", a=2),
                    pGH[:].rearrange("p (a b) -> p a b", a=2)[:, :, 0:128],
                    MM[:, s * 512 + 256:(s + 1) * 512].rearrange("p (a b) -> p a b", a=2),
                    op=ALU.mult),
                  inc="DV", event=f"sm:{c}:{s}")

            def inter_pair(s):
                rw = [f"rkgp:{c}", W(f"state:{c - 1}"), f"vevac:{c}",
                      (f"rd:{c}:{s - 1}" if s > 0 else W(f"rd:{c - 1}:3"))]
                if s == 0 and c >= GRP and cc == 0:
                    rw.append(W(f"out:{c - 1}"))
                for q in range(2):
                    gpr = 2 * s + q
                    E("tensor", lambda e, gpr=gpr, q=q: e.matmul(
                        pCD[:, q * 512:q * 512 + 128],
                        SBF[:, gpr * 128:(gpr + 1) * 128],
                        rkgp[:, gpr * 128:(gpr + 1) * 128],
                        start=True, stop=False),
                      wait=(rw if q == 0 else []))

            def intra_wave(s, extra=()):
                for j in range(4):
                    h = s * 4 + j
                    q = j // 2
                    last = (j % 2) == 1
                    sl = (s % 2) * 512
                    E("tensor", lambda e, h=h, q=q, sl=sl, j=j, last=last: e.matmul(
                        pCD[:, q * 512:q * 512 + 128],
                        VZ[:, h * 128:(h + 1) * 128],
                        SM[:, sl + j * 128: sl + (j + 1) * 128],
                        start=False, stop=last),
                      wait=(list(extra) if j == 0 else []),
                      inc=("PE" if j == 3 else None), event=(f"iw:{c}:{s}" if j == 3 else None))

            def rd_evac(s):
                for q in range(2):
                    gpr = 2 * s + q
                    E("vector", lambda e, gpr=gpr, q=q, s4=s4: e.tensor_copy(
                        RD[:, s4 * 1024 + gpr * 128:(s4 * 1024) + (gpr + 1) * 128],
                        pCD[:, q * 512:q * 512 + 128]),
                      wait=([f"iw:{c}:{s}"] + ([W(f"rproj:{c - 4}")] if s == 0 else [])) if q == 0 else [],
                      inc="DV", event=(f"rd:{c}:{s}" if q == 1 else None))

            sc_wave(0)
            sm_evac(0)
            inter_pair(0)
            sc_wave(1)
            sm_evac(1)
            intra_wave(0, extra=[f"sm:{c}:0"])
            rd_evac(0)
            inter_pair(1)
            sc_wave(2)
            sm_evac(2)
            intra_wave(1, extra=[f"sm:{c}:1"])
            rd_evac(1)
            inter_pair(2)
            sc_wave(3)
            sm_evac(3)
            intra_wave(2, extra=[f"sm:{c}:2"])
            rd_evac(2)
            inter_pair(3)
            intra_wave(3, extra=[f"sm:{c}:3"])
            rd_evac(3)
            p.ev[f"reads:{c}"] = p.ev[f"iw:{c}:3"]
            p.ev[f"rd:{c}"] = p.ev[f"rd:{c}:3"]

            for u in range(2):
                uw = [f"wkT:{c}", f"vevac:{c}", f"sm:{c}:3"]
                if u == 0:
                    uw.append(W(f"state:{c - 1}"))
                else:
                    uw.append(f"stdec:{c}:0")
                for q in range(4):
                    gpr = 4 * u + q
                    pt, off = SCB[q]
                    E("tensor", lambda e, gpr=gpr, pt=pt, off=off: e.matmul(
                        pt[:, off:off + 128],
                        WKT[:, gpr * 128:(gpr + 1) * 128],
                        gv[:, gpr * 128:(gpr + 1) * 128],
                        start=True, stop=True),
                      wait=(uw if q == 0 else []),
                      inc=("PE" if q == 3 else None),
                      event=(f"upd:{c}:{u}" if q == 3 else None))
                for q in range(4):
                    gpr = 4 * u + q
                    E("vector", lambda e, gpr=gpr: e.tensor_scalar(
                        SF32a[:, gpr * 128:(gpr + 1) * 128], SF32a[:, gpr * 128:(gpr + 1) * 128],
                        gcp[:, gpr:gpr + 1], None, op0=ALU.mult),
                      wait=[f"upd:{c}:{u}", "gcp"] if q == 0 else [])
                b0 = 4 * u * 128
                E("vector", lambda e, b0=b0: e.tensor_tensor(
                    SF32a[:, b0:b0 + 256].rearrange("p (a b) -> p a b", a=2),
                    SF32a[:, b0:b0 + 256].rearrange("p (a b) -> p a b", a=2),
                    pEF[:].rearrange("p (a b) -> p a b", a=2)[:, :, 0:128],
                    op=ALU.add), inc="DV")
                E("vector", lambda e, b0=b0: e.tensor_tensor(
                    SF32a[:, b0 + 256:b0 + 512].rearrange("p (a b) -> p a b", a=2),
                    SF32a[:, b0 + 256:b0 + 512].rearrange("p (a b) -> p a b", a=2),
                    pGH[:].rearrange("p (a b) -> p a b", a=2)[:, :, 0:128],
                    op=ALU.add), inc="DV", event=f"stdec:{c}:{u}")
            E("vector", lambda e: e.tensor_tensor(SBF[:], SF32a[:], DGM[:], op=ALU.mult),
              inc="DV", event=f"state:{c}")

            if cc == GRP - 1 and not no_b:
                for m in range(32):
                    f1w = [f"xqf8:{c}", W(f"state:{c}")] if m < 2 else [f"xqf8:{c}"]
                    if m >= 2:
                        f1w.append(f"wev:{g}:{m - 2}")
                    elif g > 0:
                        f1w.append(f"wev:{g - 1}:{30 + m}")
                    for db in range(8):
                        E("tensor", lambda e, m=m, db=db, gs=gs: e.matmul(
                            pGH[:, (m % 2) * 512:(m % 2 + 1) * 512],
                            W1T8[:, db * 4096 + m * 128: db * 4096 + (m + 1) * 128],
                            XQF8[:, db * 1024 + gs * 512: db * 1024 + (gs + 1) * 512],
                            start=(db == 0), stop=(db == 7)),
                          wait=(f1w if db == 0 else []),
                          inc=("PE" if db == 7 else None),
                          event=(f"ffn1m:{g}:{m}" if db == 7 else None))
                    E("scalar", lambda e, m=m: e.activation(
                        wbf[:, (m % 2) * 512:(m % 2 + 1) * 512], pGH[:, (m % 2) * 512:(m % 2 + 1) * 512],
                        AF.Relu, bias=b1qf[:, m:m + 1]),
                      wait=[f"ffn1m:{g}:{m}", W(f"hq:{g}:{m - 2}")], inc="AC", event=f"wev:{g}:{m}")
                    hqw = [f"wev:{g}:{m}"]
                    if m == 0 and g >= 1:
                        hqw.append(f"ffn2:{g - 1}:3")
                    E("gpsimd", lambda e, m=m: e.tensor_copy(
                        HQ8[:, m * 512:(m + 1) * 512], wbf[:, (m % 2) * 512:(m % 2 + 1) * 512]),
                      wait=hqw, inc="PL", event=f"hq:{g}:{m}")
                p.ev[f"ffn1:{g}"] = p.ev[f"ffn1m:{g}:31"]

                for cc2 in range(GRP):
                    c2 = g * GRP + cc2
                    sB2, sB4 = c2 % 2, c2 % 4
                    E("sync", lambda e, c2=c2, sB2=sB2: e.dma_start(
                        out=maskb[:, sB2 * 1024:(sB2 + 1) * 1024], in_=mkap[c2 * 128:(c2 + 1) * 128, :]),
                      wait=[W(f"ydrop:{c2 - 2}")], inc=f"LM{sB2}", amt=16, event=f"mld:{c2}")
                    f2w = [f"hq:{g}:31"]
                    f2w.append(f"y2:{g}:{cc2 - 1}" if cc2 > 0 else f"vevac:{g * GRP + 3}")
                    for nb in range(32):
                        for hf in range(2):
                            E("tensor", lambda e, nb=nb, hf=hf, cc2=cc2: e.matmul(
                                pAB[:, hf * 512:(hf + 1) * 512],
                                HQ8[:, nb * 512 + cc2 * 128: nb * 512 + (cc2 + 1) * 128],
                                W2T8[:, nb * 1024 + hf * 512: nb * 1024 + (hf + 1) * 512],
                                start=(nb == 0), stop=(nb == 31)),
                              wait=(f2w if (nb == 0 and hf == 0) else []),
                              inc=("PE" if (nb == 31 and hf == 1) else None),
                              event=(f"ffn2:{g}:{cc2}" if (nb == 31 and hf == 1) else None))
                    E("vector", lambda e: e.tensor_tensor(y2b[:], pAB[:], b2row[:], op=ALU.add),
                      wait=[f"ffn2:{g}:{cc2}", "b2done"], inc="DV", event=f"y2:{g}:{cc2}")
                    E("vector", lambda e, sB2=sB2: e.tensor_tensor(
                        ydrop[:], y2b[:], maskb[:, sB2 * 1024:(sB2 + 1) * 1024], op=ALU.mult),
                      wait=[f"mld:{c2}"], inc="DV", event=f"ydrop:{c2}")
                    rpw = [f"rd:{c2}", "wrfold", f"y2:{g}:{cc2}"]
                    rpw.append(f"out:{c2 - 1}" if cc2 > 0 else f"rd:{g * GRP + 3}")
                    for jb in range(8):
                        for hf in range(2):
                            E("tensor", lambda e, jb=jb, hf=hf, sB4=sB4: e.matmul(
                                pCD[:, hf * 512:(hf + 1) * 512],
                                RD[:, sB4 * 1024 + jb * 128: sB4 * 1024 + (jb + 1) * 128],
                                WrT[:, jb * 1024 + hf * 512: jb * 1024 + (hf + 1) * 512],
                                start=(jb == 0), stop=(jb == 7)),
                              wait=(rpw if (jb == 0 and hf == 0) else []),
                              inc=("PE" if (jb == 7 and hf == 1) else None),
                              event=(f"rproj:{c2}" if (jb == 7 and hf == 1) else None))
                    E("vector", lambda e, sB4=sB4: e.scalar_tensor_tensor(
                        tcomb[:], pCD[:], 0.0, xnbf[:, sB4 * 1024:(sB4 + 1) * 1024],
                        op0=ALU.add, op1=ALU.add),
                      wait=[f"rproj:{c2}", f"ydrop:{c2}", f"xnbf:{c2}", W(f"st:{c2 - 1}")])
                    E("vector", lambda e: e.tensor_tensor(tcomb[:], tcomb[:], ydrop[:], op=ALU.add),
                      inc="DV", event=f"out:{c2}")
                    if not no_accum:
                        E("gpsimd", lambda e, c2=c2: e.dma_start(
                            out=tcomb[:], in_=xap[c2 * 128:(c2 + 1) * 128, :], accum_op=ALU.add),
                          wait=[f"out:{c2}"], inc="GA", amt=16, event=f"acc:{c2}", dma=True)
                    E("gpsimd", lambda e, c2=c2: e.dma_start(
                        out=yap[c2 * 128:(c2 + 1) * 128, :], in_=tcomb[:]),
                      wait=[f"acc:{c2}" if not no_accum else f"out:{c2}"],
                      inc="ST", amt=16, event=f"st:{c2}", dma=True)

        if dbg2:
            dbg_b1 = nc.dram_tensor("dbg_b1", [128, 32], F32, kind="ExternalOutput").ap()
            dbg_b2 = nc.dram_tensor("dbg_b2", [128, 1024], FP8, kind="ExternalOutput").ap()
            allw2 = [(s, p.cnt.get(s, 0)) for s in p.cnt]
            E("gpsimd", lambda e: e.dma_start(out=dbg_b1, in_=b1qf[:]), wait=allw2, inc="ST", amt=16, dma=True)
            E("gpsimd", lambda e: e.dma_start(out=dbg_b2, in_=b2row[:]), inc="ST", amt=16, dma=True)
        if dbg:
            dbg_rd = nc.dram_tensor("dbg_rd", [128, 4096], BF16, kind="ExternalOutput").ap()
            dbg_hq = nc.dram_tensor("dbg_hq", [128, 32 * 512], FP8, kind="ExternalOutput").ap()
            dbg_sbf = nc.dram_tensor("dbg_sbf", [128, 1024], BF16, kind="ExternalOutput").ap()
            dbg_vz = nc.dram_tensor("dbg_vz", [128, 2048], BF16, kind="ExternalOutput").ap()
            dbg_xq = nc.dram_tensor("dbg_xq", [128, 8192], FP8, kind="ExternalOutput").ap()
            allw = [(s, p.cnt.get(s, 0)) for s in p.cnt]
            E("gpsimd", lambda e: e.dma_start(out=dbg_rd, in_=RD[:]), wait=allw, inc="ST", amt=16, dma=True)
            E("gpsimd", lambda e: e.dma_start(out=dbg_hq, in_=HQ8[:]), inc="ST", amt=16, dma=True)
            E("gpsimd", lambda e: e.dma_start(out=dbg_sbf, in_=SBF[:]), inc="ST", amt=16, dma=True)
            E("gpsimd", lambda e: e.dma_start(out=dbg_vz, in_=VZ[:]), inc="ST", amt=16, dma=True)
            E("gpsimd", lambda e: e.dma_start(out=dbg_xq, in_=XQF8[:]), inc="ST", amt=16, dma=True)
        if "ST" in p.cnt:
            E("gpsimd", lambda e: e.wait_ge(sems["ST"], p.cnt["ST"]), dma=True)

        with nc.Block() as block:
            @block.sync
            def _(eng):
                p.replay(eng, "sync", sems)

            @block.tensor
            def _(eng):
                p.replay(eng, "tensor", sems)

            @block.vector
            def _(eng):
                p.replay(eng, "vector", sems)

            @block.scalar
            def _(eng):
                p.replay(eng, "scalar", sems)

            @block.gpsimd
            def _(eng):
                p.replay(eng, "gpsimd", sems)
    return nc


_CACHE = {}


def _bernoulli_keep(shape, k0=0, k1=42, p=0.8):
    rot1 = (13, 15, 26, 6)
    rot2 = (17, 29, 16, 24)
    ks = (np.uint32(k0), np.uint32(k1),
          np.uint32(k0) ^ np.uint32(k1) ^ np.uint32(0x1BD11BDA))
    n = int(np.prod(shape))
    x1 = np.arange(n, dtype=np.uint32)
    x0 = np.zeros(n, dtype=np.uint32)
    x0 = (x0 + ks[0]).astype(np.uint32)
    x1 = (x1 + ks[1]).astype(np.uint32)
    for d in range(5):
        for r in (rot1 if d % 2 == 0 else rot2):
            x0 = (x0 + x1).astype(np.uint32)
            x1 = ((x1 << np.uint32(r)) | (x1 >> np.uint32(32 - r))).astype(np.uint32)
            x1 = (x1 ^ x0).astype(np.uint32)
        x0 = (x0 + ks[(d + 1) % 3]).astype(np.uint32)
        x1 = (x1 + ks[(d + 2) % 3] + np.uint32(d + 1)).astype(np.uint32)
    bits = x0 ^ x1
    u = ((bits >> np.uint32(9)) | np.uint32(0x3F800000)).view(np.float32) - np.float32(1.0)
    return (u < np.float32(p)).reshape(shape)


def kernel(L0_attn_out, L0_attn_W_write, L0_attn_W_read, L0_attn_decay, L0_attn_log_alpha,
           L0_ffn_W1, L0_ffn_b1, L0_ffn_W2, L0_ffn_b2):
    x = np.ascontiguousarray(np.asarray(L0_attn_out, dtype=np.float32))
    B = x.shape[0]
    keep = _bernoulli_keep(x.shape)
    mask = np.where(keep, np.float32(1.25), np.float32(0.0)).astype(ml_dtypes.float8_e5m2)

    if "nc" not in _CACHE:
        _CACHE["nc"] = build_nc()
    nc = _CACHE["nc"]

    ww = np.ascontiguousarray(np.asarray(L0_attn_W_write, dtype=np.float32))
    wr = np.ascontiguousarray(np.asarray(L0_attn_W_read, dtype=np.float32))
    w1 = np.ascontiguousarray(np.asarray(L0_ffn_W1, dtype=np.float32))
    w2 = np.ascontiguousarray(np.asarray(L0_ffn_W2, dtype=np.float32))
    b1 = np.ascontiguousarray(np.asarray(L0_ffn_b1, dtype=np.float32))
    b2 = np.ascontiguousarray(np.asarray(L0_ffn_b2, dtype=np.float32))
    dc = np.ascontiguousarray(np.asarray(L0_attn_decay, dtype=np.float32))
    la = np.ascontiguousarray(np.asarray(L0_attn_log_alpha, dtype=np.float32))

    in_maps = []
    for b in range(B):
        in_maps.append({
            "x": np.ascontiguousarray(x[b]),
            "mask": np.ascontiguousarray(mask[b]),
            "Ww": ww, "Wr": wr, "W1": w1, "W2": w2,
            "b1": b1, "b2": b2, "decay": dc, "logalpha": la,
        })
    res = run_bass_kernel_spmd(nc, in_maps, core_ids=list(range(B)))
    out = np.stack([np.asarray(res.results[b]["y"]) for b in range(B)], axis=0)
    return out.astype(np.float32)


# version 07
# speedup vs baseline: 1.0961x; 1.0961x over previous
import numpy as np
import ml_dtypes
import concourse.bass as bass
import concourse.mybir as mybir
from concourse.bass_utils import run_bass_kernel_spmd
from contextlib import ExitStack

F32 = mybir.dt.float32
BF16 = mybir.dt.bfloat16
FP8 = mybir.dt.float8e5
AF = mybir.ActivationFunctionType
ALU = mybir.AluOpType
AXX = mybir.AxisListType.X

T, D = 8192, 1024
CH = 128
NCH = T // CH
GRP = 4
NHEAD, HD, NPAIR = 16, 64, 8


class Prog:
    ENGS = ("sync", "tensor", "vector", "scalar", "gpsimd")

    def __init__(self):
        self.q = {e: [] for e in self.ENGS}
        self.cnt = {}
        self.ev = {}

    SELF_SEM = {"vector": "DV", "scalar": "AC", "gpsimd": "PL"}

    def emit(self, eng, fn, wait=(), inc=None, amt=1, event=None, dma=False):
        waits = []
        for w in wait:
            if w is None:
                continue
            waits.append(self.ev[w] if isinstance(w, str) else w)
        if not dma and eng in self.SELF_SEM:
            ss = self.SELF_SEM[eng]
            assert inc in (None, ss), (eng, inc)
            waits.append((ss, self.cnt.get(ss, 0)))
            inc, amt = ss, 1
        self.q[eng].append((fn, waits, inc, amt))
        if inc is not None:
            self.cnt[inc] = self.cnt.get(inc, 0) + amt
            if event is not None:
                self.ev[event] = (inc, self.cnt[inc])
        else:
            assert event is None, event

    def replay(self, eng_obj, name, sems):
        waited = {}
        for fn, waits, inc, amt in self.q[name]:
            for s, v in waits:
                if waited.get(s, -1) >= v:
                    continue
                eng_obj.wait_ge(sems[s], v)
                waited[s] = v
            ins = fn(eng_obj)
            if inc is not None:
                ins.then_inc(sems[inc], amt)


def build_nc(n_chunks=NCH, no_b=False, no_accum=False, no_pe=False, no_attn=False, no_scan=False, dbg=False, dbg2=False):
    nc = bass.Bass("TRN2", target_bir_lowering=False, dynamic_dma_scratch_size=16384)
    xap = nc.dram_tensor("x", [T, D], F32, kind="ExternalInput").ap()
    mkap = nc.dram_tensor("mask", [T, D], FP8, kind="ExternalInput").ap()
    wwap = nc.dram_tensor("Ww", [D, D], F32, kind="ExternalInput").ap()
    wrap_ = nc.dram_tensor("Wr", [D, D], F32, kind="ExternalInput").ap()
    w1ap = nc.dram_tensor("W1", [4 * D, D], F32, kind="ExternalInput").ap()
    w2ap = nc.dram_tensor("W2", [D, 4 * D], F32, kind="ExternalInput").ap()
    b1ap = nc.dram_tensor("b1", [4 * D], F32, kind="ExternalInput").ap()
    b2ap = nc.dram_tensor("b2", [D], F32, kind="ExternalInput").ap()
    dcap = nc.dram_tensor("decay", [NHEAD], F32, kind="ExternalInput").ap()
    laap = nc.dram_tensor("logalpha", [NHEAD], F32, kind="ExternalInput").ap()
    yap = nc.dram_tensor("y", [T, D], F32, kind="ExternalOutput").ap()

    p = Prog()
    E = p.emit

    def W(name):
        return name if name in p.ev else None
    with ExitStack() as ctx:
        def sb(name, shape, dt):
            return ctx.enter_context(nc.sbuf_tensor(name, shape, dt))

        WwT = sb("WwT", [128, 8 * 1024], BF16)
        WrT = sb("WrT", [128, 8 * 1024], BF16)
        W1T8 = sb("W1T8", [128, 8 * 4096], FP8)
        W2T8 = sb("W2T8", [128, 32 * 1024], FP8)
        b1qf = sb("b1qf", [128, 32], F32)
        b2row = sb("b2row", [128, 1024], FP8)
        MM = sb("MM", [128, 16 * 128], BF16)
        GP = sb("GP", [128, 8 * 128], BF16)
        GW = sb("GW", [128, 16 * 64], BF16)
        CONSTS = sb("CONSTS", [128, 4], F32)
        gcp = sb("gcp", [128, 8], F32)
        xA = sb("xA", [128, 2 * 1024], F32)
        xnbf = sb("xnbf", [128, 4 * 1024], BF16)
        prep_ctx = ExitStack()

        def pb(name, shape, dt):
            return prep_ctx.enter_context(nc.sbuf_tensor(name, shape, dt))

        IOT = pb("IOT", [128, 128], F32)
        JJ = pb("JJ", [128, 128], F32)
        KK = pb("KK", [128, 64], F32)
        CAUS = pb("CAUS", [128, 128], F32)
        E1 = pb("E1", [128, 128], F32)
        idb32 = pb("idb32", [128, 128], BF16)
        onesf = pb("onesf", [1, 128], F32)
        onesbf = pb("onesbf", [1, 128], BF16)
        ones11 = pb("ones11", [1, 1], BF16)
        rowA = pb("rowA", [1, 64], F32)
        lgh = pb("lgh", [128, 16], F32)
        alh = pb("alh", [128, 16], F32)
        gch = pb("gch", [128, 16], F32)
        lgp = pb("lgp", [128, 8], F32)
        alp = pb("alp", [128, 8], F32)
        prep_ctx.close()
        xnf = sb("xnf", [128, 1024], F32)
        xq8 = sb("xq8", [128, 1024], FP8)
        xqbf = sb("xqbf", [128, 1024], BF16)
        XQTb = sb("XQTb", [128, 1024], BF16)
        XNF = sb("XNF", [128, 8 * 384], BF16)
        XQF8 = sb("XQF8", [128, 8 * 1024], FP8)
        WKF = sb("WKF", [128, 1024], BF16)
        WKT = sb("WKT", [128, 1024], BF16)
        PREV = sb("PREV", [128, 8], BF16)
        rkgp = sb("rkgp", [128, 1024], BF16)
        VZ = sb("VZ", [128, 16 * 128], BF16)
        gv = sb("gv", [128, 1024], BF16)
        SM = sb("SM", [128, 2 * 512], BF16)
        RD = sb("RD", [128, 4 * 1024], BF16)
        SF32a = sb("SF32a", [128, 1024], F32)
        SBF = sb("SBF", [128, 1024], BF16)
        DGM = sb("DGM", [128, 1024], BF16)
        HQ8 = sb("HQ8", [128, 32 * 512], FP8)
        wbf = sb("wbf", [128, 2 * 512], BF16)
        maskb = sb("maskb", [128, 2 * 1024], FP8)
        y2b = sb("y2b", [128, 1024], BF16)
        ydrop = sb("ydrop", [128, 1024], F32)
        tcomb = sb("tcomb", [128, 1024], F32)
        mu = sb("mu", [128, 2], F32)
        ssq = sb("ssq", [128, 2], F32)
        mcol = sb("mcol", [128, 2], F32)
        m2col = sb("m2col", [128, 2], F32)
        vcol = sb("vcol", [128, 2], F32)
        lvcol = sb("lvcol", [128, 2], F32)
        rstd = sb("rstd", [128, 8], F32)
        nmb = sb("nmb", [128, 8], F32)

        pAB = ctx.enter_context(nc.psum_tensor("pAB", [128, 1024], F32))
        pCD = ctx.enter_context(nc.psum_tensor("pCD", [128, 1024], F32))
        pEF = ctx.enter_context(nc.psum_tensor("pEF", [128, 1024], F32))
        pGH = ctx.enter_context(nc.psum_tensor("pGH", [128, 1024], F32))

        sems = {}
        for s in ("LX0", "LX1", "LM0", "LM1", "LP", "TX0", "TX1", "TX2",
                  "TQ", "TW", "TP", "GA", "ST", "PE", "DV", "AC", "PL"):
            sems[s] = ctx.enter_context(nc.semaphore(s))

        XNF3 = XNF[:].rearrange("p (l t) -> p l t", l=8)
        PREV3 = PREV[:].rearrange("p (l t) -> p l t", l=8)

        E("sync", lambda e: e.dma_start(out=rowA[0:1, 0:16], in_=dcap.rearrange("(a b) -> a b", a=1)),
          inc="LP", amt=16)
        E("sync", lambda e: e.dma_start(out=rowA[0:1, 32:48], in_=laap.rearrange("(a b) -> a b", a=1)),
          inc="LP", amt=16, event="ld_small")
        p.ev["ld_decay"] = p.ev["ld_small"]
        p.ev["ld_la"] = p.ev["ld_small"]

        E("gpsimd", lambda e: e.iota(IOT[:], pattern=[[1, 128]], base=0, channel_multiplier=-1,
                                     allow_small_or_imprecise_dtypes=True), inc="PL", event="iota1")
        E("gpsimd", lambda e: e.iota(JJ[:], pattern=[[1, 128]], base=0, channel_multiplier=0,
                                     allow_small_or_imprecise_dtypes=True), inc="PL", event="iota2")
        E("gpsimd", lambda e: e.iota(KK[:], pattern=[[0, 64]], base=127, channel_multiplier=-1,
                                     allow_small_or_imprecise_dtypes=True), inc="PL", event="iota3")
        E("gpsimd", lambda e: e.memset(PREV[:], 0.0), inc="PL", event="prev0")
        E("vector", lambda e: e.memset(SF32a[:], 0.0))
        E("vector", lambda e: e.memset(SBF[:], 0.0))
        E("vector", lambda e: e.memset(VZ[:], 0.0))
        E("vector", lambda e: e.memset(DGM[:], 0.0))
        E("vector", lambda e: e.memset(CONSTS[:, 0:1], 1e-5))
        E("vector", lambda e: e.memset(CONSTS[:, 1:2], -0.5))
        E("vector", lambda e: e.memset(CONSTS[:, 2:3], 128.0))
        E("vector", lambda e: e.memset(onesf[:], 1.0))
        E("vector", lambda e: e.memset(onesbf[:], 1.0))
        E("vector", lambda e: e.memset(ones11[:], 1.0), inc="DV", event="memsets")
        E("vector", lambda e: e.tensor_scalar(CAUS[:], IOT[:], 0.5, None, op0=ALU.is_ge),
          wait=["iota1"], inc="DV", event="caus")
        E("vector", lambda e: e.tensor_scalar(E1[:], IOT[:], -1.0, None, op0=ALU.add),
          wait=["iota1"], inc="DV", event="e1")

        E("scalar", lambda e: e.activation(rowA[0:1, 16:32], rowA[0:1, 0:16], AF.Sigmoid),
          wait=["ld_decay"], inc="AC", event="grow")
        E("scalar", lambda e: e.activation(rowA[0:1, 0:16], rowA[0:1, 16:32], AF.Ln),
          wait=["grow"], inc="AC", event="lgrow")
        E("scalar", lambda e: e.activation(rowA[0:1, 48:64], rowA[0:1, 32:48], AF.Exp),
          wait=["ld_la"], inc="AC", event="alrow")

        E("tensor", lambda e: e.matmul(pEF[0:128, 0:16], onesf[:], rowA[0:1, 0:16], start=True, stop=True),
          wait=["lgrow", "memsets"], inc="PE", event="mm_lg")
        E("tensor", lambda e: e.matmul(pGH[0:128, 0:16], onesf[:], rowA[0:1, 48:64], start=True, stop=True),
          wait=["alrow"], inc="PE", event="mm_al")
        E("vector", lambda e: e.tensor_copy(lgh[:], pEF[:, 0:16]), wait=["mm_lg"], inc="DV")
        E("vector", lambda e: e.tensor_copy(alh[:], pGH[:, 0:16]), wait=["mm_al"], inc="DV", event="bcast")
        E("scalar", lambda e: e.activation(gch[:], lgh[:], AF.Exp, bias=0.0, scale=CONSTS[:, 2:3]),
          wait=["bcast"], inc="AC", event="gch")

        def halves(dst, src, ev_w, ev_name):
            for g in range(NPAIR):
                E("vector", lambda e, g=g, dst=dst, src=src:
                  e.tensor_copy(dst[0:64, g:g + 1], src[0:64, 2 * g:2 * g + 1]),
                  wait=[ev_w] if g == 0 else [])
                E("vector", lambda e, g=g, dst=dst, src=src:
                  e.tensor_copy(dst[64:128, g:g + 1], src[64:128, 2 * g + 1:2 * g + 2]),
                  inc="DV" if g == NPAIR - 1 else None,
                  event=ev_name if g == NPAIR - 1 else None)
        for h in range(NHEAD):
            gpr, rh = h // 2, (h % 2) * 64
            E("vector", lambda e, gpr=gpr, rh=rh: e.memset(
                DGM[rh:rh + 64, gpr * 128 + rh: gpr * 128 + rh + 64], 1.0))
        halves(lgp, lgh, "bcast", "lgp")
        halves(alp, alh, "bcast", "alp")
        halves(gcp, gch, "gch", "gcp")

        for h in range(NHEAD):
            E("vector", lambda e, h=h: e.tensor_scalar(
                ydrop[:, 128:256], E1[:], lgh[:, h:h + 1], 0.0, op0=ALU.mult, op1=ALU.min),
              wait=(["e1", "bcast"] if h == 0 else [f"mme:{h - 1}"]),
              inc="DV", event=f"mmu:{h}")
            E("scalar", lambda e: e.activation(ydrop[:, 256:384], ydrop[:, 128:256], AF.Exp),
              wait=[f"mmu:{h}"], inc="AC", event=f"mme:{h}")
            E("vector", lambda e, h=h: e.tensor_tensor(
                MM[:, h * 128:(h + 1) * 128], ydrop[:, 256:384], CAUS[:], op=ALU.mult),
              wait=[f"mme:{h}", "caus"], inc="DV", event=f"mmk:{h}")
        for g in range(NPAIR):
            E("vector", lambda e, g=g: e.tensor_scalar(
                ydrop[:, 384:512], JJ[:], lgp[:, g:g + 1], 0.0, op0=ALU.mult, op1=ALU.min),
              wait=(["iota2", "lgp", "mme:15"] if g == 0 else [f"gpe:{g - 1}"]),
              inc="DV", event=f"gpu:{g}")
            E("scalar", lambda e, g=g: e.activation(
                GP[:, g * 128:(g + 1) * 128], ydrop[:, 384:512], AF.Exp),
              wait=[f"gpu:{g}"], inc="AC", event=f"gpe:{g}")
        for h in range(NHEAD):
            E("vector", lambda e, h=h: e.tensor_scalar(
                ydrop[:, 128:192], KK[:], lgh[:, h:h + 1], 0.0, op0=ALU.mult, op1=ALU.min),
              wait=(["iota3", "mme:15"] if h == 0 else [f"gwe:{h - 1}"]),
              inc="DV", event=f"gwu:{h}")
            E("scalar", lambda e, h=h: e.activation(
                GW[:, h * 64:(h + 1) * 64], ydrop[:, 128:192], AF.Exp),
              wait=[f"gwu:{h}"], inc="AC", event=f"gwe:{h}")

        wi = [0]
        def prep_w_bf16(dst, src_ap):
            for mt in range(8):
                i = wi[0]
                s = i % 2
                E("sync", lambda e, mt=mt, src_ap=src_ap, s=s: e.dma_start(
                    out=xA[:, s * 1024:(s + 1) * 1024], in_=src_ap[mt * 128:(mt + 1) * 128, :]),
                  wait=([f"wcast:{i - 2}"] if i > 1 else ["mm_al"]),
                  inc=f"LX{s}", amt=16, event=f"wld:{i}")
                E("vector", lambda e, s=s: e.tensor_copy(
                    xnbf[:, s * 1024:(s + 1) * 1024], xA[:, s * 1024:(s + 1) * 1024]),
                  wait=[f"wld:{i}", W(f"wtr:{i - 2}"), W(f"wtrb:{i - 2}")], inc="DV", event=f"wcast:{i}")
                for db in range(8):
                    eng = "scalar" if db < 4 else "sync"
                    sem = "TP" if db < 4 else "TW"
                    E(eng, lambda e, db=db, mt=mt, dst=dst, s=s: e.dma_start(
                        out=dst[:, db * 1024 + mt * 128: db * 1024 + (mt + 1) * 128],
                        in_=xnbf[:, s * 1024 + db * 128: s * 1024 + (db + 1) * 128], transpose=True),
                      wait=[f"wcast:{i}"] if db in (0, 4) else [],
                      inc=sem, amt=16,
                      event=(f"wtr:{i}" if db == 3 else (f"wtrb:{i}" if db == 7 else None)), dma=True)
                wi[0] += 1
        prep_w_bf16(WwT, wwap)
        prep_w_bf16(WrT, wrap_)
        nw = wi[0]
        for db in range(8):
            E("vector", lambda e, db=db: e.tensor_scalar(
                WrT[:, db * 1024:(db + 1) * 1024], WrT[:, db * 1024:(db + 1) * 1024],
                alp[:, db:db + 1], None, op0=ALU.mult),
              wait=[f"wtr:{nw - 1}", f"wtrb:{nw - 1}", "alp"] if db == 0 else [],
              inc="DV" if db == 7 else None, event="wrfold" if db == 7 else None)

        qi = [0]
        def prep_w_fp8(src_slice, lane_dst3):
            i = qi[0]
            s = i % 2
            tdst = XQTb if s == 0 else WKT
            E("sync", lambda e, src_slice=src_slice, s=s: e.dma_start(
                out=xA[:, s * 1024:(s + 1) * 1024], in_=src_slice),
              wait=([f"qc1:{i - 2}"] if i > 1 else [f"wtr:{nw - 1}", f"wtrb:{nw - 1}"]),
              inc=f"LX{s}", amt=16, event=f"qld:{i}")
            E("vector", lambda e, s=s: e.tensor_copy(
                maskb[:, s * 1024:(s + 1) * 1024], xA[:, s * 1024:(s + 1) * 1024]),
              wait=[f"qld:{i}"], inc="DV", event=f"qc1:{i}")
            E("vector", lambda e, s=s: e.tensor_copy(
                xnbf[:, s * 1024:(s + 1) * 1024], maskb[:, s * 1024:(s + 1) * 1024]),
              wait=[f"qc1:{i}", W(f"qtr:{i - 2}"), W(f"qtrb:{i - 2}")], inc="DV", event=f"qc2:{i}")
            for db in range(8):
                eng = "scalar" if db < 4 else "sync"
                sem = "TP" if db < 4 else "TW"
                E(eng, lambda e, db=db, s=s, tdst=tdst: e.dma_start(
                    out=tdst[:, db * 128:(db + 1) * 128],
                    in_=xnbf[:, s * 1024 + db * 128: s * 1024 + (db + 1) * 128], transpose=True),
                  wait=([f"qc2:{i}", W(f"q8t:{i - 2}")]) if db in (0, 4) else [],
                  inc=sem, amt=16,
                  event=(f"qtr:{i}" if db == 3 else (f"qtrb:{i}" if db == 7 else None)), dma=True)
            E("gpsimd", lambda e, lane_dst3=lane_dst3, tdst=tdst: e.tensor_copy(
                lane_dst3, tdst[:, 0:1024].rearrange("p (l t) -> p l t", l=8)),
              wait=[f"qtr:{i}", f"qtrb:{i}"], inc="PL", event=f"q8t:{i}")
            qi[0] += 1
        for nt in range(32):
            d3 = W1T8[:].rearrange("p (l n) -> p l n", l=8)[:, :, nt * 128:(nt + 1) * 128]
            prep_w_fp8(w1ap[nt * 128:(nt + 1) * 128, :], d3)
        for mt in range(8):
            for pc in range(4):
                d3 = W2T8[:].rearrange("p (l n) -> p l n", l=32)[:, pc * 8:(pc + 1) * 8, mt * 128:(mt + 1) * 128]
                prep_w_fp8(w2ap[mt * 128:(mt + 1) * 128, pc * 1024:(pc + 1) * 1024], d3)

        E("vector", lambda e: e.tensor_scalar(ydrop[:, 0:128], IOT[:], 0.0, None, op0=ALU.is_equal),
          wait=["iota1"], inc="DV")
        E("vector", lambda e: e.tensor_copy(idb32[:], ydrop[:, 0:128]), inc="DV", event="idb")
        E("sync", lambda e: e.dma_start(
            out=xA[0:32, 0:128], in_=b1ap.rearrange("(a b) -> a b", a=32)),
          wait=[f"q8t:{qi[0] - 1}", "idb"], inc="LP", amt=16, event="b1l")
        E("vector", lambda e: e.tensor_copy(xq8[0:32, 0:128], xA[0:32, 0:128]),
          wait=["b1l"], inc="DV")
        E("vector", lambda e: e.tensor_copy(xnbf[0:32, 0:128], xq8[0:32, 0:128]), inc="DV", event="b1q")
        E("tensor", lambda e: e.matmul(
            pEF[:, 0:32], xnbf[0:32, 0:128], idb32[0:32, 0:32], start=True, stop=True),
          wait=["b1q", "idb"], inc="PE", event="b1mm")
        E("vector", lambda e: e.tensor_copy(b1qf[:], pEF[:, 0:32]), wait=["b1mm"], inc="DV", event="b1done")
        E("sync", lambda e: e.dma_start(out=tcomb[0:1, 0:1024], in_=b2ap.rearrange("(a b) -> a b", a=1)),
          wait=["b1l"], inc="LP", amt=16, event="b2l")
        E("vector", lambda e: e.tensor_copy(HQ8[0:1, 0:1024], tcomb[0:1, 0:1024]),
          wait=["b2l", "b1q"], inc="DV")
        E("vector", lambda e: e.tensor_copy(RD[0:1, 0:1024], HQ8[0:1, 0:1024]), inc="DV", event="b2q")
        for hf in range(2):
            E("tensor", lambda e, hf=hf: e.matmul(
                pAB[:, hf * 512:(hf + 1) * 512], onesbf[:], RD[0:1, hf * 512:(hf + 1) * 512],
                start=True, stop=True),
              wait=["b2q", "b1mm"] if hf == 0 else [], inc="PE" if hf == 1 else None,
              event="b2mm" if hf == 1 else None)
        E("vector", lambda e: e.tensor_copy(b2row[:], pAB[:]), wait=["b2mm"], inc="DV", event="b2done")

        if dbg2:
            pdump = [("p_iot", IOT[:], F32), ("p_caus", CAUS[:], F32), ("p_ydtmp", ydrop[:, 0:128], F32),
                     ("p_tcomb", tcomb[0:1, 0:1024], F32), ("p_hqrow", HQ8[0:1, 0:1024], FP8),
                     ("p_rdrow", RD[0:1, 0:1024], BF16), ("p_ones", onesbf[0:1, 0:128], BF16),
                     ("p_idb", idb32[:], BF16), ("p_mm", MM[:], BF16), ("p_gw", GW[:], BF16),
                     ("p_gp", GP[:], BF16), ("p_gcp", gcp[:], F32), ("p_dgm", DGM[:], BF16),
                     ("p_b2row", b2row[:], FP8)]
            first = True
            for nm, src, dt_ in pdump:
                dst = nc.dram_tensor(nm, list(src.shape), dt_, kind="ExternalOutput").ap()
                E("gpsimd", lambda e, dst=dst, src=src: e.dma_start(out=dst, in_=src),
                  wait=([(s, p.cnt.get(s, 0)) for s in ("LP", "TP", "PE", "DV", "AC", "PL")] if first else []),
                  inc="ST", amt=16, dma=True)
                first = False
        E("sync", lambda e: e.nop())
        prepw = [(s, p.cnt.get(s, 0)) for s in ("LP", "TP", "TW", "LX0", "LX1", "PE", "DV", "AC", "PL", "ST")]
        for eng in Prog.ENGS:
            E(eng, lambda e: e.nop(), wait=prepw)

        for c in range(n_chunks):
            s2, s3, s4 = c % 2, c % 3, c % 4
            g, cc = c // GRP, c % GRP
            gs = g % 2

            E("sync", lambda e, c=c, s2=s2: e.dma_start(
                out=xA[:, s2 * 1024:(s2 + 1) * 1024], in_=xap[c * 128:(c + 1) * 128, :]),
              wait=[W(f"xn:{c - 2}")], inc=f"LX{s2}", amt=16, event=f"xld:{c}")

            E("vector", lambda e, s2=s2: e.tensor_reduce(
                mu[:, s2:s2 + 1], xA[:, s2 * 1024:(s2 + 1) * 1024], axis=AXX, op=ALU.add),
              wait=[f"xld:{c}"], inc="DV", event=f"mu:{c}")
            E("scalar", lambda e, s2=s2: e.activation(
                xnf[:, 0:1024], xA[:, s2 * 1024:(s2 + 1) * 1024], AF.Square,
                accum_out=ssq[:, s2:s2 + 1]),
              wait=[f"xld:{c}", W(f"xq8:{c - 1}"), W(f"xnbf:{c - 1}")], inc="AC", event=f"ssq:{c}")
            E("vector", lambda e, s2=s2: e.tensor_scalar(
                mcol[:, s2:s2 + 1], mu[:, s2:s2 + 1], 1.0 / 1024.0, None, op0=ALU.mult),
              wait=[f"ssq:{c}", f"mu:{c}"])
            E("vector", lambda e, s2=s2: e.tensor_tensor(
                m2col[:, s2:s2 + 1], mcol[:, s2:s2 + 1], mcol[:, s2:s2 + 1], op=ALU.mult))
            E("vector", lambda e, s2=s2: e.scalar_tensor_tensor(
                vcol[:, s2:s2 + 1], ssq[:, s2:s2 + 1], 1.0 / 1024.0, m2col[:, s2:s2 + 1],
                op0=ALU.mult, op1=ALU.subtract), inc="DV", event=f"var:{c}")
            E("scalar", lambda e, s2=s2: e.activation(
                lvcol[:, s2:s2 + 1], vcol[:, s2:s2 + 1], AF.Ln, bias=CONSTS[:, 0:1]),
              wait=[f"var:{c}"])
            E("scalar", lambda e, s2=s2, s8=c % 8: e.activation(
                rstd[:, s8:s8 + 1], lvcol[:, s2:s2 + 1], AF.Exp, bias=0.0, scale=CONSTS[:, 1:2]),
              inc="AC", event=f"rstd:{c}")
            E("vector", lambda e, s2=s2, s8=c % 8: e.scalar_tensor_tensor(
                nmb[:, s8:s8 + 1], mcol[:, s2:s2 + 1], -1.0, rstd[:, s8:s8 + 1],
                op0=ALU.mult, op1=ALU.mult),
              wait=[f"rstd:{c}"], inc="DV", event=f"nmb:{c}")
            E("vector", lambda e, s2=s2, s8=c % 8: e.tensor_scalar(
                xnf[:, 0:1024], xA[:, s2 * 1024:(s2 + 1) * 1024],
                rstd[:, s8:s8 + 1], nmb[:, s8:s8 + 1], op0=ALU.mult, op1=ALU.add),
              wait=[f"nmb:{c}", f"ssq:{c}"], inc="DV", event=f"xn:{c}")

            E("vector", lambda e, s4=s4: e.tensor_copy(
                xnbf[:, s4 * 1024:(s4 + 1) * 1024], xnf[:, 0:1024]),
              wait=[f"xn:{c}", W(f"out:{c - 4}"), W(f"xnT:{c - 4}")], inc="DV", event=f"xnbf:{c}")
            E("gpsimd", lambda e: e.tensor_copy(xq8[:, 0:1024], xnf[:, 0:1024]),
              wait=[f"xn:{c}"], inc="PL", event=f"xq8:{c}")
            E("gpsimd", lambda e: e.tensor_copy(xqbf[:, 0:1024], xq8[:, 0:1024]),
              wait=[W(f"xqT:{c - 1}")], inc="PL", event=f"xqbf:{c}")

            for db in range(8):
                E("scalar", lambda e, db=db, s3=s3, s4=s4: e.dma_start(
                    out=XNF[:, db * 384 + s3 * 128: db * 384 + (s3 + 1) * 128],
                    in_=xnbf[:, s4 * 1024 + db * 128: s4 * 1024 + (db + 1) * 128], transpose=True),
                  wait=([f"xnbf:{c}", W(f"sc:{c - 3}:3"), W(f"rkgp:{c - 3}")] if db == 0 else []),
                  inc=f"TX{s3}", amt=16, event=(f"xnT:{c}" if db == 7 else None), dma=True)
            for db in range(8):
                E("scalar", lambda e, db=db: e.dma_start(
                    out=XQTb[:, db * 128:(db + 1) * 128],
                    in_=xqbf[:, db * 128:(db + 1) * 128], transpose=True),
                  wait=([f"xqbf:{c}", W(f"xqf8:{c - 1}")] if db == 0 else []),
                  inc="TQ", amt=16, event=(f"xqT:{c}" if db == 7 else None), dma=True)
            E("gpsimd", lambda e, s4=s4, gs=gs: e.tensor_copy(
                XQF8[:].rearrange("p (l t) -> p l t", l=8)[:, :, gs * 512 + s4 * 128: gs * 512 + (s4 + 1) * 128],
                XQTb[:, 0:1024].rearrange("p (l t) -> p l t", l=8)),
              wait=[f"xqT:{c}", W(f"ffn1:{g - 2}")], inc="PL", event=f"xqf8:{c}")

            E("gpsimd", lambda e, s3=s3: e.tensor_copy(
                WKF[:].rearrange("p (l t) -> p l t", l=8)[:, :, 1:128],
                XNF3[:, :, s3 * 128: s3 * 128 + 127]),
              wait=[f"xnT:{c}", W(f"sc:{c - 1}:3"), W(f"wkT:{c - 1}"), "prev0"], inc="PL")
            E("gpsimd", lambda e: e.tensor_copy(
                WKF[:].rearrange("p (l t) -> p l t", l=8)[:, :, 0:1], PREV3), inc="PL")
            E("gpsimd", lambda e, s3=s3: e.tensor_copy(
                PREV3, XNF3[:, :, s3 * 128 + 127: s3 * 128 + 128]),
              inc="PL", event=f"wkf:{c}")
            for db in range(8):
                E("scalar", lambda e, db=db: e.dma_start(
                    out=WKT[:, db * 128:(db + 1) * 128],
                    in_=WKF[:, db * 128:(db + 1) * 128], transpose=True),
                  wait=([f"wkf:{c}", W(f"upd:{c - 1}:1")] if db == 0 else []),
                  inc="TW", amt=16, event=(f"wkT:{c}" if db == 7 else None), dma=True)

            E("vector", lambda e, s3=s3: e.tensor_tensor(
                rkgp[:].rearrange("p (l t) -> p l t", l=8),
                XNF3[:, :, s3 * 128:(s3 + 1) * 128],
                GP[:].rearrange("p (l t) -> p l t", l=8), op=ALU.mult),
              wait=[f"xnT:{c}", W(f"reads:{c - 1}")], inc="DV", event=f"rkgp:{c}")

            vw = [f"xnT:{c}", W(f"vevac:{c - 1}")]
            if cc == 0 and g > 0:
                vw.append(f"y2:{g - 1}:3")
            for db in range(8):
                for hf in range(2):
                    E("tensor", lambda e, db=db, hf=hf, s3=s3: e.matmul(
                        pAB[:, hf * 512:(hf + 1) * 512],
                        XNF[:, db * 384 + s3 * 128: db * 384 + (s3 + 1) * 128],
                        WwT[:, db * 1024 + hf * 512: db * 1024 + (hf + 1) * 512],
                        start=(db == 0), stop=(db == 7)),
                      wait=(vw if (db == 0 and hf == 0) else []),
                      inc=("PE" if (db == 7 and hf == 1) else None),
                      event=(f"vproj:{c}" if (db == 7 and hf == 1) else None))
            first = True
            for h in range(NHEAD):
                off = h * 128 + (h % 2) * 64
                E("vector", lambda e, h=h, off=off: e.tensor_copy(
                    VZ[:, off:off + 64], pAB[:, h * 64:(h + 1) * 64]),
                  wait=([f"vproj:{c}", W(f"upd:{c - 1}:1")] if first else []), inc="DV")
                first = False
            E("vector", lambda e: e.tensor_tensor(gv[:, 0:1024], pAB[:], GW[:], op=ALU.mult),
              inc="DV", event=f"vevac:{c}")
            if dbg2 and c >= 11 and c <= 15:
                dvz = nc.dram_tensor(f"dbg_vz{c}", [128, 2048], BF16, kind="ExternalOutput").ap()
                dgv = nc.dram_tensor(f"dbg_gv{c}", [128, 1024], BF16, kind="ExternalOutput").ap()
                dxn = nc.dram_tensor(f"dbg_xn{c}", [128, 3072], BF16, kind="ExternalOutput").ap()
                E("gpsimd", lambda e, dvz=dvz: e.dma_start(out=dvz, in_=VZ[:]),
                  wait=[f"vevac:{c}"], inc="GA", amt=16, dma=True)
                E("gpsimd", lambda e, dgv=dgv: e.dma_start(out=dgv, in_=gv[:]),
                  inc="GA", amt=16, dma=True)
                E("gpsimd", lambda e, dxn=dxn: e.dma_start(out=dxn, in_=XNF[:]),
                  inc="GA", amt=16, event=f"dbgd:{c}", dma=True)
                p.q["tensor"].append((lambda e, _w=p.ev[f"dbgd:{c}"]: e.wait_ge(sems[_w[0]], _w[1]), [], None, 1))

            SCB = [(pEF, 0), (pEF, 512), (pGH, 0), (pGH, 512)]

            if no_scan:
                continue

            def sc_wave(s):
                sw = [f"wkf:{c}", f"xnT:{c}"]
                if s == 0:
                    sw.append(W(f"sm:{c - 1}:3"))
                    sw.append(W(f"state:{c - 1}"))
                    if cc == 0 and g > 0:
                        sw.append(W(f"wev:{g - 1}:31"))
                else:
                    sw.append(f"sm:{c}:{s - 1}")
                for j in range(4):
                    h = s * 4 + j
                    db, rh = h // 2, (h % 2) * 64
                    pt, off = SCB[j]
                    E("tensor", lambda e, db=db, rh=rh, pt=pt, off=off, s3=s3: e.matmul(
                        pt[:, off:off + 128],
                        WKF[rh:rh + 64, db * 128:(db + 1) * 128],
                        XNF[rh:rh + 64, db * 384 + s3 * 128: db * 384 + (s3 + 1) * 128],
                        start=True, stop=True),
                      wait=(sw if j == 0 else []),
                      inc=("PE" if j == 3 else None), event=(f"sc:{c}:{s}" if j == 3 else None))

            def sm_evac(s):
                smw = [f"sc:{c}:{s}"]
                if s >= 2:
                    smw.append(W(f"iw:{c}:{s - 2}"))
                elif c > 0:
                    smw.append(W(f"iw:{c - 1}:{s + 2}"))
                if s == 0:
                    smw.append(W(f"reads:{c - 1}"))
                sl = (s % 2) * 512
                E("vector", lambda e, s=s, sl=sl: e.tensor_tensor(
                    SM[:, sl:sl + 256].rearrange("p (a b) -> p a b", a=2),
                    pEF[:].rearrange("p (a b) -> p a b", a=2)[:, :, 0:128],
                    MM[:, s * 512:s * 512 + 256].rearrange("p (a b) -> p a b", a=2),
                    op=ALU.mult),
                  wait=smw, inc="DV")
                E("vector", lambda e, s=s, sl=sl: e.tensor_tensor(
                    SM[:, sl + 256:sl + 512].rearrange("p (a b) -> p a b", a=2),
                    pGH[:].rearrange("p (a b) -> p a b", a=2)[:, :, 0:128],
                    MM[:, s * 512 + 256:(s + 1) * 512].rearrange("p (a b) -> p a b", a=2),
                    op=ALU.mult),
                  inc="DV", event=f"sm:{c}:{s}")

            def inter_pair(s):
                rw = [f"rkgp:{c}", W(f"state:{c - 1}"), f"vevac:{c}",
                      (f"rd:{c}:{s - 1}" if s > 0 else W(f"rd:{c - 1}:3"))]
                if s == 0 and c >= GRP and cc == 0:
                    rw.append(W(f"out:{c - 1}"))
                for q in range(2):
                    gpr = 2 * s + q
                    E("tensor", lambda e, gpr=gpr, q=q: e.matmul(
                        pCD[:, q * 512:q * 512 + 128],
                        SBF[:, gpr * 128:(gpr + 1) * 128],
                        rkgp[:, gpr * 128:(gpr + 1) * 128],
                        start=True, stop=False),
                      wait=(rw if q == 0 else []))

            def intra_wave(s, extra=()):
                for j in range(4):
                    h = s * 4 + j
                    q = j // 2
                    last = (j % 2) == 1
                    sl = (s % 2) * 512
                    E("tensor", lambda e, h=h, q=q, sl=sl, j=j, last=last: e.matmul(
                        pCD[:, q * 512:q * 512 + 128],
                        VZ[:, h * 128:(h + 1) * 128],
                        SM[:, sl + j * 128: sl + (j + 1) * 128],
                        start=False, stop=last),
                      wait=(list(extra) if j == 0 else []),
                      inc=("PE" if j == 3 else None), event=(f"iw:{c}:{s}" if j == 3 else None))

            def rd_evac(s):
                for q in range(2):
                    gpr = 2 * s + q
                    E("vector", lambda e, gpr=gpr, q=q, s4=s4: e.tensor_copy(
                        RD[:, s4 * 1024 + gpr * 128:(s4 * 1024) + (gpr + 1) * 128],
                        pCD[:, q * 512:q * 512 + 128]),
                      wait=([f"iw:{c}:{s}"] + ([W(f"rproj:{c - 4}")] if s == 0 else [])) if q == 0 else [],
                      inc="DV", event=(f"rd:{c}:{s}" if q == 1 else None))

            sc_wave(0)
            sm_evac(0)
            inter_pair(0)
            sc_wave(1)
            sm_evac(1)
            intra_wave(0, extra=[f"sm:{c}:0"])
            rd_evac(0)
            inter_pair(1)
            sc_wave(2)
            sm_evac(2)
            intra_wave(1, extra=[f"sm:{c}:1"])
            rd_evac(1)
            inter_pair(2)
            sc_wave(3)
            sm_evac(3)
            intra_wave(2, extra=[f"sm:{c}:2"])
            rd_evac(2)
            inter_pair(3)
            intra_wave(3, extra=[f"sm:{c}:3"])
            rd_evac(3)
            p.ev[f"reads:{c}"] = p.ev[f"iw:{c}:3"]
            p.ev[f"rd:{c}"] = p.ev[f"rd:{c}:3"]

            for u in range(2):
                uw = [f"wkT:{c}", f"vevac:{c}", f"sm:{c}:3"]
                if u == 0:
                    uw.append(W(f"state:{c - 1}"))
                else:
                    uw.append(f"stdec:{c}:0")
                for q in range(4):
                    gpr = 4 * u + q
                    pt, off = SCB[q]
                    E("tensor", lambda e, gpr=gpr, pt=pt, off=off: e.matmul(
                        pt[:, off:off + 128],
                        WKT[:, gpr * 128:(gpr + 1) * 128],
                        gv[:, gpr * 128:(gpr + 1) * 128],
                        start=True, stop=True),
                      wait=(uw if q == 0 else []),
                      inc=("PE" if q == 3 else None),
                      event=(f"upd:{c}:{u}" if q == 3 else None))
                for q in range(4):
                    gpr = 4 * u + q
                    E("vector", lambda e, gpr=gpr: e.tensor_scalar(
                        SF32a[:, gpr * 128:(gpr + 1) * 128], SF32a[:, gpr * 128:(gpr + 1) * 128],
                        gcp[:, gpr:gpr + 1], None, op0=ALU.mult),
                      wait=[f"upd:{c}:{u}", "gcp"] if q == 0 else [])
                b0 = 4 * u * 128
                E("vector", lambda e, b0=b0: e.tensor_tensor(
                    SF32a[:, b0:b0 + 256].rearrange("p (a b) -> p a b", a=2),
                    SF32a[:, b0:b0 + 256].rearrange("p (a b) -> p a b", a=2),
                    pEF[:].rearrange("p (a b) -> p a b", a=2)[:, :, 0:128],
                    op=ALU.add), inc="DV")
                E("vector", lambda e, b0=b0: e.tensor_tensor(
                    SF32a[:, b0 + 256:b0 + 512].rearrange("p (a b) -> p a b", a=2),
                    SF32a[:, b0 + 256:b0 + 512].rearrange("p (a b) -> p a b", a=2),
                    pGH[:].rearrange("p (a b) -> p a b", a=2)[:, :, 0:128],
                    op=ALU.add), inc="DV", event=f"stdec:{c}:{u}")
            E("vector", lambda e: e.tensor_tensor(SBF[:], SF32a[:], DGM[:], op=ALU.mult),
              inc="DV", event=f"state:{c}")

            if cc == GRP - 1 and not no_b:
                for m in range(32):
                    f1w = [f"xqf8:{c}", W(f"state:{c}")] if m < 2 else [f"xqf8:{c}"]
                    if m >= 2:
                        f1w.append(f"wev:{g}:{m - 2}")
                    elif g > 0:
                        f1w.append(f"wev:{g - 1}:{30 + m}")
                    for db in range(8):
                        E("tensor", lambda e, m=m, db=db, gs=gs: e.matmul(
                            pGH[:, (m % 2) * 512:(m % 2 + 1) * 512],
                            W1T8[:, db * 4096 + m * 128: db * 4096 + (m + 1) * 128],
                            XQF8[:, db * 1024 + gs * 512: db * 1024 + (gs + 1) * 512],
                            start=(db == 0), stop=(db == 7)),
                          wait=(f1w if db == 0 else []),
                          inc=("PE" if db == 7 else None),
                          event=(f"ffn1m:{g}:{m}" if db == 7 else None))
                    E("scalar", lambda e, m=m: e.activation(
                        wbf[:, (m % 2) * 512:(m % 2 + 1) * 512], pGH[:, (m % 2) * 512:(m % 2 + 1) * 512],
                        AF.Relu, bias=b1qf[:, m:m + 1]),
                      wait=[f"ffn1m:{g}:{m}", W(f"hq:{g}:{m - 2}")], inc="AC", event=f"wev:{g}:{m}")
                    hqw = [f"wev:{g}:{m}"]
                    if m == 0 and g >= 1:
                        hqw.append(f"ffn2:{g - 1}:3")
                    E("gpsimd", lambda e, m=m: e.tensor_copy(
                        HQ8[:, m * 512:(m + 1) * 512], wbf[:, (m % 2) * 512:(m % 2 + 1) * 512]),
                      wait=hqw, inc="PL", event=f"hq:{g}:{m}")
                p.ev[f"ffn1:{g}"] = p.ev[f"ffn1m:{g}:31"]

                for cc2 in range(GRP):
                    c2 = g * GRP + cc2
                    sB2, sB4 = c2 % 2, c2 % 4
                    E("sync", lambda e, c2=c2, sB2=sB2: e.dma_start(
                        out=maskb[:, sB2 * 1024:(sB2 + 1) * 1024], in_=mkap[c2 * 128:(c2 + 1) * 128, :]),
                      wait=[W(f"ydrop:{c2 - 2}")], inc=f"LM{sB2}", amt=16, event=f"mld:{c2}")
                    f2w = [f"hq:{g}:31"]
                    f2w.append(f"y2:{g}:{cc2 - 1}" if cc2 > 0 else f"vevac:{g * GRP + 3}")
                    for nb in range(32):
                        for hf in range(2):
                            E("tensor", lambda e, nb=nb, hf=hf, cc2=cc2: e.matmul(
                                pAB[:, hf * 512:(hf + 1) * 512],
                                HQ8[:, nb * 512 + cc2 * 128: nb * 512 + (cc2 + 1) * 128],
                                W2T8[:, nb * 1024 + hf * 512: nb * 1024 + (hf + 1) * 512],
                                start=(nb == 0), stop=(nb == 31)),
                              wait=(f2w if (nb == 0 and hf == 0) else []),
                              inc=("PE" if (nb == 31 and hf == 1) else None),
                              event=(f"ffn2:{g}:{cc2}" if (nb == 31 and hf == 1) else None))
                    E("vector", lambda e: e.tensor_tensor(y2b[:], pAB[:], b2row[:], op=ALU.add),
                      wait=[f"ffn2:{g}:{cc2}", "b2done"], inc="DV", event=f"y2:{g}:{cc2}")
                    E("vector", lambda e, sB2=sB2: e.tensor_tensor(
                        ydrop[:], y2b[:], maskb[:, sB2 * 1024:(sB2 + 1) * 1024], op=ALU.mult),
                      wait=[f"mld:{c2}"], inc="DV", event=f"ydrop:{c2}")
                    rpw = [f"rd:{c2}", "wrfold", f"y2:{g}:{cc2}"]
                    rpw.append(f"out:{c2 - 1}" if cc2 > 0 else f"rd:{g * GRP + 3}")
                    for jb in range(8):
                        for hf in range(2):
                            E("tensor", lambda e, jb=jb, hf=hf, sB4=sB4: e.matmul(
                                pCD[:, hf * 512:(hf + 1) * 512],
                                RD[:, sB4 * 1024 + jb * 128: sB4 * 1024 + (jb + 1) * 128],
                                WrT[:, jb * 1024 + hf * 512: jb * 1024 + (hf + 1) * 512],
                                start=(jb == 0), stop=(jb == 7)),
                              wait=(rpw if (jb == 0 and hf == 0) else []),
                              inc=("PE" if (jb == 7 and hf == 1) else None),
                              event=(f"rproj:{c2}" if (jb == 7 and hf == 1) else None))
                    E("vector", lambda e, sB4=sB4: e.scalar_tensor_tensor(
                        tcomb[:], pCD[:], 0.0, xnbf[:, sB4 * 1024:(sB4 + 1) * 1024],
                        op0=ALU.add, op1=ALU.add),
                      wait=[f"rproj:{c2}", f"ydrop:{c2}", f"xnbf:{c2}", W(f"st:{c2 - 1}")])
                    E("vector", lambda e: e.tensor_tensor(tcomb[:], tcomb[:], ydrop[:], op=ALU.add),
                      inc="DV", event=f"out:{c2}")
                    if not no_accum:
                        E("gpsimd", lambda e, c2=c2: e.dma_start(
                            out=tcomb[:], in_=xap[c2 * 128:(c2 + 1) * 128, :], accum_op=ALU.add),
                          wait=[f"out:{c2}"], inc="GA", amt=16, event=f"acc:{c2}", dma=True)
                    E("gpsimd", lambda e, c2=c2: e.dma_start(
                        out=yap[c2 * 128:(c2 + 1) * 128, :], in_=tcomb[:]),
                      wait=[f"acc:{c2}" if not no_accum else f"out:{c2}"],
                      inc="ST", amt=16, event=f"st:{c2}", dma=True)

        if dbg2:
            dbg_b1 = nc.dram_tensor("dbg_b1", [128, 32], F32, kind="ExternalOutput").ap()
            dbg_b2 = nc.dram_tensor("dbg_b2", [128, 1024], FP8, kind="ExternalOutput").ap()
            allw2 = [(s, p.cnt.get(s, 0)) for s in p.cnt]
            E("gpsimd", lambda e: e.dma_start(out=dbg_b1, in_=b1qf[:]), wait=allw2, inc="ST", amt=16, dma=True)
            E("gpsimd", lambda e: e.dma_start(out=dbg_b2, in_=b2row[:]), inc="ST", amt=16, dma=True)
        if dbg:
            dbg_rd = nc.dram_tensor("dbg_rd", [128, 4096], BF16, kind="ExternalOutput").ap()
            dbg_hq = nc.dram_tensor("dbg_hq", [128, 32 * 512], FP8, kind="ExternalOutput").ap()
            dbg_sbf = nc.dram_tensor("dbg_sbf", [128, 1024], BF16, kind="ExternalOutput").ap()
            dbg_vz = nc.dram_tensor("dbg_vz", [128, 2048], BF16, kind="ExternalOutput").ap()
            dbg_xq = nc.dram_tensor("dbg_xq", [128, 8192], FP8, kind="ExternalOutput").ap()
            allw = [(s, p.cnt.get(s, 0)) for s in p.cnt]
            E("gpsimd", lambda e: e.dma_start(out=dbg_rd, in_=RD[:]), wait=allw, inc="ST", amt=16, dma=True)
            E("gpsimd", lambda e: e.dma_start(out=dbg_hq, in_=HQ8[:]), inc="ST", amt=16, dma=True)
            E("gpsimd", lambda e: e.dma_start(out=dbg_sbf, in_=SBF[:]), inc="ST", amt=16, dma=True)
            E("gpsimd", lambda e: e.dma_start(out=dbg_vz, in_=VZ[:]), inc="ST", amt=16, dma=True)
            E("gpsimd", lambda e: e.dma_start(out=dbg_xq, in_=XQF8[:]), inc="ST", amt=16, dma=True)
        if "ST" in p.cnt:
            E("gpsimd", lambda e: e.wait_ge(sems["ST"], p.cnt["ST"]), dma=True)

        with nc.Block() as block:
            @block.sync
            def _(eng):
                p.replay(eng, "sync", sems)

            @block.tensor
            def _(eng):
                p.replay(eng, "tensor", sems)

            @block.vector
            def _(eng):
                p.replay(eng, "vector", sems)

            @block.scalar
            def _(eng):
                p.replay(eng, "scalar", sems)

            @block.gpsimd
            def _(eng):
                p.replay(eng, "gpsimd", sems)
    return nc


_CACHE = {}


def _bernoulli_keep(shape, k0=0, k1=42, p=0.8):
    rot1 = (13, 15, 26, 6)
    rot2 = (17, 29, 16, 24)
    ks = (np.uint32(k0), np.uint32(k1),
          np.uint32(k0) ^ np.uint32(k1) ^ np.uint32(0x1BD11BDA))
    n = int(np.prod(shape))
    x1 = np.arange(n, dtype=np.uint32)
    x0 = np.zeros(n, dtype=np.uint32)
    x0 = (x0 + ks[0]).astype(np.uint32)
    x1 = (x1 + ks[1]).astype(np.uint32)
    for d in range(5):
        for r in (rot1 if d % 2 == 0 else rot2):
            x0 = (x0 + x1).astype(np.uint32)
            x1 = ((x1 << np.uint32(r)) | (x1 >> np.uint32(32 - r))).astype(np.uint32)
            x1 = (x1 ^ x0).astype(np.uint32)
        x0 = (x0 + ks[(d + 1) % 3]).astype(np.uint32)
        x1 = (x1 + ks[(d + 2) % 3] + np.uint32(d + 1)).astype(np.uint32)
    bits = x0 ^ x1
    u = ((bits >> np.uint32(9)) | np.uint32(0x3F800000)).view(np.float32) - np.float32(1.0)
    return (u < np.float32(p)).reshape(shape)


def kernel(L0_attn_out, L0_attn_W_write, L0_attn_W_read, L0_attn_decay, L0_attn_log_alpha,
           L0_ffn_W1, L0_ffn_b1, L0_ffn_W2, L0_ffn_b2):
    x = np.ascontiguousarray(np.asarray(L0_attn_out, dtype=np.float32))
    B = x.shape[0]
    keep = _bernoulli_keep(x.shape)
    mask = np.where(keep, np.float32(1.25), np.float32(0.0)).astype(ml_dtypes.float8_e5m2)

    if "nc" not in _CACHE:
        _CACHE["nc"] = build_nc()
    nc = _CACHE["nc"]

    ww = np.ascontiguousarray(np.asarray(L0_attn_W_write, dtype=np.float32))
    wr = np.ascontiguousarray(np.asarray(L0_attn_W_read, dtype=np.float32))
    w1 = np.ascontiguousarray(np.asarray(L0_ffn_W1, dtype=np.float32))
    w2 = np.ascontiguousarray(np.asarray(L0_ffn_W2, dtype=np.float32))
    b1 = np.ascontiguousarray(np.asarray(L0_ffn_b1, dtype=np.float32))
    b2 = np.ascontiguousarray(np.asarray(L0_ffn_b2, dtype=np.float32))
    dc = np.ascontiguousarray(np.asarray(L0_attn_decay, dtype=np.float32))
    la = np.ascontiguousarray(np.asarray(L0_attn_log_alpha, dtype=np.float32))

    in_maps = []
    for b in range(B):
        in_maps.append({
            "x": np.ascontiguousarray(x[b]),
            "mask": np.ascontiguousarray(mask[b]),
            "Ww": ww, "Wr": wr, "W1": w1, "W2": w2,
            "b1": b1, "b2": b2, "decay": dc, "logalpha": la,
        })
    res = run_bass_kernel_spmd(nc, in_maps, core_ids=list(range(B)))
    out = np.stack([np.asarray(res.results[b]["y"]) for b in range(B)], axis=0)
    return out.astype(np.float32)


# version 08
# speedup vs baseline: 1.0982x; 1.0019x over previous
import numpy as np
import ml_dtypes
import concourse.bass as bass
import concourse.mybir as mybir
from concourse.bass_utils import run_bass_kernel_spmd
from contextlib import ExitStack

F32 = mybir.dt.float32
BF16 = mybir.dt.bfloat16
FP8 = mybir.dt.float8e5
AF = mybir.ActivationFunctionType
ALU = mybir.AluOpType
AXX = mybir.AxisListType.X

T, D = 8192, 1024
CH = 128
NCH = T // CH
GRP = 4
NHEAD, HD, NPAIR = 16, 64, 8


class Prog:
    ENGS = ("sync", "tensor", "vector", "scalar", "gpsimd")

    def __init__(self):
        self.q = {e: [] for e in self.ENGS}
        self.cnt = {}
        self.ev = {}

    SELF_SEM = {"vector": "DV", "scalar": "AC", "gpsimd": "PL"}

    def emit(self, eng, fn, wait=(), inc=None, amt=1, event=None, dma=False):
        waits = []
        for w in wait:
            if w is None:
                continue
            waits.append(self.ev[w] if isinstance(w, str) else w)
        if not dma and eng in self.SELF_SEM:
            ss = self.SELF_SEM[eng]
            assert inc in (None, ss), (eng, inc)
            waits.append((ss, self.cnt.get(ss, 0)))
            inc, amt = ss, 1
        self.q[eng].append((fn, waits, inc, amt))
        if inc is not None:
            self.cnt[inc] = self.cnt.get(inc, 0) + amt
            if event is not None:
                self.ev[event] = (inc, self.cnt[inc])
        else:
            assert event is None, event

    def replay(self, eng_obj, name, sems):
        waited = {}
        for fn, waits, inc, amt in self.q[name]:
            for s, v in waits:
                if waited.get(s, -1) >= v:
                    continue
                eng_obj.wait_ge(sems[s], v)
                waited[s] = v
            ins = fn(eng_obj)
            if inc is not None:
                ins.then_inc(sems[inc], amt)


def build_nc(n_chunks=NCH, no_b=False, no_accum=False, no_pe=False, no_attn=False, no_scan=False, dbg=False, dbg2=False):
    nc = bass.Bass("TRN2", target_bir_lowering=False, dynamic_dma_scratch_size=16384)
    xap = nc.dram_tensor("x", [T, D], F32, kind="ExternalInput").ap()
    mkap = nc.dram_tensor("mask", [T, D], FP8, kind="ExternalInput").ap()
    wwap = nc.dram_tensor("Ww", [D, D], F32, kind="ExternalInput").ap()
    wrap_ = nc.dram_tensor("Wr", [D, D], F32, kind="ExternalInput").ap()
    w1ap = nc.dram_tensor("W1", [4 * D, D], F32, kind="ExternalInput").ap()
    w2ap = nc.dram_tensor("W2", [D, 4 * D], F32, kind="ExternalInput").ap()
    b1ap = nc.dram_tensor("b1", [4 * D], F32, kind="ExternalInput").ap()
    b2ap = nc.dram_tensor("b2", [D], F32, kind="ExternalInput").ap()
    dcap = nc.dram_tensor("decay", [NHEAD], F32, kind="ExternalInput").ap()
    laap = nc.dram_tensor("logalpha", [NHEAD], F32, kind="ExternalInput").ap()
    yap = nc.dram_tensor("y", [T, D], F32, kind="ExternalOutput").ap()

    p = Prog()
    E = p.emit

    def W(name):
        return name if name in p.ev else None
    with ExitStack() as ctx:
        def sb(name, shape, dt):
            return ctx.enter_context(nc.sbuf_tensor(name, shape, dt))

        WwT = sb("WwT", [128, 8 * 1024], BF16)
        WrT = sb("WrT", [128, 8 * 1024], BF16)
        W1T8 = sb("W1T8", [128, 8 * 4096], FP8)
        W2T8 = sb("W2T8", [128, 32 * 1024], FP8)
        b1qf = sb("b1qf", [128, 32], F32)
        b2row = sb("b2row", [128, 1024], FP8)
        MM = sb("MM", [128, 16 * 128], BF16)
        GP = sb("GP", [128, 8 * 128], BF16)
        GW = sb("GW", [128, 16 * 64], BF16)
        CONSTS = sb("CONSTS", [128, 4], F32)
        gcp = sb("gcp", [128, 8], F32)
        xA = sb("xA", [128, 2 * 1024], F32)
        xnbf = sb("xnbf", [128, 4 * 1024], BF16)
        prep_ctx = ExitStack()

        def pb(name, shape, dt):
            return prep_ctx.enter_context(nc.sbuf_tensor(name, shape, dt))

        IOT = pb("IOT", [128, 128], F32)
        JJ = pb("JJ", [128, 128], F32)
        KK = pb("KK", [128, 64], F32)
        CAUS = pb("CAUS", [128, 128], F32)
        E1 = pb("E1", [128, 128], F32)
        idb32 = pb("idb32", [128, 128], BF16)
        onesf = pb("onesf", [1, 128], F32)
        onesbf = pb("onesbf", [1, 128], BF16)
        ones11 = pb("ones11", [1, 1], BF16)
        rowA = pb("rowA", [1, 64], F32)
        lgh = pb("lgh", [128, 16], F32)
        alh = pb("alh", [128, 16], F32)
        gch = pb("gch", [128, 16], F32)
        lgp = pb("lgp", [128, 8], F32)
        alp = pb("alp", [128, 8], F32)
        prep_ctx.close()
        xnf = sb("xnf", [128, 1024], F32)
        xq8 = sb("xq8", [128, 1024], FP8)
        xqbf = sb("xqbf", [128, 1024], BF16)
        XQTb = sb("XQTb", [128, 1024], BF16)
        XNF = sb("XNF", [128, 8 * 384], BF16)
        XQF8 = sb("XQF8", [128, 8 * 1024], FP8)
        WKF = sb("WKF", [128, 1024], BF16)
        WKT = sb("WKT", [128, 1024], BF16)
        PREV = sb("PREV", [128, 8], BF16)
        rkgp = sb("rkgp", [128, 1024], BF16)
        VZ = sb("VZ", [128, 16 * 128], BF16)
        gv = sb("gv", [128, 1024], BF16)
        SM = sb("SM", [128, 2 * 512], BF16)
        RD = sb("RD", [128, 4 * 1024], BF16)
        SF32a = sb("SF32a", [128, 1024], F32)
        SBF = sb("SBF", [128, 1024], BF16)
        DGM = sb("DGM", [128, 1024], BF16)
        HQ8 = sb("HQ8", [128, 32 * 512], FP8)
        wbf = sb("wbf", [128, 2 * 512], BF16)
        maskb = sb("maskb", [128, 2 * 1024], FP8)
        y2b = sb("y2b", [128, 1024], BF16)
        ydrop = sb("ydrop", [128, 1024], F32)
        tcomb = sb("tcomb", [128, 1024], F32)
        mu = sb("mu", [128, 2], F32)
        ssq = sb("ssq", [128, 2], F32)
        mcol = sb("mcol", [128, 2], F32)
        m2col = sb("m2col", [128, 2], F32)
        vcol = sb("vcol", [128, 2], F32)
        lvcol = sb("lvcol", [128, 2], F32)
        rstd = sb("rstd", [128, 8], F32)
        nmb = sb("nmb", [128, 8], F32)

        pAB = ctx.enter_context(nc.psum_tensor("pAB", [128, 1024], F32))
        pCD = ctx.enter_context(nc.psum_tensor("pCD", [128, 1024], F32))
        pEF = ctx.enter_context(nc.psum_tensor("pEF", [128, 1024], F32))
        pGH = ctx.enter_context(nc.psum_tensor("pGH", [128, 1024], F32))

        sems = {}
        for s in ("LX0", "LX1", "LM0", "LM1", "LP", "TX0", "TX1", "TX2",
                  "TQ", "TW", "TP", "GA", "ST", "PE", "DV", "AC", "PL"):
            sems[s] = ctx.enter_context(nc.semaphore(s))

        XNF3 = XNF[:].rearrange("p (l t) -> p l t", l=8)
        PREV3 = PREV[:].rearrange("p (l t) -> p l t", l=8)

        E("sync", lambda e: e.dma_start(out=rowA[0:1, 0:16], in_=dcap.rearrange("(a b) -> a b", a=1)),
          inc="LP", amt=16)
        E("sync", lambda e: e.dma_start(out=rowA[0:1, 32:48], in_=laap.rearrange("(a b) -> a b", a=1)),
          inc="LP", amt=16, event="ld_small")
        p.ev["ld_decay"] = p.ev["ld_small"]
        p.ev["ld_la"] = p.ev["ld_small"]

        E("gpsimd", lambda e: e.iota(IOT[:], pattern=[[1, 128]], base=0, channel_multiplier=-1,
                                     allow_small_or_imprecise_dtypes=True), inc="PL", event="iota1")
        E("gpsimd", lambda e: e.iota(JJ[:], pattern=[[1, 128]], base=0, channel_multiplier=0,
                                     allow_small_or_imprecise_dtypes=True), inc="PL", event="iota2")
        E("gpsimd", lambda e: e.iota(KK[:], pattern=[[0, 64]], base=127, channel_multiplier=-1,
                                     allow_small_or_imprecise_dtypes=True), inc="PL", event="iota3")
        E("gpsimd", lambda e: e.memset(PREV[:], 0.0), inc="PL", event="prev0")
        E("vector", lambda e: e.memset(SF32a[:], 0.0))
        E("vector", lambda e: e.memset(SBF[:], 0.0))
        E("vector", lambda e: e.memset(VZ[:], 0.0))
        E("vector", lambda e: e.memset(DGM[:], 0.0))
        E("vector", lambda e: e.memset(CONSTS[:, 0:1], 1e-5))
        E("vector", lambda e: e.memset(CONSTS[:, 1:2], -0.5))
        E("vector", lambda e: e.memset(CONSTS[:, 2:3], 128.0))
        E("vector", lambda e: e.memset(onesf[:], 1.0))
        E("vector", lambda e: e.memset(onesbf[:], 1.0))
        E("vector", lambda e: e.memset(ones11[:], 1.0), inc="DV", event="memsets")
        E("vector", lambda e: e.tensor_scalar(CAUS[:], IOT[:], 0.5, None, op0=ALU.is_ge),
          wait=["iota1"], inc="DV", event="caus")
        E("vector", lambda e: e.tensor_scalar(E1[:], IOT[:], -1.0, None, op0=ALU.add),
          wait=["iota1"], inc="DV", event="e1")

        E("scalar", lambda e: e.activation(rowA[0:1, 16:32], rowA[0:1, 0:16], AF.Sigmoid),
          wait=["ld_decay"], inc="AC", event="grow")
        E("scalar", lambda e: e.activation(rowA[0:1, 0:16], rowA[0:1, 16:32], AF.Ln),
          wait=["grow"], inc="AC", event="lgrow")
        E("scalar", lambda e: e.activation(rowA[0:1, 48:64], rowA[0:1, 32:48], AF.Exp),
          wait=["ld_la"], inc="AC", event="alrow")

        E("tensor", lambda e: e.matmul(pEF[0:128, 0:16], onesf[:], rowA[0:1, 0:16], start=True, stop=True),
          wait=["lgrow", "memsets"], inc="PE", event="mm_lg")
        E("tensor", lambda e: e.matmul(pGH[0:128, 0:16], onesf[:], rowA[0:1, 48:64], start=True, stop=True),
          wait=["alrow"], inc="PE", event="mm_al")
        E("vector", lambda e: e.tensor_copy(lgh[:], pEF[:, 0:16]), wait=["mm_lg"], inc="DV")
        E("vector", lambda e: e.tensor_copy(alh[:], pGH[:, 0:16]), wait=["mm_al"], inc="DV", event="bcast")
        E("scalar", lambda e: e.activation(gch[:], lgh[:], AF.Exp, bias=0.0, scale=CONSTS[:, 2:3]),
          wait=["bcast"], inc="AC", event="gch")

        def halves(dst, src, ev_w, ev_name):
            for g in range(NPAIR):
                E("vector", lambda e, g=g, dst=dst, src=src:
                  e.tensor_copy(dst[0:64, g:g + 1], src[0:64, 2 * g:2 * g + 1]),
                  wait=[ev_w] if g == 0 else [])
                E("vector", lambda e, g=g, dst=dst, src=src:
                  e.tensor_copy(dst[64:128, g:g + 1], src[64:128, 2 * g + 1:2 * g + 2]),
                  inc="DV" if g == NPAIR - 1 else None,
                  event=ev_name if g == NPAIR - 1 else None)
        for h in range(NHEAD):
            gpr, rh = h // 2, (h % 2) * 64
            E("vector", lambda e, gpr=gpr, rh=rh: e.memset(
                DGM[rh:rh + 64, gpr * 128 + rh: gpr * 128 + rh + 64], 1.0))
        halves(lgp, lgh, "bcast", "lgp")
        halves(alp, alh, "bcast", "alp")
        halves(gcp, gch, "gch", "gcp")

        for h in range(NHEAD):
            E("vector", lambda e, h=h: e.tensor_scalar(
                ydrop[:, 128:256], E1[:], lgh[:, h:h + 1], 0.0, op0=ALU.mult, op1=ALU.min),
              wait=(["e1", "bcast"] if h == 0 else [f"mme:{h - 1}"]),
              inc="DV", event=f"mmu:{h}")
            E("scalar", lambda e: e.activation(ydrop[:, 256:384], ydrop[:, 128:256], AF.Exp),
              wait=[f"mmu:{h}"], inc="AC", event=f"mme:{h}")
            E("vector", lambda e, h=h: e.tensor_tensor(
                MM[:, h * 128:(h + 1) * 128], ydrop[:, 256:384], CAUS[:], op=ALU.mult),
              wait=[f"mme:{h}", "caus"], inc="DV", event=f"mmk:{h}")
        for g in range(NPAIR):
            E("vector", lambda e, g=g: e.tensor_scalar(
                ydrop[:, 384:512], JJ[:], lgp[:, g:g + 1], 0.0, op0=ALU.mult, op1=ALU.min),
              wait=(["iota2", "lgp", "mme:15"] if g == 0 else [f"gpe:{g - 1}"]),
              inc="DV", event=f"gpu:{g}")
            E("scalar", lambda e, g=g: e.activation(
                GP[:, g * 128:(g + 1) * 128], ydrop[:, 384:512], AF.Exp),
              wait=[f"gpu:{g}"], inc="AC", event=f"gpe:{g}")
        for h in range(NHEAD):
            E("vector", lambda e, h=h: e.tensor_scalar(
                ydrop[:, 128:192], KK[:], lgh[:, h:h + 1], 0.0, op0=ALU.mult, op1=ALU.min),
              wait=(["iota3", "mme:15"] if h == 0 else [f"gwe:{h - 1}"]),
              inc="DV", event=f"gwu:{h}")
            E("scalar", lambda e, h=h: e.activation(
                GW[:, h * 64:(h + 1) * 64], ydrop[:, 128:192], AF.Exp),
              wait=[f"gwu:{h}"], inc="AC", event=f"gwe:{h}")

        wi = [0]
        def prep_w_bf16(dst, src_ap):
            for mt in range(8):
                i = wi[0]
                s = i % 2
                E("sync", lambda e, mt=mt, src_ap=src_ap, s=s: e.dma_start(
                    out=xA[:, s * 1024:(s + 1) * 1024], in_=src_ap[mt * 128:(mt + 1) * 128, :]),
                  wait=([f"wcast:{i - 2}"] if i > 1 else ["mm_al"]),
                  inc=f"LX{s}", amt=16, event=f"wld:{i}")
                E("vector", lambda e, s=s: e.tensor_copy(
                    xnbf[:, s * 1024:(s + 1) * 1024], xA[:, s * 1024:(s + 1) * 1024]),
                  wait=[f"wld:{i}", W(f"wtr:{i - 2}"), W(f"wtrb:{i - 2}")], inc="DV", event=f"wcast:{i}")
                for db in range(8):
                    eng = "scalar" if db < 4 else "sync"
                    sem = ("TP" if db < 4 else "TW") if s == 0 else ("TX0" if db < 4 else "TX1")
                    E(eng, lambda e, db=db, mt=mt, dst=dst, s=s: e.dma_start(
                        out=dst[:, db * 1024 + mt * 128: db * 1024 + (mt + 1) * 128],
                        in_=xnbf[:, s * 1024 + db * 128: s * 1024 + (db + 1) * 128], transpose=True),
                      wait=[f"wcast:{i}"] if db in (0, 4) else [],
                      inc=sem, amt=16,
                      event=(f"wtr:{i}" if db == 3 else (f"wtrb:{i}" if db == 7 else None)), dma=True)
                wi[0] += 1
        prep_w_bf16(WwT, wwap)
        prep_w_bf16(WrT, wrap_)
        nw = wi[0]
        for db in range(8):
            E("vector", lambda e, db=db: e.tensor_scalar(
                WrT[:, db * 1024:(db + 1) * 1024], WrT[:, db * 1024:(db + 1) * 1024],
                alp[:, db:db + 1], None, op0=ALU.mult),
              wait=[f"wtr:{nw - 1}", f"wtrb:{nw - 1}", f"wtr:{nw - 2}", f"wtrb:{nw - 2}", "alp"] if db == 0 else [],
              inc="DV" if db == 7 else None, event="wrfold" if db == 7 else None)

        qi = [0]
        def prep_w_fp8(src_slice, lane_dst3):
            i = qi[0]
            s = i % 2
            tdst = XQTb if s == 0 else WKT
            E("sync", lambda e, src_slice=src_slice, s=s: e.dma_start(
                out=xA[:, s * 1024:(s + 1) * 1024], in_=src_slice),
              wait=([f"qc1:{i - 2}"] if i > 1 else [f"wtr:{nw - 1}", f"wtrb:{nw - 1}", f"wtr:{nw - 2}", f"wtrb:{nw - 2}"]),
              inc=f"LX{s}", amt=16, event=f"qld:{i}")
            E("vector", lambda e, s=s: e.tensor_copy(
                maskb[:, s * 1024:(s + 1) * 1024], xA[:, s * 1024:(s + 1) * 1024]),
              wait=[f"qld:{i}"], inc="DV", event=f"qc1:{i}")
            E("vector", lambda e, s=s: e.tensor_copy(
                xnbf[:, s * 1024:(s + 1) * 1024], maskb[:, s * 1024:(s + 1) * 1024]),
              wait=[f"qc1:{i}", W(f"qtr:{i - 2}"), W(f"qtrb:{i - 2}")], inc="DV", event=f"qc2:{i}")
            for db in range(8):
                eng = "scalar" if db < 4 else "sync"
                sem = ("TP" if db < 4 else "TW") if s == 0 else ("TX0" if db < 4 else "TX1")
                E(eng, lambda e, db=db, s=s, tdst=tdst: e.dma_start(
                    out=tdst[:, db * 128:(db + 1) * 128],
                    in_=xnbf[:, s * 1024 + db * 128: s * 1024 + (db + 1) * 128], transpose=True),
                  wait=([f"qc2:{i}", W(f"q8t:{i - 2}")]) if db in (0, 4) else [],
                  inc=sem, amt=16,
                  event=(f"qtr:{i}" if db == 3 else (f"qtrb:{i}" if db == 7 else None)), dma=True)
            E("gpsimd", lambda e, lane_dst3=lane_dst3, tdst=tdst: e.tensor_copy(
                lane_dst3, tdst[:, 0:1024].rearrange("p (l t) -> p l t", l=8)),
              wait=[f"qtr:{i}", f"qtrb:{i}"], inc="PL", event=f"q8t:{i}")
            qi[0] += 1
        for nt in range(32):
            d3 = W1T8[:].rearrange("p (l n) -> p l n", l=8)[:, :, nt * 128:(nt + 1) * 128]
            prep_w_fp8(w1ap[nt * 128:(nt + 1) * 128, :], d3)
        for mt in range(8):
            for pc in range(4):
                d3 = W2T8[:].rearrange("p (l n) -> p l n", l=32)[:, pc * 8:(pc + 1) * 8, mt * 128:(mt + 1) * 128]
                prep_w_fp8(w2ap[mt * 128:(mt + 1) * 128, pc * 1024:(pc + 1) * 1024], d3)

        E("vector", lambda e: e.tensor_scalar(ydrop[:, 0:128], IOT[:], 0.0, None, op0=ALU.is_equal),
          wait=["iota1"], inc="DV")
        E("vector", lambda e: e.tensor_copy(idb32[:], ydrop[:, 0:128]), inc="DV", event="idb")
        E("sync", lambda e: e.dma_start(
            out=xA[0:32, 0:128], in_=b1ap.rearrange("(a b) -> a b", a=32)),
          wait=[f"q8t:{qi[0] - 1}", "idb"], inc="LP", amt=16, event="b1l")
        E("vector", lambda e: e.tensor_copy(xq8[0:32, 0:128], xA[0:32, 0:128]),
          wait=["b1l"], inc="DV")
        E("vector", lambda e: e.tensor_copy(xnbf[0:32, 0:128], xq8[0:32, 0:128]), inc="DV", event="b1q")
        E("tensor", lambda e: e.matmul(
            pEF[:, 0:32], xnbf[0:32, 0:128], idb32[0:32, 0:32], start=True, stop=True),
          wait=["b1q", "idb"], inc="PE", event="b1mm")
        E("vector", lambda e: e.tensor_copy(b1qf[:], pEF[:, 0:32]), wait=["b1mm"], inc="DV", event="b1done")
        E("sync", lambda e: e.dma_start(out=tcomb[0:1, 0:1024], in_=b2ap.rearrange("(a b) -> a b", a=1)),
          wait=["b1l"], inc="LP", amt=16, event="b2l")
        E("vector", lambda e: e.tensor_copy(HQ8[0:1, 0:1024], tcomb[0:1, 0:1024]),
          wait=["b2l", "b1q"], inc="DV")
        E("vector", lambda e: e.tensor_copy(RD[0:1, 0:1024], HQ8[0:1, 0:1024]), inc="DV", event="b2q")
        for hf in range(2):
            E("tensor", lambda e, hf=hf: e.matmul(
                pAB[:, hf * 512:(hf + 1) * 512], onesbf[:], RD[0:1, hf * 512:(hf + 1) * 512],
                start=True, stop=True),
              wait=["b2q", "b1mm"] if hf == 0 else [], inc="PE" if hf == 1 else None,
              event="b2mm" if hf == 1 else None)
        E("vector", lambda e: e.tensor_copy(b2row[:], pAB[:]), wait=["b2mm"], inc="DV", event="b2done")

        if dbg2:
            pdump = [("p_iot", IOT[:], F32), ("p_caus", CAUS[:], F32), ("p_ydtmp", ydrop[:, 0:128], F32),
                     ("p_tcomb", tcomb[0:1, 0:1024], F32), ("p_hqrow", HQ8[0:1, 0:1024], FP8),
                     ("p_rdrow", RD[0:1, 0:1024], BF16), ("p_ones", onesbf[0:1, 0:128], BF16),
                     ("p_idb", idb32[:], BF16), ("p_mm", MM[:], BF16), ("p_gw", GW[:], BF16),
                     ("p_gp", GP[:], BF16), ("p_gcp", gcp[:], F32), ("p_dgm", DGM[:], BF16),
                     ("p_b2row", b2row[:], FP8)]
            first = True
            for nm, src, dt_ in pdump:
                dst = nc.dram_tensor(nm, list(src.shape), dt_, kind="ExternalOutput").ap()
                E("gpsimd", lambda e, dst=dst, src=src: e.dma_start(out=dst, in_=src),
                  wait=([(s, p.cnt.get(s, 0)) for s in ("LP", "TP", "PE", "DV", "AC", "PL")] if first else []),
                  inc="ST", amt=16, dma=True)
                first = False
        E("sync", lambda e: e.nop())
        prepw = [(s, p.cnt.get(s, 0)) for s in ("LP", "TP", "TW", "TX0", "TX1", "LX0", "LX1", "PE", "DV", "AC", "PL", "ST")]
        for eng in Prog.ENGS:
            E(eng, lambda e: e.nop(), wait=prepw)

        for c in range(n_chunks):
            s2, s3, s4 = c % 2, c % 3, c % 4
            g, cc = c // GRP, c % GRP
            gs = g % 2

            E("sync", lambda e, c=c, s2=s2: e.dma_start(
                out=xA[:, s2 * 1024:(s2 + 1) * 1024], in_=xap[c * 128:(c + 1) * 128, :]),
              wait=[W(f"xn:{c - 2}")], inc=f"LX{s2}", amt=16, event=f"xld:{c}")

            E("vector", lambda e, s2=s2: e.tensor_reduce(
                mu[:, s2:s2 + 1], xA[:, s2 * 1024:(s2 + 1) * 1024], axis=AXX, op=ALU.add),
              wait=[f"xld:{c}"], inc="DV", event=f"mu:{c}")
            E("scalar", lambda e, s2=s2: e.activation(
                xnf[:, 0:1024], xA[:, s2 * 1024:(s2 + 1) * 1024], AF.Square,
                accum_out=ssq[:, s2:s2 + 1]),
              wait=[f"xld:{c}", W(f"xq8:{c - 1}"), W(f"xnbf:{c - 1}")], inc="AC", event=f"ssq:{c}")
            E("vector", lambda e, s2=s2: e.tensor_scalar(
                mcol[:, s2:s2 + 1], mu[:, s2:s2 + 1], 1.0 / 1024.0, None, op0=ALU.mult),
              wait=[f"ssq:{c}", f"mu:{c}"])
            E("vector", lambda e, s2=s2: e.tensor_tensor(
                m2col[:, s2:s2 + 1], mcol[:, s2:s2 + 1], mcol[:, s2:s2 + 1], op=ALU.mult))
            E("vector", lambda e, s2=s2: e.scalar_tensor_tensor(
                vcol[:, s2:s2 + 1], ssq[:, s2:s2 + 1], 1.0 / 1024.0, m2col[:, s2:s2 + 1],
                op0=ALU.mult, op1=ALU.subtract), inc="DV", event=f"var:{c}")
            E("scalar", lambda e, s2=s2: e.activation(
                lvcol[:, s2:s2 + 1], vcol[:, s2:s2 + 1], AF.Ln, bias=CONSTS[:, 0:1]),
              wait=[f"var:{c}"])
            E("scalar", lambda e, s2=s2, s8=c % 8: e.activation(
                rstd[:, s8:s8 + 1], lvcol[:, s2:s2 + 1], AF.Exp, bias=0.0, scale=CONSTS[:, 1:2]),
              inc="AC", event=f"rstd:{c}")
            E("vector", lambda e, s2=s2, s8=c % 8: e.scalar_tensor_tensor(
                nmb[:, s8:s8 + 1], mcol[:, s2:s2 + 1], -1.0, rstd[:, s8:s8 + 1],
                op0=ALU.mult, op1=ALU.mult),
              wait=[f"rstd:{c}"], inc="DV", event=f"nmb:{c}")
            E("vector", lambda e, s2=s2, s8=c % 8: e.tensor_scalar(
                xnf[:, 0:1024], xA[:, s2 * 1024:(s2 + 1) * 1024],
                rstd[:, s8:s8 + 1], nmb[:, s8:s8 + 1], op0=ALU.mult, op1=ALU.add),
              wait=[f"nmb:{c}", f"ssq:{c}"], inc="DV", event=f"xn:{c}")

            E("vector", lambda e, s4=s4: e.tensor_copy(
                xnbf[:, s4 * 1024:(s4 + 1) * 1024], xnf[:, 0:1024]),
              wait=[f"xn:{c}", W(f"out:{c - 4}"), W(f"xnT:{c - 4}")], inc="DV", event=f"xnbf:{c}")
            E("gpsimd", lambda e: e.tensor_copy(xq8[:, 0:1024], xnf[:, 0:1024]),
              wait=[f"xn:{c}"], inc="PL", event=f"xq8:{c}")
            E("gpsimd", lambda e: e.tensor_copy(xqbf[:, 0:1024], xq8[:, 0:1024]),
              wait=[W(f"xqT:{c - 1}")], inc="PL", event=f"xqbf:{c}")

            for db in range(8):
                E("scalar", lambda e, db=db, s3=s3, s4=s4: e.dma_start(
                    out=XNF[:, db * 384 + s3 * 128: db * 384 + (s3 + 1) * 128],
                    in_=xnbf[:, s4 * 1024 + db * 128: s4 * 1024 + (db + 1) * 128], transpose=True),
                  wait=([f"xnbf:{c}", W(f"sc:{c - 3}:3"), W(f"rkgp:{c - 3}")] if db == 0 else []),
                  inc=f"TX{s3}", amt=16, event=(f"xnT:{c}" if db == 7 else None), dma=True)
            for db in range(8):
                E("scalar", lambda e, db=db: e.dma_start(
                    out=XQTb[:, db * 128:(db + 1) * 128],
                    in_=xqbf[:, db * 128:(db + 1) * 128], transpose=True),
                  wait=([f"xqbf:{c}", W(f"xqf8:{c - 1}")] if db == 0 else []),
                  inc="TQ", amt=16, event=(f"xqT:{c}" if db == 7 else None), dma=True)
            E("gpsimd", lambda e, s4=s4, gs=gs: e.tensor_copy(
                XQF8[:].rearrange("p (l t) -> p l t", l=8)[:, :, gs * 512 + s4 * 128: gs * 512 + (s4 + 1) * 128],
                XQTb[:, 0:1024].rearrange("p (l t) -> p l t", l=8)),
              wait=[f"xqT:{c}", W(f"ffn1:{g - 2}")], inc="PL", event=f"xqf8:{c}")

            E("gpsimd", lambda e, s3=s3: e.tensor_copy(
                WKF[:].rearrange("p (l t) -> p l t", l=8)[:, :, 1:128],
                XNF3[:, :, s3 * 128: s3 * 128 + 127]),
              wait=[f"xnT:{c}", W(f"sc:{c - 1}:3"), W(f"wkT:{c - 1}"), "prev0"], inc="PL")
            E("gpsimd", lambda e: e.tensor_copy(
                WKF[:].rearrange("p (l t) -> p l t", l=8)[:, :, 0:1], PREV3), inc="PL")
            E("gpsimd", lambda e, s3=s3: e.tensor_copy(
                PREV3, XNF3[:, :, s3 * 128 + 127: s3 * 128 + 128]),
              inc="PL", event=f"wkf:{c}")
            for db in range(8):
                E("scalar", lambda e, db=db: e.dma_start(
                    out=WKT[:, db * 128:(db + 1) * 128],
                    in_=WKF[:, db * 128:(db + 1) * 128], transpose=True),
                  wait=([f"wkf:{c}", W(f"upd:{c - 1}:1")] if db == 0 else []),
                  inc="TW", amt=16, event=(f"wkT:{c}" if db == 7 else None), dma=True)

            E("vector", lambda e, s3=s3: e.tensor_tensor(
                rkgp[:].rearrange("p (l t) -> p l t", l=8),
                XNF3[:, :, s3 * 128:(s3 + 1) * 128],
                GP[:].rearrange("p (l t) -> p l t", l=8), op=ALU.mult),
              wait=[f"xnT:{c}", W(f"reads:{c - 1}")], inc="DV", event=f"rkgp:{c}")

            vw = [f"xnT:{c}", W(f"vevac:{c - 1}")]
            if cc == 0 and g > 0:
                vw.append(f"y2:{g - 1}:3")
            for db in range(8):
                for hf in range(2):
                    E("tensor", lambda e, db=db, hf=hf, s3=s3: e.matmul(
                        pAB[:, hf * 512:(hf + 1) * 512],
                        XNF[:, db * 384 + s3 * 128: db * 384 + (s3 + 1) * 128],
                        WwT[:, db * 1024 + hf * 512: db * 1024 + (hf + 1) * 512],
                        start=(db == 0), stop=(db == 7)),
                      wait=(vw if (db == 0 and hf == 0) else []),
                      inc=("PE" if (db == 7 and hf == 1) else None),
                      event=(f"vproj:{c}" if (db == 7 and hf == 1) else None))
            first = True
            for h in range(NHEAD):
                off = h * 128 + (h % 2) * 64
                E("vector", lambda e, h=h, off=off: e.tensor_copy(
                    VZ[:, off:off + 64], pAB[:, h * 64:(h + 1) * 64]),
                  wait=([f"vproj:{c}", W(f"upd:{c - 1}:1")] if first else []), inc="DV")
                first = False
            E("vector", lambda e: e.tensor_tensor(gv[:, 0:1024], pAB[:], GW[:], op=ALU.mult),
              inc="DV", event=f"vevac:{c}")
            if dbg2 and c >= 11 and c <= 15:
                dvz = nc.dram_tensor(f"dbg_vz{c}", [128, 2048], BF16, kind="ExternalOutput").ap()
                dgv = nc.dram_tensor(f"dbg_gv{c}", [128, 1024], BF16, kind="ExternalOutput").ap()
                dxn = nc.dram_tensor(f"dbg_xn{c}", [128, 3072], BF16, kind="ExternalOutput").ap()
                E("gpsimd", lambda e, dvz=dvz: e.dma_start(out=dvz, in_=VZ[:]),
                  wait=[f"vevac:{c}"], inc="GA", amt=16, dma=True)
                E("gpsimd", lambda e, dgv=dgv: e.dma_start(out=dgv, in_=gv[:]),
                  inc="GA", amt=16, dma=True)
                E("gpsimd", lambda e, dxn=dxn: e.dma_start(out=dxn, in_=XNF[:]),
                  inc="GA", amt=16, event=f"dbgd:{c}", dma=True)
                p.q["tensor"].append((lambda e, _w=p.ev[f"dbgd:{c}"]: e.wait_ge(sems[_w[0]], _w[1]), [], None, 1))

            SCB = [(pEF, 0), (pEF, 512), (pGH, 0), (pGH, 512)]

            if no_scan:
                continue

            def sc_wave(s):
                sw = [f"wkf:{c}", f"xnT:{c}"]
                if s == 0:
                    sw.append(W(f"sm:{c - 1}:3"))
                    sw.append(W(f"state:{c - 1}"))
                    if cc == 0 and g > 0:
                        sw.append(W(f"wev:{g - 1}:31"))
                else:
                    sw.append(f"sm:{c}:{s - 1}")
                for j in range(4):
                    h = s * 4 + j
                    db, rh = h // 2, (h % 2) * 64
                    pt, off = SCB[j]
                    E("tensor", lambda e, db=db, rh=rh, pt=pt, off=off, s3=s3: e.matmul(
                        pt[:, off:off + 128],
                        WKF[rh:rh + 64, db * 128:(db + 1) * 128],
                        XNF[rh:rh + 64, db * 384 + s3 * 128: db * 384 + (s3 + 1) * 128],
                        start=True, stop=True),
                      wait=(sw if j == 0 else []),
                      inc=("PE" if j == 3 else None), event=(f"sc:{c}:{s}" if j == 3 else None))

            def sm_evac(s):
                smw = [f"sc:{c}:{s}"]
                if s >= 2:
                    smw.append(W(f"iw:{c}:{s - 2}"))
                elif c > 0:
                    smw.append(W(f"iw:{c - 1}:{s + 2}"))
                if s == 0:
                    smw.append(W(f"reads:{c - 1}"))
                sl = (s % 2) * 512
                E("vector", lambda e, s=s, sl=sl: e.tensor_tensor(
                    SM[:, sl:sl + 256].rearrange("p (a b) -> p a b", a=2),
                    pEF[:].rearrange("p (a b) -> p a b", a=2)[:, :, 0:128],
                    MM[:, s * 512:s * 512 + 256].rearrange("p (a b) -> p a b", a=2),
                    op=ALU.mult),
                  wait=smw, inc="DV")
                E("vector", lambda e, s=s, sl=sl: e.tensor_tensor(
                    SM[:, sl + 256:sl + 512].rearrange("p (a b) -> p a b", a=2),
                    pGH[:].rearrange("p (a b) -> p a b", a=2)[:, :, 0:128],
                    MM[:, s * 512 + 256:(s + 1) * 512].rearrange("p (a b) -> p a b", a=2),
                    op=ALU.mult),
                  inc="DV", event=f"sm:{c}:{s}")

            def inter_pair(s):
                rw = [f"rkgp:{c}", W(f"state:{c - 1}"), f"vevac:{c}",
                      (f"rd:{c}:{s - 1}" if s > 0 else W(f"rd:{c - 1}:3"))]
                if s == 0 and c >= GRP and cc == 0:
                    rw.append(W(f"out:{c - 1}"))
                for q in range(2):
                    gpr = 2 * s + q
                    E("tensor", lambda e, gpr=gpr, q=q: e.matmul(
                        pCD[:, q * 512:q * 512 + 128],
                        SBF[:, gpr * 128:(gpr + 1) * 128],
                        rkgp[:, gpr * 128:(gpr + 1) * 128],
                        start=True, stop=False),
                      wait=(rw if q == 0 else []))

            def intra_wave(s, extra=()):
                for j in range(4):
                    h = s * 4 + j
                    q = j // 2
                    last = (j % 2) == 1
                    sl = (s % 2) * 512
                    E("tensor", lambda e, h=h, q=q, sl=sl, j=j, last=last: e.matmul(
                        pCD[:, q * 512:q * 512 + 128],
                        VZ[:, h * 128:(h + 1) * 128],
                        SM[:, sl + j * 128: sl + (j + 1) * 128],
                        start=False, stop=last),
                      wait=(list(extra) if j == 0 else []),
                      inc=("PE" if j == 3 else None), event=(f"iw:{c}:{s}" if j == 3 else None))

            def rd_evac(s):
                for q in range(2):
                    gpr = 2 * s + q
                    E("vector", lambda e, gpr=gpr, q=q, s4=s4: e.tensor_copy(
                        RD[:, s4 * 1024 + gpr * 128:(s4 * 1024) + (gpr + 1) * 128],
                        pCD[:, q * 512:q * 512 + 128]),
                      wait=([f"iw:{c}:{s}"] + ([W(f"rproj:{c - 4}")] if s == 0 else [])) if q == 0 else [],
                      inc="DV", event=(f"rd:{c}:{s}" if q == 1 else None))

            sc_wave(0)
            sm_evac(0)
            inter_pair(0)
            sc_wave(1)
            sm_evac(1)
            intra_wave(0, extra=[f"sm:{c}:0"])
            rd_evac(0)
            inter_pair(1)
            sc_wave(2)
            sm_evac(2)
            intra_wave(1, extra=[f"sm:{c}:1"])
            rd_evac(1)
            inter_pair(2)
            sc_wave(3)
            sm_evac(3)
            intra_wave(2, extra=[f"sm:{c}:2"])
            rd_evac(2)
            inter_pair(3)
            intra_wave(3, extra=[f"sm:{c}:3"])
            rd_evac(3)
            p.ev[f"reads:{c}"] = p.ev[f"iw:{c}:3"]
            p.ev[f"rd:{c}"] = p.ev[f"rd:{c}:3"]

            for u in range(2):
                uw = [f"wkT:{c}", f"vevac:{c}", f"sm:{c}:3"]
                if u == 0:
                    uw.append(W(f"state:{c - 1}"))
                else:
                    uw.append(f"stdec:{c}:0")
                for q in range(4):
                    gpr = 4 * u + q
                    pt, off = SCB[q]
                    E("tensor", lambda e, gpr=gpr, pt=pt, off=off: e.matmul(
                        pt[:, off:off + 128],
                        WKT[:, gpr * 128:(gpr + 1) * 128],
                        gv[:, gpr * 128:(gpr + 1) * 128],
                        start=True, stop=True),
                      wait=(uw if q == 0 else []),
                      inc=("PE" if q == 3 else None),
                      event=(f"upd:{c}:{u}" if q == 3 else None))
                for q in range(4):
                    gpr = 4 * u + q
                    E("vector", lambda e, gpr=gpr: e.tensor_scalar(
                        SF32a[:, gpr * 128:(gpr + 1) * 128], SF32a[:, gpr * 128:(gpr + 1) * 128],
                        gcp[:, gpr:gpr + 1], None, op0=ALU.mult),
                      wait=[f"upd:{c}:{u}", "gcp"] if q == 0 else [])
                b0 = 4 * u * 128
                E("vector", lambda e, b0=b0: e.tensor_tensor(
                    SF32a[:, b0:b0 + 256].rearrange("p (a b) -> p a b", a=2),
                    SF32a[:, b0:b0 + 256].rearrange("p (a b) -> p a b", a=2),
                    pEF[:].rearrange("p (a b) -> p a b", a=2)[:, :, 0:128],
                    op=ALU.add), inc="DV")
                E("vector", lambda e, b0=b0: e.tensor_tensor(
                    SF32a[:, b0 + 256:b0 + 512].rearrange("p (a b) -> p a b", a=2),
                    SF32a[:, b0 + 256:b0 + 512].rearrange("p (a b) -> p a b", a=2),
                    pGH[:].rearrange("p (a b) -> p a b", a=2)[:, :, 0:128],
                    op=ALU.add), inc="DV", event=f"stdec:{c}:{u}")
            E("vector", lambda e: e.tensor_tensor(SBF[:], SF32a[:], DGM[:], op=ALU.mult),
              inc="DV", event=f"state:{c}")

            if cc == GRP - 1 and not no_b:
                for m in range(32):
                    f1w = [f"xqf8:{c}", W(f"state:{c}")] if m < 2 else [f"xqf8:{c}"]
                    if m >= 2:
                        f1w.append(f"wev:{g}:{m - 2}")
                    elif g > 0:
                        f1w.append(f"wev:{g - 1}:{30 + m}")
                    for db in range(8):
                        E("tensor", lambda e, m=m, db=db, gs=gs: e.matmul(
                            pGH[:, (m % 2) * 512:(m % 2 + 1) * 512],
                            W1T8[:, db * 4096 + m * 128: db * 4096 + (m + 1) * 128],
                            XQF8[:, db * 1024 + gs * 512: db * 1024 + (gs + 1) * 512],
                            start=(db == 0), stop=(db == 7)),
                          wait=(f1w if db == 0 else []),
                          inc=("PE" if db == 7 else None),
                          event=(f"ffn1m:{g}:{m}" if db == 7 else None))
                    E("scalar", lambda e, m=m: e.activation(
                        wbf[:, (m % 2) * 512:(m % 2 + 1) * 512], pGH[:, (m % 2) * 512:(m % 2 + 1) * 512],
                        AF.Relu, bias=b1qf[:, m:m + 1]),
                      wait=[f"ffn1m:{g}:{m}", W(f"hq:{g}:{m - 2}")], inc="AC", event=f"wev:{g}:{m}")
                    hqw = [f"wev:{g}:{m}"]
                    if m == 0 and g >= 1:
                        hqw.append(f"ffn2:{g - 1}:3")
                    E("gpsimd", lambda e, m=m: e.tensor_copy(
                        HQ8[:, m * 512:(m + 1) * 512], wbf[:, (m % 2) * 512:(m % 2 + 1) * 512]),
                      wait=hqw, inc="PL", event=f"hq:{g}:{m}")
                p.ev[f"ffn1:{g}"] = p.ev[f"ffn1m:{g}:31"]

                for cc2 in range(GRP):
                    c2 = g * GRP + cc2
                    sB2, sB4 = c2 % 2, c2 % 4
                    E("sync", lambda e, c2=c2, sB2=sB2: e.dma_start(
                        out=maskb[:, sB2 * 1024:(sB2 + 1) * 1024], in_=mkap[c2 * 128:(c2 + 1) * 128, :]),
                      wait=[W(f"ydrop:{c2 - 2}")], inc=f"LM{sB2}", amt=16, event=f"mld:{c2}")
                    f2w = [f"hq:{g}:31"]
                    f2w.append(f"y2:{g}:{cc2 - 1}" if cc2 > 0 else f"vevac:{g * GRP + 3}")
                    for nb in range(32):
                        for hf in range(2):
                            E("tensor", lambda e, nb=nb, hf=hf, cc2=cc2: e.matmul(
                                pAB[:, hf * 512:(hf + 1) * 512],
                                HQ8[:, nb * 512 + cc2 * 128: nb * 512 + (cc2 + 1) * 128],
                                W2T8[:, nb * 1024 + hf * 512: nb * 1024 + (hf + 1) * 512],
                                start=(nb == 0), stop=(nb == 31)),
                              wait=(f2w if (nb == 0 and hf == 0) else []),
                              inc=("PE" if (nb == 31 and hf == 1) else None),
                              event=(f"ffn2:{g}:{cc2}" if (nb == 31 and hf == 1) else None))
                    E("vector", lambda e: e.tensor_tensor(y2b[:], pAB[:], b2row[:], op=ALU.add),
                      wait=[f"ffn2:{g}:{cc2}", "b2done"], inc="DV", event=f"y2:{g}:{cc2}")
                    E("vector", lambda e, sB2=sB2: e.tensor_tensor(
                        ydrop[:], y2b[:], maskb[:, sB2 * 1024:(sB2 + 1) * 1024], op=ALU.mult),
                      wait=[f"mld:{c2}"], inc="DV", event=f"ydrop:{c2}")
                    rpw = [f"rd:{c2}", "wrfold", f"y2:{g}:{cc2}"]
                    rpw.append(f"out:{c2 - 1}" if cc2 > 0 else f"rd:{g * GRP + 3}")
                    for jb in range(8):
                        for hf in range(2):
                            E("tensor", lambda e, jb=jb, hf=hf, sB4=sB4: e.matmul(
                                pCD[:, hf * 512:(hf + 1) * 512],
                                RD[:, sB4 * 1024 + jb * 128: sB4 * 1024 + (jb + 1) * 128],
                                WrT[:, jb * 1024 + hf * 512: jb * 1024 + (hf + 1) * 512],
                                start=(jb == 0), stop=(jb == 7)),
                              wait=(rpw if (jb == 0 and hf == 0) else []),
                              inc=("PE" if (jb == 7 and hf == 1) else None),
                              event=(f"rproj:{c2}" if (jb == 7 and hf == 1) else None))
                    E("vector", lambda e, sB4=sB4: e.scalar_tensor_tensor(
                        tcomb[:], pCD[:], 0.0, xnbf[:, sB4 * 1024:(sB4 + 1) * 1024],
                        op0=ALU.add, op1=ALU.add),
                      wait=[f"rproj:{c2}", f"ydrop:{c2}", f"xnbf:{c2}", W(f"st:{c2 - 1}")])
                    E("vector", lambda e: e.tensor_tensor(tcomb[:], tcomb[:], ydrop[:], op=ALU.add),
                      inc="DV", event=f"out:{c2}")
                    if not no_accum:
                        E("gpsimd", lambda e, c2=c2: e.dma_start(
                            out=tcomb[:], in_=xap[c2 * 128:(c2 + 1) * 128, :], accum_op=ALU.add),
                          wait=[f"out:{c2}"], inc="GA", amt=16, event=f"acc:{c2}", dma=True)
                    E("gpsimd", lambda e, c2=c2: e.dma_start(
                        out=yap[c2 * 128:(c2 + 1) * 128, :], in_=tcomb[:]),
                      wait=[f"acc:{c2}" if not no_accum else f"out:{c2}"],
                      inc="ST", amt=16, event=f"st:{c2}", dma=True)

        if dbg2:
            dbg_b1 = nc.dram_tensor("dbg_b1", [128, 32], F32, kind="ExternalOutput").ap()
            dbg_b2 = nc.dram_tensor("dbg_b2", [128, 1024], FP8, kind="ExternalOutput").ap()
            allw2 = [(s, p.cnt.get(s, 0)) for s in p.cnt]
            E("gpsimd", lambda e: e.dma_start(out=dbg_b1, in_=b1qf[:]), wait=allw2, inc="ST", amt=16, dma=True)
            E("gpsimd", lambda e: e.dma_start(out=dbg_b2, in_=b2row[:]), inc="ST", amt=16, dma=True)
        if dbg:
            dbg_rd = nc.dram_tensor("dbg_rd", [128, 4096], BF16, kind="ExternalOutput").ap()
            dbg_hq = nc.dram_tensor("dbg_hq", [128, 32 * 512], FP8, kind="ExternalOutput").ap()
            dbg_sbf = nc.dram_tensor("dbg_sbf", [128, 1024], BF16, kind="ExternalOutput").ap()
            dbg_vz = nc.dram_tensor("dbg_vz", [128, 2048], BF16, kind="ExternalOutput").ap()
            dbg_xq = nc.dram_tensor("dbg_xq", [128, 8192], FP8, kind="ExternalOutput").ap()
            allw = [(s, p.cnt.get(s, 0)) for s in p.cnt]
            E("gpsimd", lambda e: e.dma_start(out=dbg_rd, in_=RD[:]), wait=allw, inc="ST", amt=16, dma=True)
            E("gpsimd", lambda e: e.dma_start(out=dbg_hq, in_=HQ8[:]), inc="ST", amt=16, dma=True)
            E("gpsimd", lambda e: e.dma_start(out=dbg_sbf, in_=SBF[:]), inc="ST", amt=16, dma=True)
            E("gpsimd", lambda e: e.dma_start(out=dbg_vz, in_=VZ[:]), inc="ST", amt=16, dma=True)
            E("gpsimd", lambda e: e.dma_start(out=dbg_xq, in_=XQF8[:]), inc="ST", amt=16, dma=True)
        if "ST" in p.cnt:
            E("gpsimd", lambda e: e.wait_ge(sems["ST"], p.cnt["ST"]), dma=True)

        with nc.Block() as block:
            @block.sync
            def _(eng):
                p.replay(eng, "sync", sems)

            @block.tensor
            def _(eng):
                p.replay(eng, "tensor", sems)

            @block.vector
            def _(eng):
                p.replay(eng, "vector", sems)

            @block.scalar
            def _(eng):
                p.replay(eng, "scalar", sems)

            @block.gpsimd
            def _(eng):
                p.replay(eng, "gpsimd", sems)
    return nc


_CACHE = {}


def _bernoulli_keep(shape, k0=0, k1=42, p=0.8):
    rot1 = (13, 15, 26, 6)
    rot2 = (17, 29, 16, 24)
    ks = (np.uint32(k0), np.uint32(k1),
          np.uint32(k0) ^ np.uint32(k1) ^ np.uint32(0x1BD11BDA))
    n = int(np.prod(shape))
    x1 = np.arange(n, dtype=np.uint32)
    x0 = np.zeros(n, dtype=np.uint32)
    x0 = (x0 + ks[0]).astype(np.uint32)
    x1 = (x1 + ks[1]).astype(np.uint32)
    for d in range(5):
        for r in (rot1 if d % 2 == 0 else rot2):
            x0 = (x0 + x1).astype(np.uint32)
            x1 = ((x1 << np.uint32(r)) | (x1 >> np.uint32(32 - r))).astype(np.uint32)
            x1 = (x1 ^ x0).astype(np.uint32)
        x0 = (x0 + ks[(d + 1) % 3]).astype(np.uint32)
        x1 = (x1 + ks[(d + 2) % 3] + np.uint32(d + 1)).astype(np.uint32)
    bits = x0 ^ x1
    u = ((bits >> np.uint32(9)) | np.uint32(0x3F800000)).view(np.float32) - np.float32(1.0)
    return (u < np.float32(p)).reshape(shape)


def kernel(L0_attn_out, L0_attn_W_write, L0_attn_W_read, L0_attn_decay, L0_attn_log_alpha,
           L0_ffn_W1, L0_ffn_b1, L0_ffn_W2, L0_ffn_b2):
    x = np.ascontiguousarray(np.asarray(L0_attn_out, dtype=np.float32))
    B = x.shape[0]
    keep = _bernoulli_keep(x.shape)
    mask = np.where(keep, np.float32(1.25), np.float32(0.0)).astype(ml_dtypes.float8_e5m2)

    if "nc" not in _CACHE:
        _CACHE["nc"] = build_nc()
    nc = _CACHE["nc"]

    ww = np.ascontiguousarray(np.asarray(L0_attn_W_write, dtype=np.float32))
    wr = np.ascontiguousarray(np.asarray(L0_attn_W_read, dtype=np.float32))
    w1 = np.ascontiguousarray(np.asarray(L0_ffn_W1, dtype=np.float32))
    w2 = np.ascontiguousarray(np.asarray(L0_ffn_W2, dtype=np.float32))
    b1 = np.ascontiguousarray(np.asarray(L0_ffn_b1, dtype=np.float32))
    b2 = np.ascontiguousarray(np.asarray(L0_ffn_b2, dtype=np.float32))
    dc = np.ascontiguousarray(np.asarray(L0_attn_decay, dtype=np.float32))
    la = np.ascontiguousarray(np.asarray(L0_attn_log_alpha, dtype=np.float32))

    in_maps = []
    for b in range(B):
        in_maps.append({
            "x": np.ascontiguousarray(x[b]),
            "mask": np.ascontiguousarray(mask[b]),
            "Ww": ww, "Wr": wr, "W1": w1, "W2": w2,
            "b1": b1, "b2": b2, "decay": dc, "logalpha": la,
        })
    res = run_bass_kernel_spmd(nc, in_maps, core_ids=list(range(B)))
    out = np.stack([np.asarray(res.results[b]["y"]) for b in range(B)], axis=0)
    return out.astype(np.float32)


# version 09
# speedup vs baseline: 1.1228x; 1.0224x over previous
import numpy as np
import ml_dtypes
import concourse.bass as bass
import concourse.mybir as mybir
from concourse.bass_utils import run_bass_kernel_spmd
from contextlib import ExitStack

F32 = mybir.dt.float32
BF16 = mybir.dt.bfloat16
FP8 = mybir.dt.float8e5
AF = mybir.ActivationFunctionType
ALU = mybir.AluOpType
AXX = mybir.AxisListType.X

T, D = 8192, 1024
CH = 128
NCH = T // CH
GRP = 4
NHEAD, HD, NPAIR = 16, 64, 8


class Prog:
    ENGS = ("sync", "tensor", "vector", "scalar", "gpsimd")

    def __init__(self):
        self.q = {e: [] for e in self.ENGS}
        self.cnt = {}
        self.ev = {}

    SELF_SEM = {"vector": "DV", "scalar": "AC", "gpsimd": "PL"}

    def emit(self, eng, fn, wait=(), inc=None, amt=1, event=None, dma=False):
        waits = []
        for w in wait:
            if w is None:
                continue
            waits.append(self.ev[w] if isinstance(w, str) else w)
        if not dma and eng in self.SELF_SEM:
            ss = self.SELF_SEM[eng]
            assert inc in (None, ss), (eng, inc)
            waits.append((ss, self.cnt.get(ss, 0)))
            inc, amt = ss, 1
        self.q[eng].append((fn, waits, inc, amt))
        if inc is not None:
            self.cnt[inc] = self.cnt.get(inc, 0) + amt
            if event is not None:
                self.ev[event] = (inc, self.cnt[inc])
        else:
            assert event is None, event

    def replay(self, eng_obj, name, sems):
        waited = {}
        for fn, waits, inc, amt in self.q[name]:
            for s, v in waits:
                if waited.get(s, -1) >= v:
                    continue
                eng_obj.wait_ge(sems[s], v)
                waited[s] = v
            ins = fn(eng_obj)
            if inc is not None:
                ins.then_inc(sems[inc], amt)


def build_nc(n_chunks=NCH, no_b=False, no_accum=False, no_pe=False, no_attn=False, no_scan=False, dbg=False, dbg2=False):
    nc = bass.Bass("TRN2", target_bir_lowering=False, dynamic_dma_scratch_size=16384)
    xap = nc.dram_tensor("x", [T, D], F32, kind="ExternalInput").ap()
    mkap = nc.dram_tensor("mask", [T, D], FP8, kind="ExternalInput").ap()
    wwap = nc.dram_tensor("Ww", [D, D], F32, kind="ExternalInput").ap()
    wrap_ = nc.dram_tensor("Wr", [D, D], F32, kind="ExternalInput").ap()
    w1ap = nc.dram_tensor("W1", [4 * D, D], F32, kind="ExternalInput").ap()
    w2ap = nc.dram_tensor("W2", [D, 4 * D], F32, kind="ExternalInput").ap()
    b1ap = nc.dram_tensor("b1", [4 * D], F32, kind="ExternalInput").ap()
    b2ap = nc.dram_tensor("b2", [D], F32, kind="ExternalInput").ap()
    dcap = nc.dram_tensor("decay", [NHEAD], F32, kind="ExternalInput").ap()
    laap = nc.dram_tensor("logalpha", [NHEAD], F32, kind="ExternalInput").ap()
    yap = nc.dram_tensor("y", [T, D], F32, kind="ExternalOutput").ap()

    p = Prog()
    E = p.emit

    def W(name):
        return name if name in p.ev else None
    with ExitStack() as ctx:
        def sb(name, shape, dt):
            return ctx.enter_context(nc.sbuf_tensor(name, shape, dt))

        WwT = sb("WwT", [128, 8 * 1024], BF16)
        WrT = sb("WrT", [128, 8 * 1024], BF16)
        W1T8 = sb("W1T8", [128, 8 * 4096], FP8)
        W2T8 = sb("W2T8", [128, 32 * 1024], FP8)
        b1qf = sb("b1qf", [128, 32], F32)
        b2row = sb("b2row", [128, 1024], FP8)
        MM = sb("MM", [128, 16 * 128], BF16)
        GP = sb("GP", [128, 8 * 128], BF16)
        GW = sb("GW", [128, 16 * 64], BF16)
        CONSTS = sb("CONSTS", [128, 4], F32)
        gcp = sb("gcp", [128, 8], F32)
        xA = sb("xA", [128, 2 * 1024], F32)
        xnbf = sb("xnbf", [128, 4 * 1024], BF16)
        prep_ctx = ExitStack()

        def pb(name, shape, dt):
            return prep_ctx.enter_context(nc.sbuf_tensor(name, shape, dt))

        IOT = pb("IOT", [128, 128], F32)
        JJ = pb("JJ", [128, 128], F32)
        KK = pb("KK", [128, 64], F32)
        CAUS = pb("CAUS", [128, 128], F32)
        E1 = pb("E1", [128, 128], F32)
        idb32 = pb("idb32", [128, 128], BF16)
        onesf = pb("onesf", [1, 128], F32)
        onesbf = pb("onesbf", [1, 128], BF16)
        ones11 = pb("ones11", [1, 1], BF16)
        rowA = pb("rowA", [1, 64], F32)
        lgh = pb("lgh", [128, 16], F32)
        alh = pb("alh", [128, 16], F32)
        gch = pb("gch", [128, 16], F32)
        lgp = pb("lgp", [128, 8], F32)
        alp = pb("alp", [128, 8], F32)
        prep_ctx.close()
        xnf = sb("xnf", [128, 1024], F32)
        xq8 = sb("xq8", [128, 1024], FP8)
        xqbf = sb("xqbf", [128, 1024], BF16)
        XQTb = sb("XQTb", [128, 1024], BF16)
        XNF = sb("XNF", [128, 8 * 384], BF16)
        XQF8 = sb("XQF8", [128, 8 * 1024], FP8)
        WKF = sb("WKF", [128, 1024], BF16)
        WKT = sb("WKT", [128, 1024], BF16)
        PREV = sb("PREV", [128, 8], BF16)
        rkgp = sb("rkgp", [128, 1024], BF16)
        VZ = sb("VZ", [128, 16 * 128], BF16)
        gv = sb("gv", [128, 1024], BF16)
        SM = sb("SM", [128, 2 * 512], BF16)
        RD = sb("RD", [128, 4 * 1024], BF16)
        SF32a = sb("SF32a", [128, 1024], F32)
        SBF = sb("SBF", [128, 1024], BF16)
        DGM = sb("DGM", [128, 1024], BF16)
        HQ8 = sb("HQ8", [128, 32 * 512], FP8)
        wbf = sb("wbf", [128, 2 * 512], BF16)
        maskb = sb("maskb", [128, 2 * 1024], FP8)
        y2b = sb("y2b", [128, 1024], BF16)
        ydrop = sb("ydrop", [128, 1024], F32)
        tcomb = sb("tcomb", [128, 1024], F32)
        mu = sb("mu", [128, 2], F32)
        ssq = sb("ssq", [128, 2], F32)
        mcol = sb("mcol", [128, 2], F32)
        m2col = sb("m2col", [128, 2], F32)
        vcol = sb("vcol", [128, 2], F32)
        lvcol = sb("lvcol", [128, 2], F32)
        rstd = sb("rstd", [128, 8], F32)
        nmb = sb("nmb", [128, 8], F32)

        pAB = ctx.enter_context(nc.psum_tensor("pAB", [128, 1024], F32))
        pCD = ctx.enter_context(nc.psum_tensor("pCD", [128, 1024], F32))
        pEF = ctx.enter_context(nc.psum_tensor("pEF", [128, 1024], F32))
        pGH = ctx.enter_context(nc.psum_tensor("pGH", [128, 1024], F32))

        sems = {}
        for s in ("LX0", "LX1", "LM0", "LM1", "LP", "TX0", "TX1", "TX2",
                  "TQ", "TW", "TP", "GA", "ST", "PE", "DV", "AC", "PL"):
            sems[s] = ctx.enter_context(nc.semaphore(s))

        XNF3 = XNF[:].rearrange("p (l t) -> p l t", l=8)
        PREV3 = PREV[:].rearrange("p (l t) -> p l t", l=8)

        E("sync", lambda e: e.dma_start(out=rowA[0:1, 0:16], in_=dcap.rearrange("(a b) -> a b", a=1)),
          inc="LP", amt=16)
        E("sync", lambda e: e.dma_start(out=rowA[0:1, 32:48], in_=laap.rearrange("(a b) -> a b", a=1)),
          inc="LP", amt=16, event="ld_small")
        p.ev["ld_decay"] = p.ev["ld_small"]
        p.ev["ld_la"] = p.ev["ld_small"]

        E("gpsimd", lambda e: e.iota(IOT[:], pattern=[[1, 128]], base=0, channel_multiplier=-1,
                                     allow_small_or_imprecise_dtypes=True), inc="PL", event="iota1")
        E("gpsimd", lambda e: e.iota(JJ[:], pattern=[[1, 128]], base=0, channel_multiplier=0,
                                     allow_small_or_imprecise_dtypes=True), inc="PL", event="iota2")
        E("gpsimd", lambda e: e.iota(KK[:], pattern=[[0, 64]], base=127, channel_multiplier=-1,
                                     allow_small_or_imprecise_dtypes=True), inc="PL", event="iota3")
        E("gpsimd", lambda e: e.memset(PREV[:], 0.0), inc="PL", event="prev0")
        E("vector", lambda e: e.memset(SF32a[:], 0.0))
        E("vector", lambda e: e.memset(SBF[:], 0.0))
        E("vector", lambda e: e.memset(VZ[:], 0.0))
        E("vector", lambda e: e.memset(DGM[:], 0.0))
        E("vector", lambda e: e.memset(CONSTS[:, 0:1], 1e-5))
        E("vector", lambda e: e.memset(CONSTS[:, 1:2], -0.5))
        E("vector", lambda e: e.memset(CONSTS[:, 2:3], 128.0))
        E("vector", lambda e: e.memset(onesf[:], 1.0))
        E("vector", lambda e: e.memset(onesbf[:], 1.0))
        E("vector", lambda e: e.memset(ones11[:], 1.0), inc="DV", event="memsets")
        E("vector", lambda e: e.tensor_scalar(CAUS[:], IOT[:], 0.5, None, op0=ALU.is_ge),
          wait=["iota1"], inc="DV", event="caus")
        E("vector", lambda e: e.tensor_scalar(E1[:], IOT[:], -1.0, None, op0=ALU.add),
          wait=["iota1"], inc="DV", event="e1")

        E("scalar", lambda e: e.activation(rowA[0:1, 16:32], rowA[0:1, 0:16], AF.Sigmoid),
          wait=["ld_decay"], inc="AC", event="grow")
        E("scalar", lambda e: e.activation(rowA[0:1, 0:16], rowA[0:1, 16:32], AF.Ln),
          wait=["grow"], inc="AC", event="lgrow")
        E("scalar", lambda e: e.activation(rowA[0:1, 48:64], rowA[0:1, 32:48], AF.Exp),
          wait=["ld_la"], inc="AC", event="alrow")

        E("tensor", lambda e: e.matmul(pEF[0:128, 0:16], onesf[:], rowA[0:1, 0:16], start=True, stop=True),
          wait=["lgrow", "memsets"], inc="PE", event="mm_lg")
        E("tensor", lambda e: e.matmul(pGH[0:128, 0:16], onesf[:], rowA[0:1, 48:64], start=True, stop=True),
          wait=["alrow"], inc="PE", event="mm_al")
        E("vector", lambda e: e.tensor_copy(lgh[:], pEF[:, 0:16]), wait=["mm_lg"], inc="DV")
        E("vector", lambda e: e.tensor_copy(alh[:], pGH[:, 0:16]), wait=["mm_al"], inc="DV", event="bcast")
        E("scalar", lambda e: e.activation(gch[:], lgh[:], AF.Exp, bias=0.0, scale=CONSTS[:, 2:3]),
          wait=["bcast"], inc="AC", event="gch")

        def halves(dst, src, ev_w, ev_name):
            for g in range(NPAIR):
                E("vector", lambda e, g=g, dst=dst, src=src:
                  e.tensor_copy(dst[0:64, g:g + 1], src[0:64, 2 * g:2 * g + 1]),
                  wait=[ev_w] if g == 0 else [])
                E("vector", lambda e, g=g, dst=dst, src=src:
                  e.tensor_copy(dst[64:128, g:g + 1], src[64:128, 2 * g + 1:2 * g + 2]),
                  inc="DV" if g == NPAIR - 1 else None,
                  event=ev_name if g == NPAIR - 1 else None)
        for h in range(NHEAD):
            gpr, rh = h // 2, (h % 2) * 64
            E("vector", lambda e, gpr=gpr, rh=rh: e.memset(
                DGM[rh:rh + 64, gpr * 128 + rh: gpr * 128 + rh + 64], 1.0))
        halves(lgp, lgh, "bcast", "lgp")
        halves(alp, alh, "bcast", "alp")
        halves(gcp, gch, "gch", "gcp")

        for h in range(NHEAD):
            E("vector", lambda e, h=h: e.tensor_scalar(
                ydrop[:, 128:256], E1[:], lgh[:, h:h + 1], 0.0, op0=ALU.mult, op1=ALU.min),
              wait=(["e1", "bcast"] if h == 0 else [f"mme:{h - 1}"]),
              inc="DV", event=f"mmu:{h}")
            E("scalar", lambda e: e.activation(ydrop[:, 256:384], ydrop[:, 128:256], AF.Exp),
              wait=[f"mmu:{h}"], inc="AC", event=f"mme:{h}")
            E("vector", lambda e, h=h: e.tensor_tensor(
                MM[:, h * 128:(h + 1) * 128], ydrop[:, 256:384], CAUS[:], op=ALU.mult),
              wait=[f"mme:{h}", "caus"], inc="DV", event=f"mmk:{h}")
        for g in range(NPAIR):
            E("vector", lambda e, g=g: e.tensor_scalar(
                ydrop[:, 384:512], JJ[:], lgp[:, g:g + 1], 0.0, op0=ALU.mult, op1=ALU.min),
              wait=(["iota2", "lgp", "mme:15"] if g == 0 else [f"gpe:{g - 1}"]),
              inc="DV", event=f"gpu:{g}")
            E("scalar", lambda e, g=g: e.activation(
                GP[:, g * 128:(g + 1) * 128], ydrop[:, 384:512], AF.Exp),
              wait=[f"gpu:{g}"], inc="AC", event=f"gpe:{g}")
        for h in range(NHEAD):
            E("vector", lambda e, h=h: e.tensor_scalar(
                ydrop[:, 128:192], KK[:], lgh[:, h:h + 1], 0.0, op0=ALU.mult, op1=ALU.min),
              wait=(["iota3", "mme:15"] if h == 0 else [f"gwe:{h - 1}"]),
              inc="DV", event=f"gwu:{h}")
            E("scalar", lambda e, h=h: e.activation(
                GW[:, h * 64:(h + 1) * 64], ydrop[:, 128:192], AF.Exp),
              wait=[f"gwu:{h}"], inc="AC", event=f"gwe:{h}")

        wi = [0]
        def prep_w_bf16(dst, src_ap):
            for mt in range(8):
                i = wi[0]
                s = i % 2
                E("sync", lambda e, mt=mt, src_ap=src_ap, s=s: e.dma_start(
                    out=xA[:, s * 1024:(s + 1) * 1024], in_=src_ap[mt * 128:(mt + 1) * 128, :]),
                  wait=([f"wcast:{i - 2}"] if i > 1 else ["mm_al"]),
                  inc=f"LX{s}", amt=16, event=f"wld:{i}")
                E("vector", lambda e, s=s: e.tensor_copy(
                    xnbf[:, s * 1024:(s + 1) * 1024], xA[:, s * 1024:(s + 1) * 1024]),
                  wait=[f"wld:{i}", W(f"wtr:{i - 2}"), W(f"wtrb:{i - 2}")], inc="DV", event=f"wcast:{i}")
                for db in range(8):
                    eng = "scalar" if db < 4 else "sync"
                    sem = ("TP" if db < 4 else "TW") if s == 0 else ("TX0" if db < 4 else "TX1")
                    E(eng, lambda e, db=db, mt=mt, dst=dst, s=s: e.dma_start(
                        out=dst[:, db * 1024 + mt * 128: db * 1024 + (mt + 1) * 128],
                        in_=xnbf[:, s * 1024 + db * 128: s * 1024 + (db + 1) * 128], transpose=True),
                      wait=[f"wcast:{i}"] if db in (0, 4) else [],
                      inc=sem, amt=16,
                      event=(f"wtr:{i}" if db == 3 else (f"wtrb:{i}" if db == 7 else None)), dma=True)
                wi[0] += 1
        prep_w_bf16(WwT, wwap)
        prep_w_bf16(WrT, wrap_)
        nw = wi[0]
        for db in range(8):
            E("vector", lambda e, db=db: e.tensor_scalar(
                WrT[:, db * 1024:(db + 1) * 1024], WrT[:, db * 1024:(db + 1) * 1024],
                alp[:, db:db + 1], None, op0=ALU.mult),
              wait=[f"wtr:{nw - 1}", f"wtrb:{nw - 1}", f"wtr:{nw - 2}", f"wtrb:{nw - 2}", "alp"] if db == 0 else [],
              inc="DV" if db == 7 else None, event="wrfold" if db == 7 else None)

        qi = [0]
        def prep_w_fp8(src_slice, lane_dst3):
            i = qi[0]
            s = i % 2
            tdst = XQTb if s == 0 else WKT
            E("sync", lambda e, src_slice=src_slice, s=s: e.dma_start(
                out=xA[:, s * 1024:(s + 1) * 1024], in_=src_slice),
              wait=([f"qc1:{i - 2}"] if i > 1 else [f"wtr:{nw - 1}", f"wtrb:{nw - 1}", f"wtr:{nw - 2}", f"wtrb:{nw - 2}"]),
              inc=f"LX{s}", amt=16, event=f"qld:{i}")
            E("vector", lambda e, s=s: e.tensor_copy(
                maskb[:, s * 1024:(s + 1) * 1024], xA[:, s * 1024:(s + 1) * 1024]),
              wait=[f"qld:{i}"], inc="DV", event=f"qc1:{i}")
            E("vector", lambda e, s=s: e.tensor_copy(
                xnbf[:, s * 1024:(s + 1) * 1024], maskb[:, s * 1024:(s + 1) * 1024]),
              wait=[f"qc1:{i}", W(f"qtr:{i - 2}"), W(f"qtrb:{i - 2}")], inc="DV", event=f"qc2:{i}")
            for db in range(8):
                eng = "scalar" if db < 4 else "sync"
                sem = ("TP" if db < 4 else "TW") if s == 0 else ("TX0" if db < 4 else "TX1")
                E(eng, lambda e, db=db, s=s, tdst=tdst: e.dma_start(
                    out=tdst[:, db * 128:(db + 1) * 128],
                    in_=xnbf[:, s * 1024 + db * 128: s * 1024 + (db + 1) * 128], transpose=True),
                  wait=([f"qc2:{i}", W(f"q8t:{i - 2}")]) if db in (0, 4) else [],
                  inc=sem, amt=16,
                  event=(f"qtr:{i}" if db == 3 else (f"qtrb:{i}" if db == 7 else None)), dma=True)
            E("gpsimd", lambda e, lane_dst3=lane_dst3, tdst=tdst: e.tensor_copy(
                lane_dst3, tdst[:, 0:1024].rearrange("p (l t) -> p l t", l=8)),
              wait=[f"qtr:{i}", f"qtrb:{i}"], inc="PL", event=f"q8t:{i}")
            qi[0] += 1
        for nt in range(32):
            d3 = W1T8[:].rearrange("p (l n) -> p l n", l=8)[:, :, nt * 128:(nt + 1) * 128]
            prep_w_fp8(w1ap[nt * 128:(nt + 1) * 128, :], d3)
        for mt in range(8):
            for pc in range(4):
                d3 = W2T8[:].rearrange("p (l n) -> p l n", l=32)[:, pc * 8:(pc + 1) * 8, mt * 128:(mt + 1) * 128]
                prep_w_fp8(w2ap[mt * 128:(mt + 1) * 128, pc * 1024:(pc + 1) * 1024], d3)

        E("vector", lambda e: e.tensor_scalar(ydrop[:, 0:128], IOT[:], 0.0, None, op0=ALU.is_equal),
          wait=["iota1"], inc="DV")
        E("vector", lambda e: e.tensor_copy(idb32[:], ydrop[:, 0:128]), inc="DV", event="idb")
        E("sync", lambda e: e.dma_start(
            out=xA[0:32, 0:128], in_=b1ap.rearrange("(a b) -> a b", a=32)),
          wait=[f"q8t:{qi[0] - 1}", "idb"], inc="LP", amt=16, event="b1l")
        E("vector", lambda e: e.tensor_copy(xq8[0:32, 0:128], xA[0:32, 0:128]),
          wait=["b1l"], inc="DV")
        E("vector", lambda e: e.tensor_copy(xnbf[0:32, 0:128], xq8[0:32, 0:128]), inc="DV", event="b1q")
        E("tensor", lambda e: e.matmul(
            pEF[:, 0:32], xnbf[0:32, 0:128], idb32[0:32, 0:32], start=True, stop=True),
          wait=["b1q", "idb"], inc="PE", event="b1mm")
        E("vector", lambda e: e.tensor_copy(b1qf[:], pEF[:, 0:32]), wait=["b1mm"], inc="DV", event="b1done")
        E("sync", lambda e: e.dma_start(out=tcomb[0:1, 0:1024], in_=b2ap.rearrange("(a b) -> a b", a=1)),
          wait=["b1l"], inc="LP", amt=16, event="b2l")
        E("vector", lambda e: e.tensor_copy(HQ8[0:1, 0:1024], tcomb[0:1, 0:1024]),
          wait=["b2l", "b1q"], inc="DV")
        E("vector", lambda e: e.tensor_copy(RD[0:1, 0:1024], HQ8[0:1, 0:1024]), inc="DV", event="b2q")
        for hf in range(2):
            E("tensor", lambda e, hf=hf: e.matmul(
                pAB[:, hf * 512:(hf + 1) * 512], onesbf[:], RD[0:1, hf * 512:(hf + 1) * 512],
                start=True, stop=True),
              wait=["b2q", "b1mm"] if hf == 0 else [], inc="PE" if hf == 1 else None,
              event="b2mm" if hf == 1 else None)
        E("vector", lambda e: e.tensor_copy(b2row[:], pAB[:]), wait=["b2mm"], inc="DV", event="b2done")

        if dbg2:
            pdump = [("p_iot", IOT[:], F32), ("p_caus", CAUS[:], F32), ("p_ydtmp", ydrop[:, 0:128], F32),
                     ("p_tcomb", tcomb[0:1, 0:1024], F32), ("p_hqrow", HQ8[0:1, 0:1024], FP8),
                     ("p_rdrow", RD[0:1, 0:1024], BF16), ("p_ones", onesbf[0:1, 0:128], BF16),
                     ("p_idb", idb32[:], BF16), ("p_mm", MM[:], BF16), ("p_gw", GW[:], BF16),
                     ("p_gp", GP[:], BF16), ("p_gcp", gcp[:], F32), ("p_dgm", DGM[:], BF16),
                     ("p_b2row", b2row[:], FP8)]
            first = True
            for nm, src, dt_ in pdump:
                dst = nc.dram_tensor(nm, list(src.shape), dt_, kind="ExternalOutput").ap()
                E("gpsimd", lambda e, dst=dst, src=src: e.dma_start(out=dst, in_=src),
                  wait=([(s, p.cnt.get(s, 0)) for s in ("LP", "TP", "PE", "DV", "AC", "PL")] if first else []),
                  inc="ST", amt=16, dma=True)
                first = False
        E("sync", lambda e: e.nop())
        prepw = [(s, p.cnt.get(s, 0)) for s in ("LP", "TP", "TW", "TX0", "TX1", "LX0", "LX1", "PE", "DV", "AC", "PL", "ST")]
        for eng in Prog.ENGS:
            E(eng, lambda e: e.nop(), wait=prepw)

        for c in range(n_chunks):
            s2, s3, s4 = c % 2, c % 3, c % 4
            g, cc = c // GRP, c % GRP
            gs = g % 2

            E("sync", lambda e, c=c, s2=s2: e.dma_start(
                out=xA[:, s2 * 1024:(s2 + 1) * 1024], in_=xap[c * 128:(c + 1) * 128, :]),
              wait=[W(f"xn:{c - 2}")], inc=f"LX{s2}", amt=16, event=f"xld:{c}")

            E("vector", lambda e, s2=s2: e.tensor_reduce(
                mu[:, s2:s2 + 1], xA[:, s2 * 1024:(s2 + 1) * 1024], axis=AXX, op=ALU.add),
              wait=[f"xld:{c}"], inc="DV", event=f"mu:{c}")
            E("scalar", lambda e, s2=s2: e.activation(
                xnf[:, 0:1024], xA[:, s2 * 1024:(s2 + 1) * 1024], AF.Square,
                accum_out=ssq[:, s2:s2 + 1]),
              wait=[f"xld:{c}", W(f"xq8:{c - 1}"), W(f"xnbf:{c - 1}")], inc="AC", event=f"ssq:{c}")
            E("vector", lambda e, s2=s2: e.tensor_scalar(
                mcol[:, s2:s2 + 1], mu[:, s2:s2 + 1], 1.0 / 1024.0, None, op0=ALU.mult),
              wait=[f"ssq:{c}", f"mu:{c}"])
            E("vector", lambda e, s2=s2: e.tensor_tensor(
                m2col[:, s2:s2 + 1], mcol[:, s2:s2 + 1], mcol[:, s2:s2 + 1], op=ALU.mult))
            E("vector", lambda e, s2=s2: e.scalar_tensor_tensor(
                vcol[:, s2:s2 + 1], ssq[:, s2:s2 + 1], 1.0 / 1024.0, m2col[:, s2:s2 + 1],
                op0=ALU.mult, op1=ALU.subtract), inc="DV", event=f"var:{c}")
            E("scalar", lambda e, s2=s2: e.activation(
                lvcol[:, s2:s2 + 1], vcol[:, s2:s2 + 1], AF.Ln, bias=CONSTS[:, 0:1]),
              wait=[f"var:{c}"])
            E("scalar", lambda e, s2=s2, s8=c % 8: e.activation(
                rstd[:, s8:s8 + 1], lvcol[:, s2:s2 + 1], AF.Exp, bias=0.0, scale=CONSTS[:, 1:2]),
              inc="AC", event=f"rstd:{c}")
            E("vector", lambda e, s2=s2, s8=c % 8: e.scalar_tensor_tensor(
                nmb[:, s8:s8 + 1], mcol[:, s2:s2 + 1], -1.0, rstd[:, s8:s8 + 1],
                op0=ALU.mult, op1=ALU.mult),
              wait=[f"rstd:{c}"], inc="DV", event=f"nmb:{c}")
            E("vector", lambda e, s2=s2, s8=c % 8: e.tensor_scalar(
                xnf[:, 0:1024], xA[:, s2 * 1024:(s2 + 1) * 1024],
                rstd[:, s8:s8 + 1], nmb[:, s8:s8 + 1], op0=ALU.mult, op1=ALU.add),
              wait=[f"nmb:{c}", f"ssq:{c}"], inc="DV", event=f"xn:{c}")

            E("vector", lambda e, s4=s4: e.tensor_copy(
                xnbf[:, s4 * 1024:(s4 + 1) * 1024], xnf[:, 0:1024]),
              wait=[f"xn:{c}", W(f"out:{c - 4}"), W(f"xnT:{c - 4}")], inc="DV", event=f"xnbf:{c}")
            E("gpsimd", lambda e: e.tensor_copy(xq8[:, 0:1024], xnf[:, 0:1024]),
              wait=[f"xn:{c}"], inc="PL", event=f"xq8:{c}")
            E("gpsimd", lambda e: e.tensor_copy(xqbf[:, 0:1024], xq8[:, 0:1024]),
              wait=[W(f"xqT:{c - 1}")], inc="PL", event=f"xqbf:{c}")

            for db in range(8):
                E("scalar", lambda e, db=db, s3=s3, s4=s4: e.dma_start(
                    out=XNF[:, db * 384 + s3 * 128: db * 384 + (s3 + 1) * 128],
                    in_=xnbf[:, s4 * 1024 + db * 128: s4 * 1024 + (db + 1) * 128], transpose=True),
                  wait=([f"xnbf:{c}", W(f"sc:{c - 3}:3"), W(f"rkgp:{c - 3}")] if db == 0 else []),
                  inc=f"TX{s3}", amt=16, event=(f"xnT:{c}" if db == 7 else None), dma=True)
            for db in range(8):
                E("sync", lambda e, db=db: e.dma_start(
                    out=XQTb[:, db * 128:(db + 1) * 128],
                    in_=xqbf[:, db * 128:(db + 1) * 128], transpose=True),
                  wait=([f"xqbf:{c}", W(f"xqf8:{c - 1}")] if db == 0 else []),
                  inc="TQ", amt=16, event=(f"xqT:{c}" if db == 7 else None), dma=True)
            E("gpsimd", lambda e, s4=s4, gs=gs: e.tensor_copy(
                XQF8[:].rearrange("p (l t) -> p l t", l=8)[:, :, gs * 512 + s4 * 128: gs * 512 + (s4 + 1) * 128],
                XQTb[:, 0:1024].rearrange("p (l t) -> p l t", l=8)),
              wait=[f"xqT:{c}", W(f"ffn1:{g - 2}")], inc="PL", event=f"xqf8:{c}")

            E("gpsimd", lambda e, s3=s3: e.tensor_copy(
                WKF[:].rearrange("p (l t) -> p l t", l=8)[:, :, 1:128],
                XNF3[:, :, s3 * 128: s3 * 128 + 127]),
              wait=[f"xnT:{c}", W(f"sc:{c - 1}:3"), W(f"wkT:{c - 1}"), "prev0"], inc="PL")
            E("gpsimd", lambda e: e.tensor_copy(
                WKF[:].rearrange("p (l t) -> p l t", l=8)[:, :, 0:1], PREV3), inc="PL")
            E("gpsimd", lambda e, s3=s3: e.tensor_copy(
                PREV3, XNF3[:, :, s3 * 128 + 127: s3 * 128 + 128]),
              inc="PL", event=f"wkf:{c}")
            for db in range(8):
                E("sync", lambda e, db=db: e.dma_start(
                    out=WKT[:, db * 128:(db + 1) * 128],
                    in_=WKF[:, db * 128:(db + 1) * 128], transpose=True),
                  wait=([f"wkf:{c}", W(f"upd:{c - 1}:1")] if db == 0 else []),
                  inc="TW", amt=16, event=(f"wkT:{c}" if db == 7 else None), dma=True)

            E("vector", lambda e, s3=s3: e.tensor_tensor(
                rkgp[:].rearrange("p (l t) -> p l t", l=8),
                XNF3[:, :, s3 * 128:(s3 + 1) * 128],
                GP[:].rearrange("p (l t) -> p l t", l=8), op=ALU.mult),
              wait=[f"xnT:{c}", W(f"reads:{c - 1}")], inc="DV", event=f"rkgp:{c}")

            vw = [f"xnT:{c}", W(f"vevac:{c - 1}")]
            if cc == 0 and g > 0:
                vw.append(f"y2:{g - 1}:3")
            for db in range(8):
                for hf in range(2):
                    E("tensor", lambda e, db=db, hf=hf, s3=s3: e.matmul(
                        pAB[:, hf * 512:(hf + 1) * 512],
                        XNF[:, db * 384 + s3 * 128: db * 384 + (s3 + 1) * 128],
                        WwT[:, db * 1024 + hf * 512: db * 1024 + (hf + 1) * 512],
                        start=(db == 0), stop=(db == 7)),
                      wait=(vw if (db == 0 and hf == 0) else []),
                      inc=("PE" if (db == 7 and hf == 1) else None),
                      event=(f"vproj:{c}" if (db == 7 and hf == 1) else None))
            first = True
            for h in range(NHEAD):
                off = h * 128 + (h % 2) * 64
                E("vector", lambda e, h=h, off=off: e.tensor_copy(
                    VZ[:, off:off + 64], pAB[:, h * 64:(h + 1) * 64]),
                  wait=([f"vproj:{c}", W(f"upd:{c - 1}:1")] if first else []), inc="DV")
                first = False
            E("vector", lambda e: e.tensor_tensor(gv[:, 0:1024], pAB[:], GW[:], op=ALU.mult),
              inc="DV", event=f"vevac:{c}")
            if dbg2 and c >= 11 and c <= 15:
                dvz = nc.dram_tensor(f"dbg_vz{c}", [128, 2048], BF16, kind="ExternalOutput").ap()
                dgv = nc.dram_tensor(f"dbg_gv{c}", [128, 1024], BF16, kind="ExternalOutput").ap()
                dxn = nc.dram_tensor(f"dbg_xn{c}", [128, 3072], BF16, kind="ExternalOutput").ap()
                E("gpsimd", lambda e, dvz=dvz: e.dma_start(out=dvz, in_=VZ[:]),
                  wait=[f"vevac:{c}"], inc="GA", amt=16, dma=True)
                E("gpsimd", lambda e, dgv=dgv: e.dma_start(out=dgv, in_=gv[:]),
                  inc="GA", amt=16, dma=True)
                E("gpsimd", lambda e, dxn=dxn: e.dma_start(out=dxn, in_=XNF[:]),
                  inc="GA", amt=16, event=f"dbgd:{c}", dma=True)
                p.q["tensor"].append((lambda e, _w=p.ev[f"dbgd:{c}"]: e.wait_ge(sems[_w[0]], _w[1]), [], None, 1))

            SCB = [(pEF, 0), (pEF, 512), (pGH, 0), (pGH, 512)]

            if no_scan:
                continue

            def sc_wave(s):
                sw = [f"wkf:{c}", f"xnT:{c}"]
                if s == 0:
                    sw.append(W(f"sm:{c - 1}:3"))
                    sw.append(W(f"state:{c - 1}"))
                    if cc == 0 and g > 0:
                        sw.append(W(f"wev:{g - 1}:31"))
                else:
                    sw.append(f"sm:{c}:{s - 1}")
                for j in range(4):
                    h = s * 4 + j
                    db, rh = h // 2, (h % 2) * 64
                    pt, off = SCB[j]
                    E("tensor", lambda e, db=db, rh=rh, pt=pt, off=off, s3=s3: e.matmul(
                        pt[:, off:off + 128],
                        WKF[rh:rh + 64, db * 128:(db + 1) * 128],
                        XNF[rh:rh + 64, db * 384 + s3 * 128: db * 384 + (s3 + 1) * 128],
                        start=True, stop=True),
                      wait=(sw if j == 0 else []),
                      inc=("PE" if j == 3 else None), event=(f"sc:{c}:{s}" if j == 3 else None))

            def sm_evac(s):
                smw = [f"sc:{c}:{s}"]
                if s >= 2:
                    smw.append(W(f"iw:{c}:{s - 2}"))
                elif c > 0:
                    smw.append(W(f"iw:{c - 1}:{s + 2}"))
                if s == 0:
                    smw.append(W(f"reads:{c - 1}"))
                sl = (s % 2) * 512
                E("vector", lambda e, s=s, sl=sl: e.tensor_tensor(
                    SM[:, sl:sl + 256].rearrange("p (a b) -> p a b", a=2),
                    pEF[:].rearrange("p (a b) -> p a b", a=2)[:, :, 0:128],
                    MM[:, s * 512:s * 512 + 256].rearrange("p (a b) -> p a b", a=2),
                    op=ALU.mult),
                  wait=smw, inc="DV")
                E("vector", lambda e, s=s, sl=sl: e.tensor_tensor(
                    SM[:, sl + 256:sl + 512].rearrange("p (a b) -> p a b", a=2),
                    pGH[:].rearrange("p (a b) -> p a b", a=2)[:, :, 0:128],
                    MM[:, s * 512 + 256:(s + 1) * 512].rearrange("p (a b) -> p a b", a=2),
                    op=ALU.mult),
                  inc="DV", event=f"sm:{c}:{s}")

            def inter_pair(s):
                rw = [f"rkgp:{c}", W(f"state:{c - 1}"), f"vevac:{c}",
                      (f"rd:{c}:{s - 1}" if s > 0 else W(f"rd:{c - 1}:3"))]
                if s == 0 and c >= GRP and cc == 0:
                    rw.append(W(f"out:{c - 1}"))
                for q in range(2):
                    gpr = 2 * s + q
                    E("tensor", lambda e, gpr=gpr, q=q: e.matmul(
                        pCD[:, q * 512:q * 512 + 128],
                        SBF[:, gpr * 128:(gpr + 1) * 128],
                        rkgp[:, gpr * 128:(gpr + 1) * 128],
                        start=True, stop=False),
                      wait=(rw if q == 0 else []))

            def intra_wave(s, extra=()):
                for j in range(4):
                    h = s * 4 + j
                    q = j // 2
                    last = (j % 2) == 1
                    sl = (s % 2) * 512
                    E("tensor", lambda e, h=h, q=q, sl=sl, j=j, last=last: e.matmul(
                        pCD[:, q * 512:q * 512 + 128],
                        VZ[:, h * 128:(h + 1) * 128],
                        SM[:, sl + j * 128: sl + (j + 1) * 128],
                        start=False, stop=last),
                      wait=(list(extra) if j == 0 else []),
                      inc=("PE" if j == 3 else None), event=(f"iw:{c}:{s}" if j == 3 else None))

            def rd_evac(s):
                for q in range(2):
                    gpr = 2 * s + q
                    E("vector", lambda e, gpr=gpr, q=q, s4=s4: e.tensor_copy(
                        RD[:, s4 * 1024 + gpr * 128:(s4 * 1024) + (gpr + 1) * 128],
                        pCD[:, q * 512:q * 512 + 128]),
                      wait=([f"iw:{c}:{s}"] + ([W(f"rproj:{c - 4}")] if s == 0 else [])) if q == 0 else [],
                      inc="DV", event=(f"rd:{c}:{s}" if q == 1 else None))

            sc_wave(0)
            sm_evac(0)
            inter_pair(0)
            sc_wave(1)
            sm_evac(1)
            intra_wave(0, extra=[f"sm:{c}:0"])
            rd_evac(0)
            inter_pair(1)
            sc_wave(2)
            sm_evac(2)
            intra_wave(1, extra=[f"sm:{c}:1"])
            rd_evac(1)
            inter_pair(2)
            sc_wave(3)
            sm_evac(3)
            intra_wave(2, extra=[f"sm:{c}:2"])
            rd_evac(2)
            inter_pair(3)
            intra_wave(3, extra=[f"sm:{c}:3"])
            rd_evac(3)
            p.ev[f"reads:{c}"] = p.ev[f"iw:{c}:3"]
            p.ev[f"rd:{c}"] = p.ev[f"rd:{c}:3"]

            for u in range(2):
                uw = [f"wkT:{c}", f"vevac:{c}", f"sm:{c}:3"]
                if u == 0:
                    uw.append(W(f"state:{c - 1}"))
                else:
                    uw.append(f"stdec:{c}:0")
                for q in range(4):
                    gpr = 4 * u + q
                    pt, off = SCB[q]
                    E("tensor", lambda e, gpr=gpr, pt=pt, off=off: e.matmul(
                        pt[:, off:off + 128],
                        WKT[:, gpr * 128:(gpr + 1) * 128],
                        gv[:, gpr * 128:(gpr + 1) * 128],
                        start=True, stop=True),
                      wait=(uw if q == 0 else []),
                      inc=("PE" if q == 3 else None),
                      event=(f"upd:{c}:{u}" if q == 3 else None))
                for q in range(4):
                    gpr = 4 * u + q
                    E("vector", lambda e, gpr=gpr: e.tensor_scalar(
                        SF32a[:, gpr * 128:(gpr + 1) * 128], SF32a[:, gpr * 128:(gpr + 1) * 128],
                        gcp[:, gpr:gpr + 1], None, op0=ALU.mult),
                      wait=[f"upd:{c}:{u}", "gcp"] if q == 0 else [])
                b0 = 4 * u * 128
                E("vector", lambda e, b0=b0: e.tensor_tensor(
                    SF32a[:, b0:b0 + 256].rearrange("p (a b) -> p a b", a=2),
                    SF32a[:, b0:b0 + 256].rearrange("p (a b) -> p a b", a=2),
                    pEF[:].rearrange("p (a b) -> p a b", a=2)[:, :, 0:128],
                    op=ALU.add), inc="DV")
                E("vector", lambda e, b0=b0: e.tensor_tensor(
                    SF32a[:, b0 + 256:b0 + 512].rearrange("p (a b) -> p a b", a=2),
                    SF32a[:, b0 + 256:b0 + 512].rearrange("p (a b) -> p a b", a=2),
                    pGH[:].rearrange("p (a b) -> p a b", a=2)[:, :, 0:128],
                    op=ALU.add), inc="DV", event=f"stdec:{c}:{u}")
            E("vector", lambda e: e.tensor_tensor(SBF[:], SF32a[:], DGM[:], op=ALU.mult),
              inc="DV", event=f"state:{c}")

            if cc == GRP - 1 and not no_b:
                for m in range(32):
                    f1w = [f"xqf8:{c}", W(f"state:{c}")] if m < 2 else [f"xqf8:{c}"]
                    if m >= 2:
                        f1w.append(f"wev:{g}:{m - 2}")
                    elif g > 0:
                        f1w.append(f"wev:{g - 1}:{30 + m}")
                    for db in range(8):
                        E("tensor", lambda e, m=m, db=db, gs=gs: e.matmul(
                            pGH[:, (m % 2) * 512:(m % 2 + 1) * 512],
                            W1T8[:, db * 4096 + m * 128: db * 4096 + (m + 1) * 128],
                            XQF8[:, db * 1024 + gs * 512: db * 1024 + (gs + 1) * 512],
                            start=(db == 0), stop=(db == 7)),
                          wait=(f1w if db == 0 else []),
                          inc=("PE" if db == 7 else None),
                          event=(f"ffn1m:{g}:{m}" if db == 7 else None))
                    E("scalar", lambda e, m=m: e.activation(
                        wbf[:, (m % 2) * 512:(m % 2 + 1) * 512], pGH[:, (m % 2) * 512:(m % 2 + 1) * 512],
                        AF.Relu, bias=b1qf[:, m:m + 1]),
                      wait=[f"ffn1m:{g}:{m}", W(f"hq:{g}:{m - 2}")], inc="AC", event=f"wev:{g}:{m}")
                    hqw = [f"wev:{g}:{m}"]
                    if m == 0 and g >= 1:
                        hqw.append(f"ffn2:{g - 1}:3")
                    E("gpsimd", lambda e, m=m: e.tensor_copy(
                        HQ8[:, m * 512:(m + 1) * 512], wbf[:, (m % 2) * 512:(m % 2 + 1) * 512]),
                      wait=hqw, inc="PL", event=f"hq:{g}:{m}")
                p.ev[f"ffn1:{g}"] = p.ev[f"ffn1m:{g}:31"]

                for cc2 in range(GRP):
                    c2 = g * GRP + cc2
                    sB2, sB4 = c2 % 2, c2 % 4
                    E("sync", lambda e, c2=c2, sB2=sB2: e.dma_start(
                        out=maskb[:, sB2 * 1024:(sB2 + 1) * 1024], in_=mkap[c2 * 128:(c2 + 1) * 128, :]),
                      wait=[W(f"ydrop:{c2 - 2}")], inc=f"LM{sB2}", amt=16, event=f"mld:{c2}")
                    f2w = [f"hq:{g}:31"]
                    f2w.append(f"y2:{g}:{cc2 - 1}" if cc2 > 0 else f"vevac:{g * GRP + 3}")
                    for nb in range(32):
                        for hf in range(2):
                            E("tensor", lambda e, nb=nb, hf=hf, cc2=cc2: e.matmul(
                                pAB[:, hf * 512:(hf + 1) * 512],
                                HQ8[:, nb * 512 + cc2 * 128: nb * 512 + (cc2 + 1) * 128],
                                W2T8[:, nb * 1024 + hf * 512: nb * 1024 + (hf + 1) * 512],
                                start=(nb == 0), stop=(nb == 31)),
                              wait=(f2w if (nb == 0 and hf == 0) else []),
                              inc=("PE" if (nb == 31 and hf == 1) else None),
                              event=(f"ffn2:{g}:{cc2}" if (nb == 31 and hf == 1) else None))
                    E("vector", lambda e: e.tensor_tensor(y2b[:], pAB[:], b2row[:], op=ALU.add),
                      wait=[f"ffn2:{g}:{cc2}", "b2done"], inc="DV", event=f"y2:{g}:{cc2}")
                    E("vector", lambda e, sB2=sB2: e.tensor_tensor(
                        ydrop[:], y2b[:], maskb[:, sB2 * 1024:(sB2 + 1) * 1024], op=ALU.mult),
                      wait=[f"mld:{c2}"], inc="DV", event=f"ydrop:{c2}")
                    rpw = [f"rd:{c2}", "wrfold", f"y2:{g}:{cc2}"]
                    rpw.append(f"out:{c2 - 1}" if cc2 > 0 else f"rd:{g * GRP + 3}")
                    for jb in range(8):
                        for hf in range(2):
                            E("tensor", lambda e, jb=jb, hf=hf, sB4=sB4: e.matmul(
                                pCD[:, hf * 512:(hf + 1) * 512],
                                RD[:, sB4 * 1024 + jb * 128: sB4 * 1024 + (jb + 1) * 128],
                                WrT[:, jb * 1024 + hf * 512: jb * 1024 + (hf + 1) * 512],
                                start=(jb == 0), stop=(jb == 7)),
                              wait=(rpw if (jb == 0 and hf == 0) else []),
                              inc=("PE" if (jb == 7 and hf == 1) else None),
                              event=(f"rproj:{c2}" if (jb == 7 and hf == 1) else None))
                    E("vector", lambda e, sB4=sB4: e.scalar_tensor_tensor(
                        tcomb[:], pCD[:], 0.0, xnbf[:, sB4 * 1024:(sB4 + 1) * 1024],
                        op0=ALU.add, op1=ALU.add),
                      wait=[f"rproj:{c2}", f"ydrop:{c2}", f"xnbf:{c2}", W(f"st:{c2 - 1}")])
                    E("vector", lambda e: e.tensor_tensor(tcomb[:], tcomb[:], ydrop[:], op=ALU.add),
                      inc="DV", event=f"out:{c2}")
                    if not no_accum:
                        E("gpsimd", lambda e, c2=c2: e.dma_start(
                            out=tcomb[:], in_=xap[c2 * 128:(c2 + 1) * 128, :], accum_op=ALU.add),
                          wait=[f"out:{c2}"], inc="GA", amt=16, event=f"acc:{c2}", dma=True)
                    E("gpsimd", lambda e, c2=c2: e.dma_start(
                        out=yap[c2 * 128:(c2 + 1) * 128, :], in_=tcomb[:]),
                      wait=[f"acc:{c2}" if not no_accum else f"out:{c2}"],
                      inc="ST", amt=16, event=f"st:{c2}", dma=True)

        if dbg2:
            dbg_b1 = nc.dram_tensor("dbg_b1", [128, 32], F32, kind="ExternalOutput").ap()
            dbg_b2 = nc.dram_tensor("dbg_b2", [128, 1024], FP8, kind="ExternalOutput").ap()
            allw2 = [(s, p.cnt.get(s, 0)) for s in p.cnt]
            E("gpsimd", lambda e: e.dma_start(out=dbg_b1, in_=b1qf[:]), wait=allw2, inc="ST", amt=16, dma=True)
            E("gpsimd", lambda e: e.dma_start(out=dbg_b2, in_=b2row[:]), inc="ST", amt=16, dma=True)
        if dbg:
            dbg_rd = nc.dram_tensor("dbg_rd", [128, 4096], BF16, kind="ExternalOutput").ap()
            dbg_hq = nc.dram_tensor("dbg_hq", [128, 32 * 512], FP8, kind="ExternalOutput").ap()
            dbg_sbf = nc.dram_tensor("dbg_sbf", [128, 1024], BF16, kind="ExternalOutput").ap()
            dbg_vz = nc.dram_tensor("dbg_vz", [128, 2048], BF16, kind="ExternalOutput").ap()
            dbg_xq = nc.dram_tensor("dbg_xq", [128, 8192], FP8, kind="ExternalOutput").ap()
            allw = [(s, p.cnt.get(s, 0)) for s in p.cnt]
            E("gpsimd", lambda e: e.dma_start(out=dbg_rd, in_=RD[:]), wait=allw, inc="ST", amt=16, dma=True)
            E("gpsimd", lambda e: e.dma_start(out=dbg_hq, in_=HQ8[:]), inc="ST", amt=16, dma=True)
            E("gpsimd", lambda e: e.dma_start(out=dbg_sbf, in_=SBF[:]), inc="ST", amt=16, dma=True)
            E("gpsimd", lambda e: e.dma_start(out=dbg_vz, in_=VZ[:]), inc="ST", amt=16, dma=True)
            E("gpsimd", lambda e: e.dma_start(out=dbg_xq, in_=XQF8[:]), inc="ST", amt=16, dma=True)
        if "ST" in p.cnt:
            E("gpsimd", lambda e: e.wait_ge(sems["ST"], p.cnt["ST"]), dma=True)

        with nc.Block() as block:
            @block.sync
            def _(eng):
                p.replay(eng, "sync", sems)

            @block.tensor
            def _(eng):
                p.replay(eng, "tensor", sems)

            @block.vector
            def _(eng):
                p.replay(eng, "vector", sems)

            @block.scalar
            def _(eng):
                p.replay(eng, "scalar", sems)

            @block.gpsimd
            def _(eng):
                p.replay(eng, "gpsimd", sems)
    return nc


_CACHE = {}


def _bernoulli_keep(shape, k0=0, k1=42, p=0.8):
    rot1 = (13, 15, 26, 6)
    rot2 = (17, 29, 16, 24)
    ks = (np.uint32(k0), np.uint32(k1),
          np.uint32(k0) ^ np.uint32(k1) ^ np.uint32(0x1BD11BDA))
    n = int(np.prod(shape))
    x1 = np.arange(n, dtype=np.uint32)
    x0 = np.zeros(n, dtype=np.uint32)
    x0 = (x0 + ks[0]).astype(np.uint32)
    x1 = (x1 + ks[1]).astype(np.uint32)
    for d in range(5):
        for r in (rot1 if d % 2 == 0 else rot2):
            x0 = (x0 + x1).astype(np.uint32)
            x1 = ((x1 << np.uint32(r)) | (x1 >> np.uint32(32 - r))).astype(np.uint32)
            x1 = (x1 ^ x0).astype(np.uint32)
        x0 = (x0 + ks[(d + 1) % 3]).astype(np.uint32)
        x1 = (x1 + ks[(d + 2) % 3] + np.uint32(d + 1)).astype(np.uint32)
    bits = x0 ^ x1
    u = ((bits >> np.uint32(9)) | np.uint32(0x3F800000)).view(np.float32) - np.float32(1.0)
    return (u < np.float32(p)).reshape(shape)


def kernel(L0_attn_out, L0_attn_W_write, L0_attn_W_read, L0_attn_decay, L0_attn_log_alpha,
           L0_ffn_W1, L0_ffn_b1, L0_ffn_W2, L0_ffn_b2):
    x = np.ascontiguousarray(np.asarray(L0_attn_out, dtype=np.float32))
    B = x.shape[0]
    keep = _bernoulli_keep(x.shape)
    mask = np.where(keep, np.float32(1.25), np.float32(0.0)).astype(ml_dtypes.float8_e5m2)

    if "nc" not in _CACHE:
        _CACHE["nc"] = build_nc()
    nc = _CACHE["nc"]

    ww = np.ascontiguousarray(np.asarray(L0_attn_W_write, dtype=np.float32))
    wr = np.ascontiguousarray(np.asarray(L0_attn_W_read, dtype=np.float32))
    w1 = np.ascontiguousarray(np.asarray(L0_ffn_W1, dtype=np.float32))
    w2 = np.ascontiguousarray(np.asarray(L0_ffn_W2, dtype=np.float32))
    b1 = np.ascontiguousarray(np.asarray(L0_ffn_b1, dtype=np.float32))
    b2 = np.ascontiguousarray(np.asarray(L0_ffn_b2, dtype=np.float32))
    dc = np.ascontiguousarray(np.asarray(L0_attn_decay, dtype=np.float32))
    la = np.ascontiguousarray(np.asarray(L0_attn_log_alpha, dtype=np.float32))

    in_maps = []
    for b in range(B):
        in_maps.append({
            "x": np.ascontiguousarray(x[b]),
            "mask": np.ascontiguousarray(mask[b]),
            "Ww": ww, "Wr": wr, "W1": w1, "W2": w2,
            "b1": b1, "b2": b2, "decay": dc, "logalpha": la,
        })
    res = run_bass_kernel_spmd(nc, in_maps, core_ids=list(range(B)))
    out = np.stack([np.asarray(res.results[b]["y"]) for b in range(B)], axis=0)
    return out.astype(np.float32)


# version 12
# speedup vs baseline: 1.1620x; 1.0349x over previous
import numpy as np
import ml_dtypes
import concourse.bass as bass
import concourse.mybir as mybir
from concourse.bass_utils import run_bass_kernel_spmd
from contextlib import ExitStack

F32 = mybir.dt.float32
BF16 = mybir.dt.bfloat16
FP8 = mybir.dt.float8e5
AF = mybir.ActivationFunctionType
ALU = mybir.AluOpType
AXX = mybir.AxisListType.X

T, D = 8192, 1024
CH = 128
NCH = T // CH
GRP = 4
NHEAD, HD, NPAIR = 16, 64, 8


class Prog:
    ENGS = ("sync", "tensor", "vector", "scalar", "gpsimd")

    def __init__(self):
        self.q = {e: [] for e in self.ENGS}
        self.cnt = {}
        self.ev = {}

    SELF_SEM = {"vector": "DV", "scalar": "AC", "gpsimd": "PL"}

    def emit(self, eng, fn, wait=(), inc=None, amt=1, event=None, dma=False):
        waits = []
        for w in wait:
            if w is None:
                continue
            waits.append(self.ev[w] if isinstance(w, str) else w)
        if not dma and eng in self.SELF_SEM:
            ss = self.SELF_SEM[eng]
            assert inc in (None, ss), (eng, inc)
            waits.append((ss, self.cnt.get(ss, 0)))
            inc, amt = ss, 1
        self.q[eng].append((fn, waits, inc, amt))
        if inc is not None:
            self.cnt[inc] = self.cnt.get(inc, 0) + amt
            if event is not None:
                self.ev[event] = (inc, self.cnt[inc])
        else:
            assert event is None, event

    def replay(self, eng_obj, name, sems):
        waited = {}
        for fn, waits, inc, amt in self.q[name]:
            for s, v in waits:
                if waited.get(s, -1) >= v:
                    continue
                eng_obj.wait_ge(sems[s], v)
                waited[s] = v
            ins = fn(eng_obj)
            if inc is not None:
                ins.then_inc(sems[inc], amt)


def build_nc(n_chunks=NCH, no_b=False, no_accum=False, no_pe=False, no_attn=False, no_scan=False, dbg=False, dbg2=False):
    nc = bass.Bass("TRN2", target_bir_lowering=False, dynamic_dma_scratch_size=16384)
    xap = nc.dram_tensor("x", [T, D], F32, kind="ExternalInput").ap()
    mkap = nc.dram_tensor("mask", [T, D], FP8, kind="ExternalInput").ap()
    wwap = nc.dram_tensor("Ww", [D, D], F32, kind="ExternalInput").ap()
    wrap_ = nc.dram_tensor("Wr", [D, D], F32, kind="ExternalInput").ap()
    w1ap = nc.dram_tensor("W1", [4 * D, D], F32, kind="ExternalInput").ap()
    w2ap = nc.dram_tensor("W2", [D, 4 * D], F32, kind="ExternalInput").ap()
    b1ap = nc.dram_tensor("b1", [4 * D], F32, kind="ExternalInput").ap()
    b2ap = nc.dram_tensor("b2", [D], F32, kind="ExternalInput").ap()
    dcap = nc.dram_tensor("decay", [NHEAD], F32, kind="ExternalInput").ap()
    laap = nc.dram_tensor("logalpha", [NHEAD], F32, kind="ExternalInput").ap()
    yap = nc.dram_tensor("y", [T, D], F32, kind="ExternalOutput").ap()

    p = Prog()
    E = p.emit

    def W(name):
        return name if name in p.ev else None
    with ExitStack() as ctx:
        def sb(name, shape, dt):
            return ctx.enter_context(nc.sbuf_tensor(name, shape, dt))

        WwT = sb("WwT", [128, 8 * 1024], BF16)
        WrT = sb("WrT", [128, 8 * 1024], BF16)
        W1T8 = sb("W1T8", [128, 8 * 4096], FP8)
        W2T8 = sb("W2T8", [128, 32 * 1024], FP8)
        b1qf = sb("b1qf", [128, 32], F32)
        b2row = sb("b2row", [128, 1024], FP8)
        MM = sb("MM", [128, 16 * 128], BF16)
        GP = sb("GP", [128, 8 * 128], BF16)
        GW = sb("GW", [128, 16 * 64], BF16)
        CONSTS = sb("CONSTS", [128, 4], F32)
        gcp = sb("gcp", [128, 8], F32)
        xA = sb("xA", [128, 2 * 1024], F32)
        xnbf = sb("xnbf", [128, 4 * 1024], BF16)
        prep_ctx = ExitStack()

        def pb(name, shape, dt):
            return prep_ctx.enter_context(nc.sbuf_tensor(name, shape, dt))

        IOT = pb("IOT", [128, 128], F32)
        JJ = pb("JJ", [128, 128], F32)
        KK = pb("KK", [128, 64], F32)
        CAUS = pb("CAUS", [128, 128], F32)
        E1 = pb("E1", [128, 128], F32)
        idb32 = pb("idb32", [128, 128], BF16)
        onesf = pb("onesf", [1, 128], F32)
        onesbf = pb("onesbf", [1, 128], BF16)
        ones11 = pb("ones11", [1, 1], BF16)
        rowA = pb("rowA", [1, 64], F32)
        lgh = pb("lgh", [128, 16], F32)
        alh = pb("alh", [128, 16], F32)
        gch = pb("gch", [128, 16], F32)
        lgp = pb("lgp", [128, 8], F32)
        alp = pb("alp", [128, 8], F32)
        prep_ctx.close()
        xnf = sb("xnf", [128, 1024], F32)
        xq8 = sb("xq8", [128, 1024], FP8)
        xqbf = sb("xqbf", [128, 1024], BF16)
        XQTb = sb("XQTb", [128, 1024], BF16)
        XNF = sb("XNF", [128, 8 * 384], BF16)
        XQF8 = sb("XQF8", [128, 8 * 1024], FP8)
        WKF = sb("WKF", [128, 1024], BF16)
        WKT = sb("WKT", [128, 1024], BF16)
        PREV = sb("PREV", [128, 8], BF16)
        rkgp = sb("rkgp", [128, 1024], BF16)
        VZ = sb("VZ", [128, 16 * 128], BF16)
        gv = sb("gv", [128, 1024], BF16)
        SM = sb("SM", [128, 2 * 512], BF16)
        RD = sb("RD", [128, 4 * 1024], BF16)
        SF32a = sb("SF32a", [128, 1024], F32)
        SBF = sb("SBF", [128, 1024], BF16)
        DGM = sb("DGM", [128, 1024], BF16)
        HQ8 = sb("HQ8", [128, 32 * 512], FP8)
        wbf = sb("wbf", [128, 2 * 512], BF16)
        maskb = sb("maskb", [128, 2 * 1024], FP8)
        y2b = sb("y2b", [128, 1024], BF16)
        ydrop = sb("ydrop", [128, 1024], F32)
        tcomb = sb("tcomb", [128, 1024], F32)
        mu = sb("mu", [128, 2], F32)
        ssq = sb("ssq", [128, 2], F32)
        mcol = sb("mcol", [128, 2], F32)
        m2col = sb("m2col", [128, 2], F32)
        vcol = sb("vcol", [128, 2], F32)
        lvcol = sb("lvcol", [128, 2], F32)
        rstd = sb("rstd", [128, 8], F32)
        nmb = sb("nmb", [128, 8], F32)

        pAB = ctx.enter_context(nc.psum_tensor("pAB", [128, 1024], F32))
        pCD = ctx.enter_context(nc.psum_tensor("pCD", [128, 1024], F32))
        pEF = ctx.enter_context(nc.psum_tensor("pEF", [128, 1024], F32))
        pGH = ctx.enter_context(nc.psum_tensor("pGH", [128, 1024], F32))

        sems = {}
        for s in ("LX0", "LX1", "LM0", "LM1", "LP", "TX0", "TX1", "TX2",
                  "TQ", "TW", "TP", "GA", "ST", "PE", "DV", "AC", "PL"):
            sems[s] = ctx.enter_context(nc.semaphore(s))

        XNF3 = XNF[:].rearrange("p (l t) -> p l t", l=8)
        PREV3 = PREV[:].rearrange("p (l t) -> p l t", l=8)

        E("sync", lambda e: e.dma_start(out=rowA[0:1, 0:16], in_=dcap.rearrange("(a b) -> a b", a=1)),
          inc="LP", amt=16)
        E("sync", lambda e: e.dma_start(out=rowA[0:1, 32:48], in_=laap.rearrange("(a b) -> a b", a=1)),
          inc="LP", amt=16, event="ld_small")
        p.ev["ld_decay"] = p.ev["ld_small"]
        p.ev["ld_la"] = p.ev["ld_small"]

        E("gpsimd", lambda e: e.iota(IOT[:], pattern=[[1, 128]], base=0, channel_multiplier=-1,
                                     allow_small_or_imprecise_dtypes=True), inc="PL", event="iota1")
        E("gpsimd", lambda e: e.iota(JJ[:], pattern=[[1, 128]], base=0, channel_multiplier=0,
                                     allow_small_or_imprecise_dtypes=True), inc="PL", event="iota2")
        E("gpsimd", lambda e: e.iota(KK[:], pattern=[[0, 64]], base=127, channel_multiplier=-1,
                                     allow_small_or_imprecise_dtypes=True), inc="PL", event="iota3")
        E("gpsimd", lambda e: e.memset(PREV[:], 0.0), inc="PL", event="prev0")
        E("vector", lambda e: e.memset(SF32a[:], 0.0))
        E("vector", lambda e: e.memset(SBF[:], 0.0))
        E("vector", lambda e: e.memset(VZ[:], 0.0))
        E("vector", lambda e: e.memset(DGM[:], 0.0))
        E("vector", lambda e: e.memset(CONSTS[:, 0:1], 1e-5))
        E("vector", lambda e: e.memset(CONSTS[:, 1:2], -0.5))
        E("vector", lambda e: e.memset(CONSTS[:, 2:3], 128.0))
        E("vector", lambda e: e.memset(onesf[:], 1.0))
        E("vector", lambda e: e.memset(onesbf[:], 1.0))
        E("vector", lambda e: e.memset(ones11[:], 1.0), inc="DV", event="memsets")
        E("vector", lambda e: e.tensor_scalar(CAUS[:], IOT[:], 0.5, None, op0=ALU.is_ge),
          wait=["iota1"], inc="DV", event="caus")
        E("vector", lambda e: e.tensor_scalar(E1[:], IOT[:], -1.0, None, op0=ALU.add),
          wait=["iota1"], inc="DV", event="e1")

        E("scalar", lambda e: e.activation(rowA[0:1, 16:32], rowA[0:1, 0:16], AF.Sigmoid),
          wait=["ld_decay"], inc="AC", event="grow")
        E("scalar", lambda e: e.activation(rowA[0:1, 0:16], rowA[0:1, 16:32], AF.Ln),
          wait=["grow"], inc="AC", event="lgrow")
        E("scalar", lambda e: e.activation(rowA[0:1, 48:64], rowA[0:1, 32:48], AF.Exp),
          wait=["ld_la"], inc="AC", event="alrow")

        E("tensor", lambda e: e.matmul(pEF[0:128, 0:16], onesf[:], rowA[0:1, 0:16], start=True, stop=True),
          wait=["lgrow", "memsets"], inc="PE", event="mm_lg")
        E("tensor", lambda e: e.matmul(pGH[0:128, 0:16], onesf[:], rowA[0:1, 48:64], start=True, stop=True),
          wait=["alrow"], inc="PE", event="mm_al")
        E("vector", lambda e: e.tensor_copy(lgh[:], pEF[:, 0:16]), wait=["mm_lg"], inc="DV")
        E("vector", lambda e: e.tensor_copy(alh[:], pGH[:, 0:16]), wait=["mm_al"], inc="DV", event="bcast")
        E("scalar", lambda e: e.activation(gch[:], lgh[:], AF.Exp, bias=0.0, scale=CONSTS[:, 2:3]),
          wait=["bcast"], inc="AC", event="gch")

        def halves(dst, src, ev_w, ev_name):
            for g in range(NPAIR):
                E("vector", lambda e, g=g, dst=dst, src=src:
                  e.tensor_copy(dst[0:64, g:g + 1], src[0:64, 2 * g:2 * g + 1]),
                  wait=[ev_w] if g == 0 else [])
                E("vector", lambda e, g=g, dst=dst, src=src:
                  e.tensor_copy(dst[64:128, g:g + 1], src[64:128, 2 * g + 1:2 * g + 2]),
                  inc="DV" if g == NPAIR - 1 else None,
                  event=ev_name if g == NPAIR - 1 else None)
        for h in range(NHEAD):
            gpr, rh = h // 2, (h % 2) * 64
            E("vector", lambda e, gpr=gpr, rh=rh: e.memset(
                DGM[rh:rh + 64, gpr * 128 + rh: gpr * 128 + rh + 64], 1.0))
        halves(lgp, lgh, "bcast", "lgp")
        halves(alp, alh, "bcast", "alp")
        halves(gcp, gch, "gch", "gcp")

        for h in range(NHEAD):
            E("vector", lambda e, h=h: e.tensor_scalar(
                ydrop[:, 128:256], E1[:], lgh[:, h:h + 1], 0.0, op0=ALU.mult, op1=ALU.min),
              wait=(["e1", "bcast"] if h == 0 else [f"mme:{h - 1}"]),
              inc="DV", event=f"mmu:{h}")
            E("scalar", lambda e: e.activation(ydrop[:, 256:384], ydrop[:, 128:256], AF.Exp),
              wait=[f"mmu:{h}"], inc="AC", event=f"mme:{h}")
            E("vector", lambda e, h=h: e.tensor_tensor(
                MM[:, h * 128:(h + 1) * 128], ydrop[:, 256:384], CAUS[:], op=ALU.mult),
              wait=[f"mme:{h}", "caus"], inc="DV", event=f"mmk:{h}")
        for g in range(NPAIR):
            E("vector", lambda e, g=g: e.tensor_scalar(
                ydrop[:, 384:512], JJ[:], lgp[:, g:g + 1], 0.0, op0=ALU.mult, op1=ALU.min),
              wait=(["iota2", "lgp", "mme:15"] if g == 0 else [f"gpe:{g - 1}"]),
              inc="DV", event=f"gpu:{g}")
            E("scalar", lambda e, g=g: e.activation(
                GP[:, g * 128:(g + 1) * 128], ydrop[:, 384:512], AF.Exp),
              wait=[f"gpu:{g}"], inc="AC", event=f"gpe:{g}")
        for h in range(NHEAD):
            E("vector", lambda e, h=h: e.tensor_scalar(
                ydrop[:, 128:192], KK[:], lgh[:, h:h + 1], 0.0, op0=ALU.mult, op1=ALU.min),
              wait=(["iota3", "mme:15"] if h == 0 else [f"gwe:{h - 1}"]),
              inc="DV", event=f"gwu:{h}")
            E("scalar", lambda e, h=h: e.activation(
                GW[:, h * 64:(h + 1) * 64], ydrop[:, 128:192], AF.Exp),
              wait=[f"gwu:{h}"], inc="AC", event=f"gwe:{h}")

        wi = [0]
        def prep_w_bf16(dst, src_ap):
            for mt in range(8):
                i = wi[0]
                s = i % 2
                E("sync", lambda e, mt=mt, src_ap=src_ap, s=s: e.dma_start(
                    out=xA[:, s * 1024:(s + 1) * 1024], in_=src_ap[mt * 128:(mt + 1) * 128, :]),
                  wait=([f"wcast:{i - 2}"] if i > 1 else ["mm_al"]),
                  inc=f"LX{s}", amt=16, event=f"wld:{i}")
                E("vector", lambda e, s=s: e.tensor_copy(
                    xnbf[:, s * 1024:(s + 1) * 1024], xA[:, s * 1024:(s + 1) * 1024]),
                  wait=[f"wld:{i}", W(f"wtr:{i - 2}"), W(f"wtrb:{i - 2}")], inc="DV", event=f"wcast:{i}")
                for db in range(8):
                    eng = "scalar" if db < 4 else "sync"
                    sem = ("TP" if db < 4 else "TW") if s == 0 else ("TX0" if db < 4 else "TX1")
                    E(eng, lambda e, db=db, mt=mt, dst=dst, s=s: e.dma_start(
                        out=dst[:, db * 1024 + mt * 128: db * 1024 + (mt + 1) * 128],
                        in_=xnbf[:, s * 1024 + db * 128: s * 1024 + (db + 1) * 128], transpose=True),
                      wait=[f"wcast:{i}"] if db in (0, 4) else [],
                      inc=sem, amt=16,
                      event=(f"wtr:{i}" if db == 3 else (f"wtrb:{i}" if db == 7 else None)), dma=True)
                wi[0] += 1
        prep_w_bf16(WwT, wwap)
        prep_w_bf16(WrT, wrap_)
        nw = wi[0]
        for db in range(8):
            E("vector", lambda e, db=db: e.tensor_scalar(
                WrT[:, db * 1024:(db + 1) * 1024], WrT[:, db * 1024:(db + 1) * 1024],
                alp[:, db:db + 1], None, op0=ALU.mult),
              wait=[f"wtr:{nw - 1}", f"wtrb:{nw - 1}", f"wtr:{nw - 2}", f"wtrb:{nw - 2}", "alp"] if db == 0 else [],
              inc="DV" if db == 7 else None, event="wrfold" if db == 7 else None)

        qi = [0]
        def prep_w_fp8(src_slice, lane_dst3):
            i = qi[0]
            s = i % 2
            tdst = XQTb if s == 0 else WKT
            E("sync", lambda e, src_slice=src_slice, s=s: e.dma_start(
                out=xA[:, s * 1024:(s + 1) * 1024], in_=src_slice),
              wait=([f"qc1:{i - 2}"] if i > 1 else [f"wtr:{nw - 1}", f"wtrb:{nw - 1}", f"wtr:{nw - 2}", f"wtrb:{nw - 2}"]),
              inc=f"LX{s}", amt=16, event=f"qld:{i}")
            E("vector", lambda e, s=s: e.tensor_copy(
                maskb[:, s * 1024:(s + 1) * 1024], xA[:, s * 1024:(s + 1) * 1024]),
              wait=[f"qld:{i}"], inc="DV", event=f"qc1:{i}")
            E("vector", lambda e, s=s: e.tensor_copy(
                xnbf[:, s * 1024:(s + 1) * 1024], maskb[:, s * 1024:(s + 1) * 1024]),
              wait=[f"qc1:{i}", W(f"qtr:{i - 2}"), W(f"qtrb:{i - 2}")], inc="DV", event=f"qc2:{i}")
            for db in range(8):
                eng = "scalar" if db < 4 else "sync"
                sem = ("TP" if db < 4 else "TW") if s == 0 else ("TX0" if db < 4 else "TX1")
                E(eng, lambda e, db=db, s=s, tdst=tdst: e.dma_start(
                    out=tdst[:, db * 128:(db + 1) * 128],
                    in_=xnbf[:, s * 1024 + db * 128: s * 1024 + (db + 1) * 128], transpose=True),
                  wait=([f"qc2:{i}", W(f"q8t:{i - 2}")]) if db in (0, 4) else [],
                  inc=sem, amt=16,
                  event=(f"qtr:{i}" if db == 3 else (f"qtrb:{i}" if db == 7 else None)), dma=True)
            E("gpsimd", lambda e, lane_dst3=lane_dst3, tdst=tdst: e.tensor_copy(
                lane_dst3, tdst[:, 0:1024].rearrange("p (l t) -> p l t", l=8)),
              wait=[f"qtr:{i}", f"qtrb:{i}"], inc="PL", event=f"q8t:{i}")
            qi[0] += 1
        for nt in range(32):
            d3 = W1T8[:].rearrange("p (l n) -> p l n", l=8)[:, :, nt * 128:(nt + 1) * 128]
            prep_w_fp8(w1ap[nt * 128:(nt + 1) * 128, :], d3)
        for mt in range(8):
            for pc in range(4):
                d3 = W2T8[:].rearrange("p (l n) -> p l n", l=32)[:, pc * 8:(pc + 1) * 8, mt * 128:(mt + 1) * 128]
                prep_w_fp8(w2ap[mt * 128:(mt + 1) * 128, pc * 1024:(pc + 1) * 1024], d3)

        E("vector", lambda e: e.tensor_scalar(ydrop[:, 0:128], IOT[:], 0.0, None, op0=ALU.is_equal),
          wait=["iota1"], inc="DV")
        E("vector", lambda e: e.tensor_copy(idb32[:], ydrop[:, 0:128]), inc="DV", event="idb")
        E("sync", lambda e: e.dma_start(
            out=xA[0:32, 0:128], in_=b1ap.rearrange("(a b) -> a b", a=32)),
          wait=[f"q8t:{qi[0] - 1}", "idb"], inc="LP", amt=16, event="b1l")
        E("vector", lambda e: e.tensor_copy(xq8[0:32, 0:128], xA[0:32, 0:128]),
          wait=["b1l"], inc="DV")
        E("vector", lambda e: e.tensor_copy(xnbf[0:32, 0:128], xq8[0:32, 0:128]), inc="DV", event="b1q")
        E("tensor", lambda e: e.matmul(
            pEF[:, 0:32], xnbf[0:32, 0:128], idb32[0:32, 0:32], start=True, stop=True),
          wait=["b1q", "idb"], inc="PE", event="b1mm")
        E("vector", lambda e: e.tensor_copy(b1qf[:], pEF[:, 0:32]), wait=["b1mm"], inc="DV", event="b1done")
        E("sync", lambda e: e.dma_start(out=tcomb[0:1, 0:1024], in_=b2ap.rearrange("(a b) -> a b", a=1)),
          wait=["b1l"], inc="LP", amt=16, event="b2l")
        E("vector", lambda e: e.tensor_copy(HQ8[0:1, 0:1024], tcomb[0:1, 0:1024]),
          wait=["b2l", "b1q"], inc="DV")
        E("vector", lambda e: e.tensor_copy(RD[0:1, 0:1024], HQ8[0:1, 0:1024]), inc="DV", event="b2q")
        for hf in range(2):
            E("tensor", lambda e, hf=hf: e.matmul(
                pAB[:, hf * 512:(hf + 1) * 512], onesbf[:], RD[0:1, hf * 512:(hf + 1) * 512],
                start=True, stop=True),
              wait=["b2q", "b1mm"] if hf == 0 else [], inc="PE" if hf == 1 else None,
              event="b2mm" if hf == 1 else None)
        E("vector", lambda e: e.tensor_copy(b2row[:], pAB[:]), wait=["b2mm"], inc="DV", event="b2done")

        if dbg2:
            pdump = [("p_iot", IOT[:], F32), ("p_caus", CAUS[:], F32), ("p_ydtmp", ydrop[:, 0:128], F32),
                     ("p_tcomb", tcomb[0:1, 0:1024], F32), ("p_hqrow", HQ8[0:1, 0:1024], FP8),
                     ("p_rdrow", RD[0:1, 0:1024], BF16), ("p_ones", onesbf[0:1, 0:128], BF16),
                     ("p_idb", idb32[:], BF16), ("p_mm", MM[:], BF16), ("p_gw", GW[:], BF16),
                     ("p_gp", GP[:], BF16), ("p_gcp", gcp[:], F32), ("p_dgm", DGM[:], BF16),
                     ("p_b2row", b2row[:], FP8)]
            first = True
            for nm, src, dt_ in pdump:
                dst = nc.dram_tensor(nm, list(src.shape), dt_, kind="ExternalOutput").ap()
                E("gpsimd", lambda e, dst=dst, src=src: e.dma_start(out=dst, in_=src),
                  wait=([(s, p.cnt.get(s, 0)) for s in ("LP", "TP", "PE", "DV", "AC", "PL")] if first else []),
                  inc="ST", amt=16, dma=True)
                first = False
        E("sync", lambda e: e.nop())
        prepw = [(s, p.cnt.get(s, 0)) for s in ("LP", "TP", "TW", "TX0", "TX1", "LX0", "LX1", "PE", "DV", "AC", "PL", "ST")]
        for eng in Prog.ENGS:
            E(eng, lambda e: e.nop(), wait=prepw)

        for c in range(n_chunks):
            s2, s3, s4 = c % 2, c % 3, c % 4
            g, cc = c // GRP, c % GRP
            gs = g % 2

            E("sync", lambda e, c=c, s2=s2: e.dma_start(
                out=xA[:, s2 * 1024:(s2 + 1) * 1024], in_=xap[c * 128:(c + 1) * 128, :]),
              wait=[W(f"xn:{c - 2}")], inc=f"LX{s2}", amt=16, event=f"xld:{c}")

            E("vector", lambda e, s2=s2: e.tensor_reduce(
                mu[:, s2:s2 + 1], xA[:, s2 * 1024:(s2 + 1) * 1024], axis=AXX, op=ALU.add),
              wait=[f"xld:{c}"], inc="DV", event=f"mu:{c}")
            E("scalar", lambda e, s2=s2: e.activation(
                xnf[:, 0:1024], xA[:, s2 * 1024:(s2 + 1) * 1024], AF.Square,
                accum_out=ssq[:, s2:s2 + 1]),
              wait=[f"xld:{c}", W(f"xq8:{c - 1}"), W(f"xnbf:{c - 1}")], inc="AC", event=f"ssq:{c}")
            E("vector", lambda e, s2=s2: e.tensor_scalar(
                mcol[:, s2:s2 + 1], mu[:, s2:s2 + 1], 1.0 / 1024.0, None, op0=ALU.mult),
              wait=[f"ssq:{c}", f"mu:{c}"])
            E("vector", lambda e, s2=s2: e.tensor_tensor(
                m2col[:, s2:s2 + 1], mcol[:, s2:s2 + 1], mcol[:, s2:s2 + 1], op=ALU.mult))
            E("vector", lambda e, s2=s2: e.scalar_tensor_tensor(
                vcol[:, s2:s2 + 1], ssq[:, s2:s2 + 1], 1.0 / 1024.0, m2col[:, s2:s2 + 1],
                op0=ALU.mult, op1=ALU.subtract), inc="DV", event=f"var:{c}")
            E("scalar", lambda e, s2=s2: e.activation(
                lvcol[:, s2:s2 + 1], vcol[:, s2:s2 + 1], AF.Ln, bias=CONSTS[:, 0:1]),
              wait=[f"var:{c}"])
            E("scalar", lambda e, s2=s2, s8=c % 8: e.activation(
                rstd[:, s8:s8 + 1], lvcol[:, s2:s2 + 1], AF.Exp, bias=0.0, scale=CONSTS[:, 1:2]),
              inc="AC", event=f"rstd:{c}")
            E("vector", lambda e, s2=s2, s8=c % 8: e.scalar_tensor_tensor(
                nmb[:, s8:s8 + 1], mcol[:, s2:s2 + 1], -1.0, rstd[:, s8:s8 + 1],
                op0=ALU.mult, op1=ALU.mult),
              wait=[f"rstd:{c}"], inc="DV", event=f"nmb:{c}")
            E("vector", lambda e, s2=s2, s8=c % 8: e.tensor_scalar(
                xnf[:, 0:1024], xA[:, s2 * 1024:(s2 + 1) * 1024],
                rstd[:, s8:s8 + 1], nmb[:, s8:s8 + 1], op0=ALU.mult, op1=ALU.add),
              wait=[f"nmb:{c}", f"ssq:{c}"], inc="DV", event=f"xn:{c}")

            E("vector", lambda e, s4=s4: e.tensor_copy(
                xnbf[:, s4 * 1024:(s4 + 1) * 1024], xnf[:, 0:1024]),
              wait=[f"xn:{c}", W(f"out:{c - 4}"), W(f"xnT:{c - 4}")], inc="DV", event=f"xnbf:{c}")
            E("gpsimd", lambda e: e.tensor_copy(xq8[:, 0:1024], xnf[:, 0:1024]),
              wait=[f"xn:{c}"], inc="PL", event=f"xq8:{c}")
            E("gpsimd", lambda e: e.tensor_copy(xqbf[:, 0:1024], xq8[:, 0:1024]),
              wait=[W(f"xqT:{c - 1}")], inc="PL", event=f"xqbf:{c}")

            for db in range(8):
                E("scalar", lambda e, db=db, s3=s3, s4=s4: e.dma_start(
                    out=XNF[:, db * 384 + s3 * 128: db * 384 + (s3 + 1) * 128],
                    in_=xnbf[:, s4 * 1024 + db * 128: s4 * 1024 + (db + 1) * 128], transpose=True),
                  wait=([f"xnbf:{c}", W(f"sc:{c - 3}:3"), W(f"rkgp:{c - 3}")] if db == 0 else []),
                  inc=f"TX{s3}", amt=16, event=(f"xnT:{c}" if db == 7 else None), dma=True)
            for db in range(8):
                E("sync", lambda e, db=db: e.dma_start(
                    out=XQTb[:, db * 128:(db + 1) * 128],
                    in_=xqbf[:, db * 128:(db + 1) * 128], transpose=True),
                  wait=([f"xqbf:{c}", W(f"xqf8:{c - 1}")] if db == 0 else []),
                  inc="TQ", amt=16, event=(f"xqT:{c}" if db == 7 else None), dma=True)
            E("gpsimd", lambda e, s4=s4, gs=gs: e.tensor_copy(
                XQF8[:].rearrange("p (l t) -> p l t", l=8)[:, :, gs * 512 + s4 * 128: gs * 512 + (s4 + 1) * 128],
                XQTb[:, 0:1024].rearrange("p (l t) -> p l t", l=8)),
              wait=[f"xqT:{c}", W(f"ffn1:{g - 2}")], inc="PL", event=f"xqf8:{c}")

            E("gpsimd", lambda e, s3=s3: e.tensor_copy(
                WKF[:].rearrange("p (l t) -> p l t", l=8)[:, :, 1:128],
                XNF3[:, :, s3 * 128: s3 * 128 + 127]),
              wait=[f"xnT:{c}", W(f"sc:{c - 1}:3"), W(f"wkT:{c - 1}"), "prev0"], inc="PL")
            E("gpsimd", lambda e: e.tensor_copy(
                WKF[:].rearrange("p (l t) -> p l t", l=8)[:, :, 0:1], PREV3), inc="PL")
            E("gpsimd", lambda e, s3=s3: e.tensor_copy(
                PREV3, XNF3[:, :, s3 * 128 + 127: s3 * 128 + 128]),
              inc="PL", event=f"wkf:{c}")
            for db in range(8):
                E("sync", lambda e, db=db: e.dma_start(
                    out=WKT[:, db * 128:(db + 1) * 128],
                    in_=WKF[:, db * 128:(db + 1) * 128], transpose=True),
                  wait=([f"wkf:{c}", W(f"upd:{c - 1}:1")] if db == 0 else []),
                  inc="TW", amt=16, event=(f"wkT:{c}" if db == 7 else None), dma=True)

            E("vector", lambda e, s3=s3: e.tensor_tensor(
                rkgp[:].rearrange("p (l t) -> p l t", l=8),
                XNF3[:, :, s3 * 128:(s3 + 1) * 128],
                GP[:].rearrange("p (l t) -> p l t", l=8), op=ALU.mult),
              wait=[f"xnT:{c}", W(f"reads:{c - 1}")], inc="DV", event=f"rkgp:{c}")

            vw = [f"xnT:{c}", W(f"vevac:{c - 1}")]
            if cc == 0 and g > 0:
                vw.append(f"y2:{g - 1}:3")
            for db in range(8):
                for hf in range(2):
                    E("tensor", lambda e, db=db, hf=hf, s3=s3: e.matmul(
                        pAB[:, hf * 512:(hf + 1) * 512],
                        XNF[:, db * 384 + s3 * 128: db * 384 + (s3 + 1) * 128],
                        WwT[:, db * 1024 + hf * 512: db * 1024 + (hf + 1) * 512],
                        start=(db == 0), stop=(db == 7)),
                      wait=(vw if (db == 0 and hf == 0) else []),
                      inc=("PE" if (db == 7 and hf == 1) else None),
                      event=(f"vproj:{c}" if (db == 7 and hf == 1) else None))
            pV3 = pAB[:].rearrange("p (a b) -> p a b", a=16)
            VZ3v = VZ[:].rearrange("p (a b) -> p a b", a=16)
            E("vector", lambda e, pV3=pV3, VZ3v=VZ3v: e.tensor_copy(
                VZ3v[:, 0::2, 0:64], pV3[:, 0::2, :]),
              wait=[f"vproj:{c}", W(f"upd:{c - 1}:1")], inc="DV")
            E("vector", lambda e, pV3=pV3, VZ3v=VZ3v: e.tensor_copy(
                VZ3v[:, 1::2, 64:128], pV3[:, 1::2, :]),
              inc="DV")
            E("vector", lambda e: e.tensor_tensor(gv[:, 0:1024], pAB[:], GW[:], op=ALU.mult),
              inc="DV", event=f"vevac:{c}")
            if dbg2 and c >= 11 and c <= 15:
                dvz = nc.dram_tensor(f"dbg_vz{c}", [128, 2048], BF16, kind="ExternalOutput").ap()
                dgv = nc.dram_tensor(f"dbg_gv{c}", [128, 1024], BF16, kind="ExternalOutput").ap()
                dxn = nc.dram_tensor(f"dbg_xn{c}", [128, 3072], BF16, kind="ExternalOutput").ap()
                E("gpsimd", lambda e, dvz=dvz: e.dma_start(out=dvz, in_=VZ[:]),
                  wait=[f"vevac:{c}"], inc="GA", amt=16, dma=True)
                E("gpsimd", lambda e, dgv=dgv: e.dma_start(out=dgv, in_=gv[:]),
                  inc="GA", amt=16, dma=True)
                E("gpsimd", lambda e, dxn=dxn: e.dma_start(out=dxn, in_=XNF[:]),
                  inc="GA", amt=16, event=f"dbgd:{c}", dma=True)
                p.q["tensor"].append((lambda e, _w=p.ev[f"dbgd:{c}"]: e.wait_ge(sems[_w[0]], _w[1]), [], None, 1))

            SCB = [(pEF, 0), (pEF, 512), (pGH, 0), (pGH, 512)]

            if no_scan:
                continue

            def sc_wave(s):
                sw = [f"wkf:{c}", f"xnT:{c}"]
                if s == 0:
                    sw.append(W(f"sm:{c - 1}:3"))
                    sw.append(W(f"state:{c - 1}"))
                    if cc == 0 and g > 0:
                        sw.append(W(f"wev:{g - 1}:31"))
                else:
                    sw.append(f"sm:{c}:{s - 1}")
                for j in range(4):
                    h = s * 4 + j
                    db, rh = h // 2, (h % 2) * 64
                    pt, off = SCB[j]
                    E("tensor", lambda e, db=db, rh=rh, pt=pt, off=off, s3=s3: e.matmul(
                        pt[:, off:off + 128],
                        WKF[rh:rh + 64, db * 128:(db + 1) * 128],
                        XNF[rh:rh + 64, db * 384 + s3 * 128: db * 384 + (s3 + 1) * 128],
                        start=True, stop=True),
                      wait=(sw if j == 0 else []),
                      inc=("PE" if j == 3 else None), event=(f"sc:{c}:{s}" if j == 3 else None))

            def sm_evac(s):
                smw = [f"sc:{c}:{s}"]
                if s >= 2:
                    smw.append(W(f"iw:{c}:{s - 2}"))
                elif c > 0:
                    smw.append(W(f"iw:{c - 1}:{s + 2}"))
                if s == 0:
                    smw.append(W(f"reads:{c - 1}"))
                sl = (s % 2) * 512
                E("vector", lambda e, s=s, sl=sl: e.tensor_tensor(
                    SM[:, sl:sl + 256].rearrange("p (a b) -> p a b", a=2),
                    pEF[:].rearrange("p (a b) -> p a b", a=2)[:, :, 0:128],
                    MM[:, s * 512:s * 512 + 256].rearrange("p (a b) -> p a b", a=2),
                    op=ALU.mult),
                  wait=smw, inc="DV")
                E("vector", lambda e, s=s, sl=sl: e.tensor_tensor(
                    SM[:, sl + 256:sl + 512].rearrange("p (a b) -> p a b", a=2),
                    pGH[:].rearrange("p (a b) -> p a b", a=2)[:, :, 0:128],
                    MM[:, s * 512 + 256:(s + 1) * 512].rearrange("p (a b) -> p a b", a=2),
                    op=ALU.mult),
                  inc="DV", event=f"sm:{c}:{s}")

            def inter_pair(s):
                rw = [f"rkgp:{c}", W(f"state:{c - 1}"), f"vevac:{c}",
                      (f"rd:{c}:{s - 1}" if s > 0 else W(f"rd:{c - 1}:3"))]
                if s == 0 and c >= GRP and cc == 0:
                    rw.append(W(f"out:{c - 1}"))
                for q in range(2):
                    gpr = 2 * s + q
                    E("tensor", lambda e, gpr=gpr, q=q: e.matmul(
                        pCD[:, q * 512:q * 512 + 128],
                        SBF[:, gpr * 128:(gpr + 1) * 128],
                        rkgp[:, gpr * 128:(gpr + 1) * 128],
                        start=True, stop=False),
                      wait=(rw if q == 0 else []))

            def intra_wave(s, extra=()):
                for j in range(4):
                    h = s * 4 + j
                    q = j // 2
                    last = (j % 2) == 1
                    sl = (s % 2) * 512
                    E("tensor", lambda e, h=h, q=q, sl=sl, j=j, last=last: e.matmul(
                        pCD[:, q * 512:q * 512 + 128],
                        VZ[:, h * 128:(h + 1) * 128],
                        SM[:, sl + j * 128: sl + (j + 1) * 128],
                        start=False, stop=last),
                      wait=(list(extra) if j == 0 else []),
                      inc=("PE" if j == 3 else None), event=(f"iw:{c}:{s}" if j == 3 else None))

            def rd_evac(s):
                for q in range(2):
                    gpr = 2 * s + q
                    E("vector", lambda e, gpr=gpr, q=q, s4=s4: e.tensor_copy(
                        RD[:, s4 * 1024 + gpr * 128:(s4 * 1024) + (gpr + 1) * 128],
                        pCD[:, q * 512:q * 512 + 128]),
                      wait=([f"iw:{c}:{s}"] + ([W(f"rproj:{c - 4}")] if s == 0 else [])) if q == 0 else [],
                      inc="DV", event=(f"rd:{c}:{s}" if q == 1 else None))

            sc_wave(0)
            sm_evac(0)
            inter_pair(0)
            sc_wave(1)
            sm_evac(1)
            intra_wave(0, extra=[f"sm:{c}:0"])
            rd_evac(0)
            inter_pair(1)
            sc_wave(2)
            sm_evac(2)
            intra_wave(1, extra=[f"sm:{c}:1"])
            rd_evac(1)
            inter_pair(2)
            sc_wave(3)
            sm_evac(3)
            intra_wave(2, extra=[f"sm:{c}:2"])
            rd_evac(2)
            inter_pair(3)
            intra_wave(3, extra=[f"sm:{c}:3"])
            rd_evac(3)
            p.ev[f"reads:{c}"] = p.ev[f"iw:{c}:3"]
            p.ev[f"rd:{c}"] = p.ev[f"rd:{c}:3"]

            for u in range(2):
                uw = [f"wkT:{c}", f"vevac:{c}", f"sm:{c}:3"]
                if u == 0:
                    uw.append(W(f"state:{c - 1}"))
                else:
                    uw.append(f"stdec:{c}:0")
                for q in range(4):
                    gpr = 4 * u + q
                    pt, off = SCB[q]
                    E("tensor", lambda e, gpr=gpr, pt=pt, off=off: e.matmul(
                        pt[:, off:off + 128],
                        WKT[:, gpr * 128:(gpr + 1) * 128],
                        gv[:, gpr * 128:(gpr + 1) * 128],
                        start=True, stop=True),
                      wait=(uw if q == 0 else []),
                      inc=("PE" if q == 3 else None),
                      event=(f"upd:{c}:{u}" if q == 3 else None))
                for q in range(4):
                    gpr = 4 * u + q
                    E("vector", lambda e, gpr=gpr: e.tensor_scalar(
                        SF32a[:, gpr * 128:(gpr + 1) * 128], SF32a[:, gpr * 128:(gpr + 1) * 128],
                        gcp[:, gpr:gpr + 1], None, op0=ALU.mult),
                      wait=[f"upd:{c}:{u}", "gcp"] if q == 0 else [])
                b0 = 4 * u * 128
                E("vector", lambda e, b0=b0: e.tensor_tensor(
                    SF32a[:, b0:b0 + 256].rearrange("p (a b) -> p a b", a=2),
                    SF32a[:, b0:b0 + 256].rearrange("p (a b) -> p a b", a=2),
                    pEF[:].rearrange("p (a b) -> p a b", a=2)[:, :, 0:128],
                    op=ALU.add), inc="DV")
                E("vector", lambda e, b0=b0: e.tensor_tensor(
                    SF32a[:, b0 + 256:b0 + 512].rearrange("p (a b) -> p a b", a=2),
                    SF32a[:, b0 + 256:b0 + 512].rearrange("p (a b) -> p a b", a=2),
                    pGH[:].rearrange("p (a b) -> p a b", a=2)[:, :, 0:128],
                    op=ALU.add), inc="DV", event=f"stdec:{c}:{u}")
            E("vector", lambda e: e.tensor_tensor(SBF[:], SF32a[:], DGM[:], op=ALU.mult),
              inc="DV", event=f"state:{c}")

            if cc == GRP - 1 and not no_b:
                for m in range(32):
                    f1w = [f"xqf8:{c}", W(f"state:{c}")] if m < 2 else [f"xqf8:{c}"]
                    if m >= 2:
                        f1w.append(f"wev:{g}:{m - 2}")
                    elif g > 0:
                        f1w.append(f"wev:{g - 1}:{30 + m}")
                    for db in range(8):
                        E("tensor", lambda e, m=m, db=db, gs=gs: e.matmul(
                            pGH[:, (m % 2) * 512:(m % 2 + 1) * 512],
                            W1T8[:, db * 4096 + m * 128: db * 4096 + (m + 1) * 128],
                            XQF8[:, db * 1024 + gs * 512: db * 1024 + (gs + 1) * 512],
                            start=(db == 0), stop=(db == 7)),
                          wait=(f1w if db == 0 else []),
                          inc=("PE" if db == 7 else None),
                          event=(f"ffn1m:{g}:{m}" if db == 7 else None))
                    E("scalar", lambda e, m=m: e.activation(
                        wbf[:, (m % 2) * 512:(m % 2 + 1) * 512], pGH[:, (m % 2) * 512:(m % 2 + 1) * 512],
                        AF.Relu, bias=b1qf[:, m:m + 1]),
                      wait=[f"ffn1m:{g}:{m}", W(f"hq:{g}:{m - 2}")], inc="AC", event=f"wev:{g}:{m}")
                    hqw = [f"wev:{g}:{m}"]
                    if m == 0 and g >= 1:
                        hqw.append(f"ffn2:{g - 1}:3")
                    E("scalar", lambda e, m=m: e.activation(
                        HQ8[:, m * 512:(m + 1) * 512], wbf[:, (m % 2) * 512:(m % 2 + 1) * 512],
                        AF.Copy),
                      wait=hqw, inc="AC", event=f"hq:{g}:{m}")
                p.ev[f"ffn1:{g}"] = p.ev[f"ffn1m:{g}:31"]

                for cc2 in range(GRP):
                    c2 = g * GRP + cc2
                    sB2, sB4 = c2 % 2, c2 % 4
                    E("sync", lambda e, c2=c2, sB2=sB2: e.dma_start(
                        out=maskb[:, sB2 * 1024:(sB2 + 1) * 1024], in_=mkap[c2 * 128:(c2 + 1) * 128, :]),
                      wait=[W(f"ydrop:{c2 - 2}")], inc=f"LM{sB2}", amt=16, event=f"mld:{c2}")
                    f2w = [f"hq:{g}:31"]
                    f2w.append(f"y2:{g}:{cc2 - 1}" if cc2 > 0 else f"vevac:{g * GRP + 3}")
                    for nb in range(32):
                        for hf in range(2):
                            E("tensor", lambda e, nb=nb, hf=hf, cc2=cc2: e.matmul(
                                pAB[:, hf * 512:(hf + 1) * 512],
                                HQ8[:, nb * 512 + cc2 * 128: nb * 512 + (cc2 + 1) * 128],
                                W2T8[:, nb * 1024 + hf * 512: nb * 1024 + (hf + 1) * 512],
                                start=(nb == 0), stop=(nb == 31)),
                              wait=(f2w if (nb == 0 and hf == 0) else []),
                              inc=("PE" if (nb == 31 and hf == 1) else None),
                              event=(f"ffn2:{g}:{cc2}" if (nb == 31 and hf == 1) else None))
                    E("vector", lambda e: e.tensor_tensor(y2b[:], pAB[:], b2row[:], op=ALU.add),
                      wait=[f"ffn2:{g}:{cc2}", "b2done"], inc="DV", event=f"y2:{g}:{cc2}")
                    E("vector", lambda e, sB2=sB2: e.tensor_tensor(
                        ydrop[:], y2b[:], maskb[:, sB2 * 1024:(sB2 + 1) * 1024], op=ALU.mult),
                      wait=[f"mld:{c2}"], inc="DV", event=f"ydrop:{c2}")
                    rpw = [f"rd:{c2}", "wrfold", f"y2:{g}:{cc2}"]
                    rpw.append(f"out:{c2 - 1}" if cc2 > 0 else f"rd:{g * GRP + 3}")
                    for jb in range(8):
                        for hf in range(2):
                            E("tensor", lambda e, jb=jb, hf=hf, sB4=sB4: e.matmul(
                                pCD[:, hf * 512:(hf + 1) * 512],
                                RD[:, sB4 * 1024 + jb * 128: sB4 * 1024 + (jb + 1) * 128],
                                WrT[:, jb * 1024 + hf * 512: jb * 1024 + (hf + 1) * 512],
                                start=(jb == 0), stop=(jb == 7)),
                              wait=(rpw if (jb == 0 and hf == 0) else []),
                              inc=("PE" if (jb == 7 and hf == 1) else None),
                              event=(f"rproj:{c2}" if (jb == 7 and hf == 1) else None))
                    E("vector", lambda e, sB4=sB4: e.scalar_tensor_tensor(
                        tcomb[:], pCD[:], 0.0, xnbf[:, sB4 * 1024:(sB4 + 1) * 1024],
                        op0=ALU.add, op1=ALU.add),
                      wait=[f"rproj:{c2}", f"ydrop:{c2}", f"xnbf:{c2}", W(f"st:{c2 - 1}")])
                    E("vector", lambda e: e.tensor_tensor(tcomb[:], tcomb[:], ydrop[:], op=ALU.add),
                      inc="DV", event=f"out:{c2}")
                    if not no_accum:
                        E("gpsimd", lambda e, c2=c2: e.dma_start(
                            out=tcomb[:], in_=xap[c2 * 128:(c2 + 1) * 128, :], accum_op=ALU.add),
                          wait=[f"out:{c2}"], inc="GA", amt=16, event=f"acc:{c2}", dma=True)
                    E("gpsimd", lambda e, c2=c2: e.dma_start(
                        out=yap[c2 * 128:(c2 + 1) * 128, :], in_=tcomb[:]),
                      wait=[f"acc:{c2}" if not no_accum else f"out:{c2}"],
                      inc="ST", amt=16, event=f"st:{c2}", dma=True)

        if dbg2:
            dbg_b1 = nc.dram_tensor("dbg_b1", [128, 32], F32, kind="ExternalOutput").ap()
            dbg_b2 = nc.dram_tensor("dbg_b2", [128, 1024], FP8, kind="ExternalOutput").ap()
            allw2 = [(s, p.cnt.get(s, 0)) for s in p.cnt]
            E("gpsimd", lambda e: e.dma_start(out=dbg_b1, in_=b1qf[:]), wait=allw2, inc="ST", amt=16, dma=True)
            E("gpsimd", lambda e: e.dma_start(out=dbg_b2, in_=b2row[:]), inc="ST", amt=16, dma=True)
        if dbg:
            dbg_rd = nc.dram_tensor("dbg_rd", [128, 4096], BF16, kind="ExternalOutput").ap()
            dbg_hq = nc.dram_tensor("dbg_hq", [128, 32 * 512], FP8, kind="ExternalOutput").ap()
            dbg_sbf = nc.dram_tensor("dbg_sbf", [128, 1024], BF16, kind="ExternalOutput").ap()
            dbg_vz = nc.dram_tensor("dbg_vz", [128, 2048], BF16, kind="ExternalOutput").ap()
            dbg_xq = nc.dram_tensor("dbg_xq", [128, 8192], FP8, kind="ExternalOutput").ap()
            allw = [(s, p.cnt.get(s, 0)) for s in p.cnt]
            E("gpsimd", lambda e: e.dma_start(out=dbg_rd, in_=RD[:]), wait=allw, inc="ST", amt=16, dma=True)
            E("gpsimd", lambda e: e.dma_start(out=dbg_hq, in_=HQ8[:]), inc="ST", amt=16, dma=True)
            E("gpsimd", lambda e: e.dma_start(out=dbg_sbf, in_=SBF[:]), inc="ST", amt=16, dma=True)
            E("gpsimd", lambda e: e.dma_start(out=dbg_vz, in_=VZ[:]), inc="ST", amt=16, dma=True)
            E("gpsimd", lambda e: e.dma_start(out=dbg_xq, in_=XQF8[:]), inc="ST", amt=16, dma=True)
        if "ST" in p.cnt:
            E("gpsimd", lambda e: e.wait_ge(sems["ST"], p.cnt["ST"]), dma=True)

        with nc.Block() as block:
            @block.sync
            def _(eng):
                p.replay(eng, "sync", sems)

            @block.tensor
            def _(eng):
                p.replay(eng, "tensor", sems)

            @block.vector
            def _(eng):
                p.replay(eng, "vector", sems)

            @block.scalar
            def _(eng):
                p.replay(eng, "scalar", sems)

            @block.gpsimd
            def _(eng):
                p.replay(eng, "gpsimd", sems)
    return nc


_CACHE = {}


def _bernoulli_keep(shape, k0=0, k1=42, p=0.8):
    rot1 = (13, 15, 26, 6)
    rot2 = (17, 29, 16, 24)
    ks = (np.uint32(k0), np.uint32(k1),
          np.uint32(k0) ^ np.uint32(k1) ^ np.uint32(0x1BD11BDA))
    n = int(np.prod(shape))
    x1 = np.arange(n, dtype=np.uint32)
    x0 = np.zeros(n, dtype=np.uint32)
    x0 = (x0 + ks[0]).astype(np.uint32)
    x1 = (x1 + ks[1]).astype(np.uint32)
    for d in range(5):
        for r in (rot1 if d % 2 == 0 else rot2):
            x0 = (x0 + x1).astype(np.uint32)
            x1 = ((x1 << np.uint32(r)) | (x1 >> np.uint32(32 - r))).astype(np.uint32)
            x1 = (x1 ^ x0).astype(np.uint32)
        x0 = (x0 + ks[(d + 1) % 3]).astype(np.uint32)
        x1 = (x1 + ks[(d + 2) % 3] + np.uint32(d + 1)).astype(np.uint32)
    bits = x0 ^ x1
    u = ((bits >> np.uint32(9)) | np.uint32(0x3F800000)).view(np.float32) - np.float32(1.0)
    return (u < np.float32(p)).reshape(shape)


def kernel(L0_attn_out, L0_attn_W_write, L0_attn_W_read, L0_attn_decay, L0_attn_log_alpha,
           L0_ffn_W1, L0_ffn_b1, L0_ffn_W2, L0_ffn_b2):
    x = np.ascontiguousarray(np.asarray(L0_attn_out, dtype=np.float32))
    B = x.shape[0]
    keep = _bernoulli_keep(x.shape)
    mask = np.where(keep, np.float32(1.25), np.float32(0.0)).astype(ml_dtypes.float8_e5m2)

    if "nc" not in _CACHE:
        _CACHE["nc"] = build_nc()
    nc = _CACHE["nc"]

    ww = np.ascontiguousarray(np.asarray(L0_attn_W_write, dtype=np.float32))
    wr = np.ascontiguousarray(np.asarray(L0_attn_W_read, dtype=np.float32))
    w1 = np.ascontiguousarray(np.asarray(L0_ffn_W1, dtype=np.float32))
    w2 = np.ascontiguousarray(np.asarray(L0_ffn_W2, dtype=np.float32))
    b1 = np.ascontiguousarray(np.asarray(L0_ffn_b1, dtype=np.float32))
    b2 = np.ascontiguousarray(np.asarray(L0_ffn_b2, dtype=np.float32))
    dc = np.ascontiguousarray(np.asarray(L0_attn_decay, dtype=np.float32))
    la = np.ascontiguousarray(np.asarray(L0_attn_log_alpha, dtype=np.float32))

    in_maps = []
    for b in range(B):
        in_maps.append({
            "x": np.ascontiguousarray(x[b]),
            "mask": np.ascontiguousarray(mask[b]),
            "Ww": ww, "Wr": wr, "W1": w1, "W2": w2,
            "b1": b1, "b2": b2, "decay": dc, "logalpha": la,
        })
    res = run_bass_kernel_spmd(nc, in_maps, core_ids=list(range(B)))
    out = np.stack([np.asarray(res.results[b]["y"]) for b in range(B)], axis=0)
    return out.astype(np.float32)
